# version 05
# baseline (speedup 1.0000x reference)
import numpy as np
import ml_dtypes
from contextlib import ExitStack

import concourse.bass as bass
import concourse.mybir as mybir
from concourse.bass_utils import run_bass_kernel_spmd

F32 = mybir.dt.float32
BF16 = mybir.dt.bfloat16
AF = mybir.ActivationFunctionType
ALU = mybir.AluOpType
AX = mybir.AxisListType
BF = ml_dtypes.bfloat16

NCORES = 8
BPC = 4
T = 2048
D = 1024
NTOK = BPC * T
NH = 16
HD = 64
LN_EPS = 1e-5

NEG_BIG = -1e30



class Buf:
    __slots__ = ("name", "w", "r")

    def __init__(self, name=""):
        self.name = name
        self.w = None
        self.r = []


class Em:

    def __init__(self, nc, ctx):
        self.nc = nc
        self.ctx = ctx
        self.blk = None
        self.sems = {}
        self.cnt = {}
        self.wm = {}
        self.pend = {}
        self.bodies = {}
        for name in ("tensor", "vector", "scalar", "gpsimd", "sync", "qs", "qg"):
            self.sems[name] = ctx.enter_context(nc.semaphore(name))
            self.cnt[name] = 0

    def _flush(self):
        if self.blk is None:
            return
        for eng, fns in self.bodies.items():
            if not fns:
                continue

            def body(engine, fns=tuple(fns)):
                for f in fns:
                    f(engine)
            getattr(self.blk, eng)(body)
        self.bodies = {}

    def block(self):
        self._flush()
        if self.blk is not None:
            self.blk.__exit__(None, None, None)
        self.blk = self.nc.Block().__enter__()
        return self.blk

    def close(self):
        self._flush()
        if self.blk is not None:
            self.blk.__exit__(None, None, None)
            self.blk = None

    def op(self, eng, fn, reads=(), writes=(), sig=True, q=None, extra=()):
        waits = {}

        def add(tok):
            if tok is None:
                return
            s, v = tok
            if eng == "tensor" and s == "tensor":
                return
            if waits.get(s, 0) < v:
                waits[s] = v

        for b in reads:
            add(b.w)
        for b in writes:
            add(b.w)
            for t in b.r:
                add(t)
        for t in extra:
            add(t)
        wm = self.wm.setdefault(eng, {})
        wlist = sorted((s, v) for s, v in waits.items() if wm.get(s, -1) < v)

        signame = q if q is not None else eng
        amt = 16 if q is not None else 1
        sem = self.sems[signame]
        if sig:
            self.cnt[signame] += amt
        cval = self.cnt[signame]
        sems = self.sems

        def body(engine):
            for s, v in wlist:
                engine.wait_ge(sems[s], v)
            inst = fn(engine)
            if sig:
                inst.then_inc(sem, amt)

        self.bodies.setdefault(eng, []).append(body)
        for s, v in wlist:
            wm[s] = max(wm.get(s, -1), v)

        p_r, p_w = self.pend.pop(eng, ((), ()))
        all_r = list(p_r) + list(reads)
        all_w = list(p_w) + list(writes)
        if sig:
            tok = (signame, cval)
            for b in all_w:
                b.w = tok
                b.r = []
            for b in all_r:
                b.r.append(tok)
            return tok
        else:
            self.pend[eng] = (tuple(all_r), tuple(all_w))
            return None

    def barrier_token(self, eng="vector"):
        return None



class Roller:

    def __init__(self, nc, ctx, name, shape, dtype, n):
        self.bufs = [ctx.enter_context(nc.sbuf_tensor(f"{name}{i}", shape, dtype))
                     for i in range(n)]
        self.tok = [Buf(f"{name}{i}") for i in range(n)]
        self.i = 0

    def next(self):
        j = self.i % len(self.bufs)
        self.i += 1
        return self.bufs[j], self.tok[j]


class Consts:

    def __init__(self, nc, em, ctx, need_masks=False, need_ident=False):
        self.em = em
        one128 = np.ones((128, 1), dtype=BF)
        one1x128 = np.ones((1, 128), dtype=BF)
        one1x512 = np.ones((1, 512), dtype=BF)
        self.d_ones128 = nc.inline_tensor(one128, "c_ones128").ap()
        self.d_ones1x128 = nc.inline_tensor(one1x128, "c_ones1x128").ap()
        self.d_ones1x512 = nc.inline_tensor(one1x512, "c_ones1x512").ap()
        self.ones128 = ctx.enter_context(nc.sbuf_tensor("ones128", [128, 1], BF16))
        self.ones1x128 = ctx.enter_context(nc.sbuf_tensor("ones1x128", [1, 128], BF16))
        self.ones1x512 = ctx.enter_context(nc.sbuf_tensor("ones1x512", [1, 512], BF16))
        self.b_ones128 = Buf("ones128")
        self.b_ones1x128 = Buf("ones1x128")
        self.b_ones1x512 = Buf("ones1x512")
        self.loaded = False

    def load(self, em):
        if self.loaded:
            return
        self.loaded = True
        em.op("sync", lambda e: e.dma_start(out=self.ones128[:, :], in_=self.d_ones128[:, :]),
              writes=[self.b_ones128], q="qs")
        em.op("sync", lambda e: e.dma_start(out=self.ones1x128[:, :], in_=self.d_ones1x128[:, :]),
              writes=[self.b_ones1x128], q="qs")
        em.op("sync", lambda e: e.dma_start(out=self.ones1x512[:, :], in_=self.d_ones1x512[:, :]),
              writes=[self.b_ones1x512], q="qs")


class LnScratch:

    def __init__(self, nc, ctx, W=512):
        self.W = W
        self.sq = ctx.enter_context(nc.sbuf_tensor("ln_sq", [128, 8 * W], BF16))
        self.ub = ctx.enter_context(nc.sbuf_tensor("ln_ub", [128, 8 * W], BF16))
        self.t1 = ctx.enter_context(nc.sbuf_tensor("ln_t1", [128, W], F32))
        self.stat = ctx.enter_context(nc.sbuf_tensor("ln_stat", [1, 5 * W], F32))
        self.statb = ctx.enter_context(nc.sbuf_tensor("ln_statb", [1, 2 * W], BF16))
        self.sbR = ctx.enter_context(nc.sbuf_tensor("ln_sbR", [128, W], F32))
        self.b_sq = Buf("ln_sq"); self.b_ub = Buf("ln_ub"); self.b_t1 = Buf("ln_t1")
        self.b_stat = Buf("ln_stat"); self.b_statb = Buf("ln_statb"); self.b_sbR = Buf("ln_sbR")


def emit_ln(em, C, S, psA, psB, psM, psR, b_psA, b_psB, b_psM, b_psR,
            src_ap, b_src, out_ap=None, b_out=None,
            res_ap=None, b_res=None, out32_ap=None, b_out32=None,
            lo_ap=None, b_lo=None,
            nchunks=8, W=512, Dred=1024, eps=LN_EPS):
    def srcb(c):
        return b_src[c] if isinstance(b_src, (list, tuple)) else b_src

    W2 = W
    for c in range(nchunks):
        em.op("scalar", lambda e, c=c: e.activation(S.sq[:, c * W2:(c + 1) * W2], src_ap(c), AF.Square),
              reads=[srcb(c)], writes=[S.b_sq])
        em.op("vector", lambda e, c=c: e.tensor_copy(S.ub[:, c * W2:(c + 1) * W2], src_ap(c)),
              reads=[srcb(c)], writes=[S.b_ub])
    for c in range(nchunks):
        em.op("tensor", lambda e, c=c: e.matmul(psA, C.ones128[:, :], S.ub[:, c * W2:(c + 1) * W2],
                                                start=(c == 0), stop=(c == nchunks - 1)),
              reads=[S.b_ub, C.b_ones128], writes=[b_psA], sig=(c == nchunks - 1))
    for c in range(nchunks):
        em.op("tensor", lambda e, c=c: e.matmul(psB, C.ones128[:, :], S.sq[:, c * W2:(c + 1) * W2],
                                                start=(c == 0), stop=(c == nchunks - 1)),
              reads=[S.b_sq, C.b_ones128], writes=[b_psB], sig=(c == nchunks - 1))
    st = S.stat
    mean = st[:, 0:W]; msq = st[:, W:2 * W]; var = st[:, 2 * W:3 * W]; sd = st[:, 3 * W:4 * W]
    r = st[:, 4 * W:5 * W]
    em.op("vector", lambda e: e.tensor_scalar_mul(mean, psA, 1.0 / Dred),
          reads=[b_psA], writes=[S.b_stat])
    em.op("vector", lambda e: e.tensor_scalar_mul(msq, psB, 1.0 / Dred),
          reads=[b_psB], writes=[S.b_stat])
    em.op("vector", lambda e: e.tensor_tensor(var, mean, mean, ALU.mult),
          reads=[S.b_stat], writes=[S.b_stat])
    em.op("vector", lambda e: e.tensor_tensor(var, msq, var, ALU.subtract),
          reads=[S.b_stat], writes=[S.b_stat])
    em.op("vector", lambda e: e.tensor_scalar_add(var, var, float(eps)),
          reads=[S.b_stat], writes=[S.b_stat])
    em.op("scalar", lambda e: e.activation(sd, var, AF.Sqrt),
          reads=[S.b_stat], writes=[S.b_stat])
    em.op("vector", lambda e: e.reciprocal(r, sd),
          reads=[S.b_stat], writes=[S.b_stat])
    mb = S.statb[:, 0:W]; rb = S.statb[:, W:2 * W]
    em.op("vector", lambda e: e.tensor_copy(mb, mean), reads=[S.b_stat], writes=[S.b_statb])
    em.op("vector", lambda e: e.tensor_copy(rb, r), reads=[S.b_stat], writes=[S.b_statb])
    em.op("tensor", lambda e: e.matmul(psM, C.ones1x128[:, :], mb, start=True, stop=True),
          reads=[S.b_statb, C.b_ones1x128], writes=[b_psM])
    em.op("tensor", lambda e: e.matmul(psR, C.ones1x128[:, :], rb, start=True, stop=True),
          reads=[S.b_statb, C.b_ones1x128], writes=[b_psR])
    em.op("vector", lambda e: e.tensor_copy(S.sbR[:, :], psR), reads=[b_psR], writes=[S.b_sbR])
    for c in range(nchunks):
        em.op("vector", lambda e, c=c: e.tensor_tensor(S.t1[:, :], src_ap(c), psM, ALU.subtract),
              reads=[srcb(c), b_psM], writes=[S.b_t1])
        if out32_ap is not None:
            em.op("vector", lambda e, c=c: e.tensor_tensor(S.t1[:, :], S.t1[:, :], S.sbR[:, :], ALU.mult),
                  reads=[S.b_t1, S.b_sbR], writes=[S.b_t1])
            if res_ap is not None:
                em.op("vector", lambda e, c=c: e.tensor_tensor(out32_ap(c), S.t1[:, :], res_ap(c), ALU.add),
                      reads=[S.b_t1, b_res[c] if isinstance(b_res, (list, tuple)) else b_res],
                      writes=[b_out32[c] if isinstance(b_out32, (list, tuple)) else b_out32])
            else:
                em.op("vector", lambda e, c=c: e.tensor_copy(out32_ap(c), S.t1[:, :]),
                      reads=[S.b_t1],
                      writes=[b_out32[c] if isinstance(b_out32, (list, tuple)) else b_out32])
            if out_ap is not None:
                em.op("vector", lambda e, c=c: e.tensor_copy(out_ap(c), S.t1[:, :]),
                      reads=[S.b_t1],
                      writes=[b_out[c] if isinstance(b_out, (list, tuple)) else b_out])
        elif lo_ap is not None:
            em.op("vector", lambda e, c=c: e.tensor_tensor(S.t1[:, :], S.t1[:, :], S.sbR[:, :], ALU.mult),
                  reads=[S.b_t1, S.b_sbR], writes=[S.b_t1])
            em.op("vector", lambda e, c=c: e.tensor_copy(out_ap(c), S.t1[:, :]),
                  reads=[S.b_t1], writes=[b_out])
            em.op("vector", lambda e, c=c: e.tensor_tensor(lo_ap(c), S.t1[:, :], out_ap(c), ALU.subtract),
                  reads=[S.b_t1, b_out], writes=[b_lo])
        else:
            em.op("vector", lambda e, c=c: e.tensor_tensor(out_ap(c), S.t1[:, :], S.sbR[:, :], ALU.mult),
                  reads=[S.b_t1, S.b_sbR],
                  writes=[b_out[c] if isinstance(b_out, (list, tuple)) else b_out])



def build_p1(debug=False, F=None):
    if F is None:
        nc = bass.Bass("TRN2", target_bir_lowering=False)
        octx = ExitStack()
        em = Em(nc, octx)
        C = Consts(nc, em, octx)
        S = LnScratch(nc, octx)
        d_x = nc.dram_tensor("x", [BPC, D, T], F32, kind="ExternalInput").ap()
        d_x1 = nc.dram_tensor("x1", [BPC, D, T], F32, kind="ExternalOutput").ap()
        bd_in = Buf("bd_in"); bd_out = Buf("bd_out"); bd_lg = Buf("bd_lg")
    else:
        nc, em, C, S, streams, dbufs = F
        d_x = streams["x"]; d_x1 = streams["x1"]
        bd_in, bd_out = dbufs["x"], dbufs["x1"]
    ctx = ExitStack()
    d_wq = nc.dram_tensor("wq", [D, D], F32, kind="ExternalInput").ap()
    d_wk = nc.dram_tensor("wk", [D, D], F32, kind="ExternalInput").ap()
    d_wv = nc.dram_tensor("wv", [D, D], F32, kind="ExternalInput").ap()
    d_wo = nc.dram_tensor("wo", [D, D], F32, kind="ExternalInput").ap()

    masks_np = np.zeros((4, 128, 512), dtype=np.float32)
    for v in range(4):
        p = np.arange(128)[:, None]
        c = np.arange(512)[None, :]
        masks_np[v] = np.where(p <= c - 128 * v, 0.0, NEG_BIG)
    d_masks = nc.inline_tensor(masks_np, "c_masks").ap()

    def slopes(n):
        import math
        start = 2 ** (-2 ** (-(math.log2(n) - 3)))
        return [start * start ** i for i in range(n)]
    sl = np.array(slopes(NH), dtype=np.float64)
    ali = (sl[:, None] * (np.arange(T, dtype=np.float64) - (T - 1))[None, :])
    ali_hi = ali.astype(BF)
    ali_lo = (ali - ali_hi.astype(np.float64)).astype(BF)
    ali_hilo = np.stack([ali_hi, ali_lo], axis=1)
    d_ali = nc.inline_tensor(ali_hilo, "c_alibi").ap()

    kind = "ExternalOutput" if debug else "Internal"
    d_qf = nc.dram_tensor("qf", [BPC, D, T], BF16, kind=kind).ap()
    d_kf = nc.dram_tensor("kf", [BPC, D, T], BF16, kind=kind).ap()
    d_vt = nc.dram_tensor("vt", [BPC, T, D], BF16, kind=kind).ap()
    if debug:
        d_af = nc.dram_tensor("af", [128, 8 * T], BF16, kind="ExternalOutput").ap()
        d_bdbg = nc.dram_tensor("bdbg", [1, T], BF16, kind="ExternalOutput").ap()
        d_sdbg = nc.dram_tensor("sdbg", [1, T], F32, kind="ExternalOutput").ap()
        d_ptdbg = nc.dram_tensor("ptdbg", [128, T], BF16, kind="ExternalOutput").ap()

    xt = ctx.enter_context(nc.sbuf_tensor("xt", [128, 4096], F32))
    xn = ctx.enter_context(nc.sbuf_tensor("xn", [128, 4096], BF16))
    b_xt = Buf("xt"); b_xn = Buf("xn")

    ctxA = ExitStack()
    wq_sb = ctxA.enter_context(nc.sbuf_tensor("wq_sb", [128, 8192], BF16))
    wk_sb = ctxA.enter_context(nc.sbuf_tensor("wk_sb", [128, 8192], BF16))
    wv_sb = ctxA.enter_context(nc.sbuf_tensor("wv_sb", [128, 8192], BF16))
    b_wq = Buf("wq"); b_wk = Buf("wk"); b_wv = Buf("wv")
    q_sb = ctxA.enter_context(nc.sbuf_tensor("q_sb", [128, 4096], BF16))
    k_sb = ctxA.enter_context(nc.sbuf_tensor("k_sb", [128, 4096], BF16))
    v_sb = ctxA.enter_context(nc.sbuf_tensor("v_sb", [128, 1024], BF16))
    b_q = Buf("q_sb"); b_k = Buf("k_sb"); b_v = Buf("v_sb")
    psq = ctxA.enter_context(nc.psum_tensor("psq", [128, 512], F32))
    psk = ctxA.enter_context(nc.psum_tensor("psk", [128, 512], F32))
    psv = ctxA.enter_context(nc.psum_tensor("psv", [128, 512], F32))
    psM = ctxA.enter_context(nc.psum_tensor("psM", [128, 512], F32))
    psR = ctxA.enter_context(nc.psum_tensor("psR", [128, 512], F32))
    b_psq = Buf("psq"); b_psk = Buf("psk"); b_psv = Buf("psv")
    b_psM = Buf("psM"); b_psR = Buf("psR")
    psA = psq[0:1, :]; psB = psk[0:1, :]

    em.block()
    C.load(em)
    for w_sb, d_w, b_w in ((wq_sb, d_wq, b_wq), (wk_sb, d_wk, b_wk), (wv_sb, d_wv, b_wv)):
        em.op("gpsimd", lambda e, w_sb=w_sb, d_w=d_w: e.dma_start(
            out=w_sb[:, :], in_=d_w.rearrange("(kc p) n -> p kc n", p=128)),
            writes=[b_w], q="qg")

    for b in range(BPC):
        for ti in range(4):
            t0 = ti * 512
            em.op("sync", lambda e, b=b, t0=t0: e.dma_start(
                out=xt[:, :], in_=d_x[b].rearrange("(c p) t -> p c t", p=128)[:, :, t0:t0 + 512]),
                reads=[bd_in], writes=[b_xt], q="qs")
            emit_ln(em, C, S, psA, psB, psM[:, :], psR[:, :], b_psq, b_psk, b_psM, b_psR,
                    src_ap=lambda c: xt[:, c * 512:(c + 1) * 512], b_src=b_xt,
                    out_ap=lambda c: xn[:, c * 512:(c + 1) * 512], b_out=b_xn)
            for (w_sb, b_w, o_sb, b_o, ps, b_ps, scale) in (
                    (wq_sb, b_wq, q_sb, b_q, psq, b_psq, 0.125),
                    (wk_sb, b_wk, k_sb, b_k, psk, b_psk, 1.0)):
                for dd in range(8):
                    for kc in range(8):
                        em.op("tensor", lambda e, w_sb=w_sb, dd=dd, kc=kc, ps=ps: e.matmul(
                            ps[:, :], w_sb[:, kc * 1024 + dd * 128: kc * 1024 + (dd + 1) * 128],
                            xn[:, kc * 512:(kc + 1) * 512],
                            start=(kc == 0), stop=(kc == 7)),
                            reads=[b_w, b_xn], writes=[b_ps], sig=(kc == 7))
                    if scale == 1.0:
                        em.op("scalar", lambda e, dd=dd, o_sb=o_sb, ps=ps: e.activation(
                            o_sb[:, dd * 512:(dd + 1) * 512], ps[:, :], AF.Copy),
                            reads=[b_ps], writes=[b_o])
                    else:
                        em.op("vector", lambda e, dd=dd, o_sb=o_sb, ps=ps, scale=scale: e.tensor_scalar_mul(
                            o_sb[:, dd * 512:(dd + 1) * 512], ps[:, :], float(scale)),
                            reads=[b_ps], writes=[b_o])
                em.op("sync", lambda e, b=b, t0=t0, o_sb=o_sb, dgt=(d_qf if o_sb is q_sb else d_kf): e.dma_start(
                    out=dgt[b].rearrange("(c p) t -> p c t", p=128)[:, :, t0:t0 + 512],
                    in_=o_sb[:, :].rearrange("p (c t) -> p c t", c=8)),
                    reads=[b_o], q="qs")
            for ts in range(4):
                for dv in range(2):
                    for kc in range(8):
                        em.op("tensor", lambda e, ts=ts, dv=dv, kc=kc: e.matmul(
                            psv[:, :], xn[:, kc * 512 + ts * 128: kc * 512 + (ts + 1) * 128],
                            wv_sb[:, kc * 1024 + dv * 512: kc * 1024 + (dv + 1) * 512],
                            start=(kc == 0), stop=(kc == 7)),
                            reads=[b_wv, b_xn], writes=[b_psv], sig=(kc == 7))
                    em.op("scalar", lambda e, dv=dv: e.activation(
                        v_sb[:, dv * 512:(dv + 1) * 512], psv[:, :], AF.Copy),
                        reads=[b_psv], writes=[b_v])
                em.op("sync", lambda e, b=b, t0=t0, ts=ts: e.dma_start(
                    out=d_vt[b][t0 + ts * 128: t0 + (ts + 1) * 128, :], in_=v_sb[:, :]),
                    reads=[b_v], q="qs")

    ctxA.close()

    wo_sb = ctx.enter_context(nc.sbuf_tensor("wo_sb", [128, 8192], BF16))
    b_wo = Buf("wo")
    q_aug = ctx.enter_context(nc.sbuf_tensor("q_aug", [66, T], BF16))
    k_aug = ctx.enter_context(nc.sbuf_tensor("k_aug", [66, T], BF16))
    b_qa = Buf("q_aug"); b_ka = Buf("k_aug")
    v_h = ctx.enter_context(nc.sbuf_tensor("v_h", [128, 16 * 65], BF16))
    b_vh = Buf("v_h")
    qsq = ctx.enter_context(nc.sbuf_tensor("qsq", [64, T], BF16))
    b_qsq = Buf("qsq")
    alirow = ctx.enter_context(nc.sbuf_tensor("alirow", [1, T], BF16))
    b_ali = Buf("alirow")
    pt_sb = ctx.enter_context(nc.sbuf_tensor("pt_sb", [128, T], BF16))
    b_pt = Buf("pt_sb")
    attn_f = ctx.enter_context(nc.sbuf_tensor("attn_f", [128, 8 * T], BF16))
    b_attn = Buf("attn_f")
    mask_sb = ctx.enter_context(nc.sbuf_tensor("mask_sb", [128, 2048], F32))
    b_mask = Buf("mask")
    brow = ctx.enter_context(nc.sbuf_tensor("brow", [1, T], F32))
    brow_bf = ctx.enter_context(nc.sbuf_tensor("brow_bf", [1, T], BF16))
    kmx = ctx.enter_context(nc.sbuf_tensor("kmx", [1, 8], F32))
    b_brow = Buf("brow"); b_browbf = Buf("brow_bf"); b_kmx = Buf("kmx")
    u_sb = ctx.enter_context(nc.sbuf_tensor("u_sb", [128, 4096], F32))
    b_u = Buf("u_sb")
    x1t = ctx.enter_context(nc.sbuf_tensor("x1t", [128, 4096], F32))
    b_x1 = Buf("x1t")

    rsum = ctx.enter_context(nc.sbuf_tensor("rsum", [1, T], F32))
    rsum_bf = ctx.enter_context(nc.sbuf_tensor("rsum_bf", [1, T], BF16))
    bc_sb = ctx.enter_context(nc.sbuf_tensor("bc_sb", [64, T], F32))
    b_rsum = Buf("rsum"); b_rsumbf = Buf("rsum_bf"); b_bc = Buf("bc_sb")

    ctxB = ExitStack()
    psS = ctxB.enter_context(nc.psum_tensor("psS", [128, T], F32))
    psO = ctxB.enter_context(nc.psum_tensor("psO", [65, T], F32))
    b_psS = Buf("psS"); b_psO = Buf("psO")
    ps_row = psS[0:1, :]

    em.block()
    em.op("gpsimd", lambda e: e.dma_start(
        out=wo_sb[:, :], in_=d_wo.rearrange("(kc p) n -> p kc n", p=128)),
        writes=[b_wo], q="qg")
    em.op("sync", lambda e: e.dma_start(
        out=mask_sb[:, :], in_=d_masks.rearrange("v p t -> p v t")),
        writes=[b_mask], q="qs")

    for b in range(BPC):
        for h in range(NH):
            em.op("sync", lambda e, b=b, h=h: e.dma_start(
                out=q_aug[0:64, :], in_=d_qf[b][64 * h:64 * (h + 1), :]),
                writes=[b_qa], q="qs")
            em.op("sync", lambda e, b=b, h=h: e.dma_start(
                out=k_aug[0:64, :], in_=d_kf[b][64 * h:64 * (h + 1), :]),
                writes=[b_ka], q="qs")
            em.op("sync", lambda e, h=h: e.dma_start(
                out=k_aug[64:66, :], in_=d_ali[h]),
                writes=[b_ka], q="qs")
            em.op("vector", lambda e: e.memset(q_aug[64:66, :], 1.0), writes=[b_qa])
            em.op("vector", lambda e: e.memset(v_h[:, :], 1.0), writes=[b_vh])
            em.op("sync", lambda e, b=b, h=h: e.dma_start(
                out=v_h[:, :].rearrange("p (c d) -> p c d", d=65)[:, :, 0:64],
                in_=d_vt[b].rearrange("(c p) d -> p c d", p=128)[:, :, 64 * h:64 * (h + 1)]),
                writes=[b_vh], q="qs")
            em.op("scalar", lambda e: e.activation(qsq[:, :], k_aug[0:64, :], AF.Square),
                  reads=[b_ka], writes=[b_qsq])
            for qc in range(4):
                em.op("tensor", lambda e, qc=qc: e.matmul(
                    ps_row[:, qc * 512:(qc + 1) * 512], C.ones128[0:64, :],
                    qsq[:, qc * 512:(qc + 1) * 512], start=True, stop=True),
                    reads=[b_qsq, C.b_ones128], writes=[b_psS], sig=(qc == 3))
            em.op("vector", lambda e: e.reduce_max(kmx[:, 0:1], ps_row, axis=AX.X),
                  reads=[b_psS], writes=[b_kmx])
            em.op("scalar", lambda e: e.activation(kmx[:, 1:2], kmx[:, 0:1], AF.Sqrt),
                  reads=[b_kmx], writes=[b_kmx])
            em.op("scalar", lambda e: e.activation(qsq[:, :], q_aug[0:64, :], AF.Square),
                  reads=[b_qa], writes=[b_qsq])
            for qc in range(4):
                em.op("tensor", lambda e, qc=qc: e.matmul(
                    ps_row[:, qc * 512:(qc + 1) * 512], C.ones128[0:64, :],
                    qsq[:, qc * 512:(qc + 1) * 512], start=True, stop=True),
                    reads=[b_qsq, C.b_ones128], writes=[b_psS], sig=(qc == 3))
            em.op("scalar", lambda e: e.activation(brow[:, :], ps_row, AF.Sqrt),
                  reads=[b_psS], writes=[b_brow])
            em.op("sync", lambda e, h=h: e.dma_start(out=alirow[:, :], in_=d_ali[h, 0:1, :]),
                  writes=[b_ali], q="qs")
            em.op("vector", lambda e: e.tensor_scalar(
                brow[:, :], brow[:, :], kmx[:, 1:2], -1.02, ALU.mult, ALU.mult),
                reads=[b_brow, b_kmx], writes=[b_brow])
            em.op("vector", lambda e: e.tensor_tensor(
                brow_bf[:, :], brow[:, :], alirow[:, :], ALU.subtract),
                reads=[b_brow, b_ali], writes=[b_browbf])

            for kc in range(16):
                qc0 = kc // 4
                for qc in range(qc0, 4):
                    em.op("tensor", lambda e, kc=kc, qc=qc: e.matmul(
                        psS[:, qc * 512:(qc + 1) * 512],
                        k_aug[:, kc * 128:(kc + 1) * 128],
                        q_aug[:, qc * 512:(qc + 1) * 512],
                        start=True, stop=False),
                        reads=[b_ka, b_qa], writes=[b_psS], sig=False)
                    em.op("tensor", lambda e, kc=kc, qc=qc: e.matmul(
                        psS[:, qc * 512:(qc + 1) * 512],
                        C.ones1x128[:, :],
                        brow_bf[:, qc * 512:(qc + 1) * 512],
                        start=False, stop=True),
                        reads=[C.b_ones1x128, b_browbf], writes=[b_psS],
                        sig=(qc == 3))
                v_id = kc % 4
                em.op("vector", lambda e, kc=kc, qc0=qc0, v_id=v_id: e.tensor_tensor(
                    psS[:, qc0 * 512:(qc0 + 1) * 512], psS[:, qc0 * 512:(qc0 + 1) * 512],
                    mask_sb[:, v_id * 512:(v_id + 1) * 512], ALU.add),
                    reads=[b_psS, b_mask], writes=[b_psS])
                em.op("scalar", lambda e, kc=kc, qc0=qc0: e.activation(
                    pt_sb[:, qc0 * 512:T], psS[:, qc0 * 512:T], AF.Exp),
                    reads=[b_psS], writes=[b_pt])
                for qc in range(qc0, 4):
                    em.op("tensor", lambda e, kc=kc, qc=qc: e.matmul(
                        psO[:, qc * 512:(qc + 1) * 512],
                        v_h[:, kc * 65:(kc + 1) * 65],
                        pt_sb[:, qc * 512:(qc + 1) * 512],
                        start=(kc == 0), stop=(kc == 4 * qc + 3)),
                        reads=[b_vh, b_pt], writes=[b_psO], sig=(kc == 4 * qc + 3))
            if debug and b == 0 and h == 15:
                em.op("vector", lambda e: e.tensor_copy(rsum[:, :], psO[64:65, :]),
                      reads=[b_psO], writes=[b_rsum])
                em.op("sync", lambda e: e.dma_start(out=d_sdbg[:, :], in_=rsum[:, :]),
                      reads=[b_rsum], q="qs")
                em.op("sync", lambda e: e.dma_start(out=d_ptdbg[:, :], in_=pt_sb[:, :]),
                      reads=[b_pt], q="qs")
            em.op("vector", lambda e: e.reciprocal(rsum[:, :], psO[64:65, :]),
                  reads=[b_psO], writes=[b_rsum])
            em.op("vector", lambda e: e.tensor_copy(rsum_bf[:, :], rsum[:, :]),
                  reads=[b_rsum], writes=[b_rsumbf])
            for qc in range(4):
                em.op("tensor", lambda e, qc=qc: e.matmul(
                    psS[0:64, qc * 512:(qc + 1) * 512], C.ones1x128[:, 0:64],
                    rsum_bf[:, qc * 512:(qc + 1) * 512], start=True, stop=True),
                    reads=[C.b_ones1x128, b_rsumbf], writes=[b_psS], sig=(qc == 3))
            em.op("vector", lambda e: e.tensor_copy(bc_sb[:, :], psS[0:64, :]),
                  reads=[b_psS], writes=[b_bc])
            ci, r0 = h // 2, 64 * (h % 2)
            for qc in range(4):
                em.op("vector", lambda e, qc=qc, ci=ci, r0=r0: e.tensor_tensor(
                    attn_f[r0:r0 + 64, ci * T + qc * 512: ci * T + (qc + 1) * 512],
                    psO[0:64, qc * 512:(qc + 1) * 512],
                    bc_sb[:, qc * 512:(qc + 1) * 512], ALU.mult),
                    reads=[b_psO, b_bc], writes=[b_attn])

        if debug and b == 0:
            em.op("sync", lambda e: e.dma_start(out=d_af[:, :], in_=attn_f[:, :]),
                  reads=[b_attn], q="qs")
            em.op("sync", lambda e: e.dma_start(out=d_bdbg[:, :], in_=brow_bf[:, :]),
                  reads=[b_browbf], q="qs")
        for ti in range(4):
            t0 = ti * 512
            for dd in range(8):
                for kc in range(8):
                    em.op("tensor", lambda e, dd=dd, kc=kc, t0=t0: e.matmul(
                        psS[:, 0:512],
                        wo_sb[:, kc * 1024 + dd * 128: kc * 1024 + (dd + 1) * 128],
                        attn_f[:, kc * T + t0: kc * T + t0 + 512],
                        start=(kc == 0), stop=(kc == 7)),
                        reads=[b_wo, b_attn], writes=[b_psS], sig=(kc == 7))
                em.op("vector", lambda e, dd=dd: e.tensor_copy(
                    u_sb[:, dd * 512:(dd + 1) * 512], psS[:, 0:512]),
                    reads=[b_psS], writes=[b_u])
            em.op("sync", lambda e, b=b, t0=t0: e.dma_start(
                out=xt[:, :], in_=d_x[b].rearrange("(c p) t -> p c t", p=128)[:, :, t0:t0 + 512]),
                reads=[bd_in], writes=[b_xt], q="qs")
            emit_ln(em, C, S,
                    psS[0:1, 1024:1536], psS[0:1, 1536:2048],
                    psS[:, 0:512], psS[:, 512:1024],
                    b_psS, b_psS, b_psS, b_psS,
                    src_ap=lambda c: u_sb[:, c * 512:(c + 1) * 512], b_src=b_u,
                    out_ap=None, b_out=None,
                    out32_ap=lambda c: x1t[:, c * 512:(c + 1) * 512], b_out32=b_x1,
                    res_ap=lambda c: xt[:, c * 512:(c + 1) * 512], b_res=b_xt)
            em.op("sync", lambda e, b=b, t0=t0: e.dma_start(
                out=d_x1[b].rearrange("(c p) t -> p c t", p=128)[:, :, t0:t0 + 512],
                in_=x1t[:, :].rearrange("p (c t) -> p c t", c=8)),
                reads=[b_x1], writes=[bd_out], q="qs")

    ctxB.close()
    ctx.close()
    if F is None:
        em.block()
        qs_final = em.cnt["qs"]
        em.op("sync", lambda e: e.nop(), extra=[("qs", qs_final)], sig=False)
        em.close()
        octx.close()
        return nc
    return None


def run_p1(x_fm_cores, weights, debug=False):
    nc = build_p1(debug=debug)
    in_maps = []
    for c in range(NCORES):
        in_maps.append({
            "x": x_fm_cores[c],
            "wq": weights["wq"], "wk": weights["wk"],
            "wv": weights["wv"], "wo": weights["wo"],
        })
    res = run_bass_kernel_spmd(nc, in_maps, core_ids=list(range(NCORES)))
    if debug:
        return res.results
    return [r["x1"] for r in res.results]



def build_p2(debug=False, F=None):
    if F is None:
        nc = bass.Bass("TRN2", target_bir_lowering=False)
        octx = ExitStack()
        em = Em(nc, octx)
        C = Consts(nc, em, octx)
        S = LnScratch(nc, octx)
        d_x1 = nc.dram_tensor("x1", [BPC, D, T], F32, kind="ExternalInput").ap()
        d_x2 = nc.dram_tensor("x2", [BPC, D, T], F32, kind="ExternalOutput").ap()
        bd_in = Buf("bd_in"); bd_out = Buf("bd_out"); bd_lg = Buf("bd_lg")
    else:
        nc, em, C, S, streams, dbufs = F
        d_x1 = streams["x1"]; d_x2 = streams["x2"]
        bd_in, bd_out = dbufs["x1"], dbufs["x2"]
    ctx = ExitStack()
    d_gw = nc.dram_tensor("gw", [128, 8, 8], F32, kind="ExternalInput").ap()
    d_w1 = nc.dram_tensor("w1t", [8, D, D], F32, kind="ExternalInput").ap()
    d_w2 = nc.dram_tensor("w2t", [8, D, D], F32, kind="ExternalInput").ap()
    d_w3 = nc.dram_tensor("w3t", [8, D, D], F32, kind="ExternalInput").ap()
    d_b1 = nc.dram_tensor("b1c", [8, 128, 8], F32, kind="ExternalInput").ap()
    d_b3 = nc.dram_tensor("b3c", [8, 128, 8], F32, kind="ExternalInput").ap()
    d_b2 = nc.dram_tensor("b2r", [8, D], F32, kind="ExternalInput").ap()
    d_sw1 = nc.dram_tensor("sw1t", [D, 2048], F32, kind="ExternalInput").ap()
    d_sw2 = nc.dram_tensor("sw2t", [D, 2048], F32, kind="ExternalInput").ap()
    d_sw3 = nc.dram_tensor("sw3t", [2048, D], F32, kind="ExternalInput").ap()

    iota_np = np.broadcast_to(np.arange(8, dtype=np.float32)[None, :], (128, 8)).copy()
    d_iota = nc.inline_tensor(iota_np, "c2_iota").ap()
    d_iotab = nc.inline_tensor(iota_np + 1e6, "c2_iotab").ap()
    d_idf = nc.inline_tensor(np.eye(128, dtype=np.float32), "c2_identf32").ap()
    d_idb = nc.inline_tensor(np.eye(128, dtype=BF), "c2_identbf").ap()
    sel_np = np.zeros((18, 18 * 128), dtype=BF)
    for r_ in range(18):
        sel_np[r_, r_ * 128:(r_ + 1) * 128] = 1
    d_sel = nc.inline_tensor(sel_np, "c2_sel").ap()

    if debug:
        d_rdbg = nc.dram_tensor("rdbg", [128, 64], F32, kind="ExternalOutput").ap()
        d_cdbg = nc.dram_tensor("cdbg", [18, 512], F32, kind="ExternalOutput").ap()
        d_ydbg = nc.dram_tensor("ydbg", [128, 8192], F32, kind="ExternalOutput").ap()

    NT = 512
    NST = NTOK // NT
    NQ5 = NT // 512

    xt = ctx.enter_context(nc.sbuf_tensor("p2_xt", [128, 4096], F32))
    xn = ctx.enter_context(nc.sbuf_tensor("p2_xn", [128, 8 * NT], BF16))
    xnlo = ctx.enter_context(nc.sbuf_tensor("p2_xnlo", [128, 4096], BF16))
    y = ctx.enter_context(nc.sbuf_tensor("p2_y", [128, 8 * NT], F32))
    X1 = ctx.enter_context(nc.sbuf_tensor("p2_X1", [128, 8 * NT], BF16))
    X3 = ctx.enter_context(nc.sbuf_tensor("p2_X3", [128, 8 * NT], BF16))
    hb = ctx.enter_context(nc.sbuf_tensor("p2_hb", [128, 8 * NT], BF16))
    rA = Roller(nc, ctx, "p2_wbufA", [128, 8192], BF16, 1)
    rB = Roller(nc, ctx, "p2_wbufB", [128, 8192], BF16, 1)
    rC = Roller(nc, ctx, "p2_wbufC", [128, 8192], BF16, 2)
    r13 = Roller(nc, ctx, "p2_b13", [128, 16], F32, 2)
    r2r = Roller(nc, ctx, "p2_b2r", [1, D], BF16, 2)
    gw_hi = ctx.enter_context(nc.sbuf_tensor("p2_gw_hi", [128, 64], BF16))
    gw_lo = ctx.enter_context(nc.sbuf_tensor("p2_gw_lo", [128, 64], BF16))
    gw_f = ctx.enter_context(nc.sbuf_tensor("p2_gw_f", [128, 64], F32))
    iota_sb = ctx.enter_context(nc.sbuf_tensor("p2_iota_sb", [128, 8], F32))
    iotab_sb = ctx.enter_context(nc.sbuf_tensor("p2_iotab_sb", [128, 8], F32))
    idf_sb = ctx.enter_context(nc.sbuf_tensor("p2_idf_sb", [128, 128], F32))
    idb_sb = ctx.enter_context(nc.sbuf_tensor("p2_idb_sb", [128, 128], BF16))
    sel_sb = ctx.enter_context(nc.sbuf_tensor("p2_sel_sb", [18, 18 * 128], BF16))
    lg_sb = ctx.enter_context(nc.sbuf_tensor("p2_lg_sb", [128, 8], F32))
    rt = ctx.enter_context(nc.sbuf_tensor("p2_rt", [128, 64], F32))
    stk = ctx.enter_context(nc.sbuf_tensor("p2_stk", [128, 18], BF16))
    coefT = ctx.enter_context(nc.sbuf_tensor("p2_coefT", [18, NT], BF16))
    wkb = ctx.enter_context(nc.sbuf_tensor("p2_wkb", [128, 2 * NT], F32))
    t3a = ctx.enter_context(nc.sbuf_tensor("p2_t3a", [128, 512], F32))
    c1t = ctx.enter_context(nc.sbuf_tensor("p2_c1t", [128, 512], F32))
    tsc = ctx.enter_context(nc.sbuf_tensor("p2_tsc", [128, 512], F32))

    b_xt = Buf("xt"); b_xn = Buf("xn"); b_xnlo = Buf("xnlo"); b_y = Buf("y")
    b_X1 = Buf("X1"); b_X3 = Buf("X3"); b_hb = Buf("hb")
    b_gwh = Buf("gw_hi"); b_gwl = Buf("gw_lo"); b_gwf = Buf("gw_f")
    b_iota = Buf("iota"); b_iotab = Buf("iotab"); b_idf = Buf("idf"); b_idb = Buf("idb")
    b_sel = Buf("sel"); b_lg = Buf("lg"); b_rt = Buf("rt"); b_stk = Buf("stk")
    b_coefT = Buf("coefT"); b_wkb = Buf("wkb")
    b_t3a = Buf("t3a"); b_c1 = Buf("c1t"); b_tsc = Buf("tsc")

    ps1 = ctx.enter_context(nc.psum_tensor("p2_ps1", [128, 512], F32))
    ps2 = ctx.enter_context(nc.psum_tensor("p2_ps2", [128, 512], F32))
    ps3 = ctx.enter_context(nc.psum_tensor("p2_ps3", [128, 512], F32))
    ps4 = ctx.enter_context(nc.psum_tensor("p2_ps4", [128, 512], F32))
    ps5 = ctx.enter_context(nc.psum_tensor("p2_ps5", [128, 512], F32))
    ps6 = ctx.enter_context(nc.psum_tensor("p2_ps6", [128, 1024], BF16))
    b_ps1 = Buf("ps1"); b_ps2 = Buf("ps2"); b_ps3 = Buf("ps3")
    b_ps4 = Buf("ps4"); b_ps5 = Buf("ps5"); b_ps6 = Buf("ps6")

    em.block()
    C.load(em)
    em.op("sync", lambda e: e.dma_start(out=gw_f[:, :], in_=d_gw.rearrange("p c e -> p (c e)")),
          writes=[b_gwf], q="qs")
    em.op("sync", lambda e: e.dma_start(out=iota_sb[:, :], in_=d_iota[:, :]), writes=[b_iota], q="qs")
    em.op("sync", lambda e: e.dma_start(out=iotab_sb[:, :], in_=d_iotab[:, :]), writes=[b_iotab], q="qs")
    em.op("sync", lambda e: e.dma_start(out=idf_sb[:, :], in_=d_idf[:, :]), writes=[b_idf], q="qs")
    em.op("sync", lambda e: e.dma_start(out=idb_sb[:, :], in_=d_idb[:, :]), writes=[b_idb], q="qs")
    em.op("sync", lambda e: e.dma_start(out=sel_sb[:, :], in_=d_sel[:, :]), writes=[b_sel], q="qs")
    em.op("vector", lambda e: e.tensor_copy(gw_hi[:, :], gw_f[:, :]), reads=[b_gwf], writes=[b_gwh])
    em.op("vector", lambda e: e.tensor_tensor(gw_lo[:, :], gw_f[:, :], gw_hi[:, :], ALU.subtract),
          reads=[b_gwf, b_gwh], writes=[b_gwl])

    def wsl(role, kc, j0, width):
        return role[:, kc * 1024 + j0: kc * 1024 + j0 + width]

    for st_i in range(NST):
        b0 = (st_i * NT) // T
        tb0 = (st_i * NT) % T

        em.op("vector", lambda e: e.memset(y[:, :], 0.0), writes=[b_y])

        for ti in range(NQ5):
            t0 = tb0 + ti * 512
            co = ti * 512
            em.op("sync", lambda e, b0=b0, t0=t0: e.dma_start(
                out=xt[:, :], in_=d_x1[b0].rearrange("(c p) t -> p c t", p=128)[:, :, t0:t0 + 512]),
                reads=[bd_in], writes=[b_xt], q="qs")

            def xn_out(c, co=co):
                return xn[:, c * NT + co: c * NT + co + 512]
            emit_ln(em, C, S, ps1[0:1, :], ps2[0:1, :], ps3[:, :], ps4[:, :],
                    b_ps1, b_ps2, b_ps3, b_ps4,
                    src_ap=lambda c: xt[:, c * 512:(c + 1) * 512], b_src=b_xt,
                    out_ap=xn_out, b_out=b_xn,
                    lo_ap=lambda c: xnlo[:, c * 512:(c + 1) * 512], b_lo=b_xnlo)
            groups = [(gw_hi, b_gwh, True), (gw_lo, b_gwl, True), (gw_hi, b_gwh, False)]
            for gi, (g_sb, b_g, x_hi) in enumerate(groups):
                for kc in range(8):
                    src = xn_out(kc) if x_hi else xnlo[:, kc * 512:(kc + 1) * 512]
                    em.op("tensor", lambda e, kc=kc, g_sb=g_sb, src=src, gi=gi: e.matmul(
                        ps3[0:8, :], g_sb[:, kc * 8:(kc + 1) * 8], src,
                        start=(gi == 0 and kc == 0), stop=(gi == 2 and kc == 7),
                        skip_group_check=True),
                        reads=[b_g, b_xn, b_xnlo], writes=[b_ps3],
                        sig=(gi == 2 and kc == 7))
            em.op("vector", lambda e: e.tensor_copy(S.t1[0:8, :], ps3[0:8, :]),
                  reads=[b_ps3], writes=[S.b_t1])
            for g in range(4):
                grp = co + g * 128
                em.op("tensor", lambda e, g=g: e.transpose(
                    ps4[0:128, 0:8], S.t1[0:8, g * 128:(g + 1) * 128], idf_sb[0:8, 0:8]),
                    reads=[S.b_t1, b_idf], writes=[b_ps4])
                em.op("vector", lambda e: e.tensor_copy(lg_sb[:, :], ps4[0:128, 0:8]),
                      reads=[b_ps4], writes=[b_lg])
                r = rt
                nmx = r[:, 0:1]; ssum = r[:, 1:2]; rs = r[:, 2:3]
                m1 = r[:, 3:4]; idx1 = r[:, 4:5]; m2 = r[:, 5:6]; idx2 = r[:, 6:7]
                probs = r[:, 8:16]; eq = r[:, 16:24]; u1 = r[:, 24:32]
                oh1 = r[:, 32:40]; oh2 = r[:, 40:48]; p2 = r[:, 48:56]
                em.op("vector", lambda e: e.reduce_max(nmx, lg_sb[:, :], axis=AX.X, negate=True),
                      reads=[b_lg], writes=[b_rt])
                em.op("scalar", lambda e: e.activation(probs, lg_sb[:, :], AF.Exp, bias=nmx, accum_out=ssum),
                      reads=[b_lg, b_rt], writes=[b_rt])
                em.op("vector", lambda e: e.reciprocal(rs, ssum), reads=[b_rt], writes=[b_rt])
                em.op("vector", lambda e: e.tensor_scalar(probs, probs, rs, None, ALU.mult),
                      reads=[b_rt], writes=[b_rt])
                em.op("vector", lambda e: e.reduce_max(m1, probs, axis=AX.X), reads=[b_rt], writes=[b_rt])
                em.op("vector", lambda e: e.tensor_scalar(eq, probs, m1, None, ALU.is_ge),
                      reads=[b_rt], writes=[b_rt])
                em.op("vector", lambda e: e.scalar_tensor_tensor(u1, eq, -1e6, iotab_sb[:, :], ALU.mult, ALU.add),
                      reads=[b_rt, b_iotab], writes=[b_rt])
                em.op("vector", lambda e: e.tensor_reduce(idx1, u1, AX.X, ALU.min),
                      reads=[b_rt], writes=[b_rt])
                em.op("vector", lambda e: e.tensor_scalar(oh1, iota_sb[:, :], idx1, None, ALU.is_equal),
                      reads=[b_rt, b_iota], writes=[b_rt])
                em.op("vector", lambda e: e.scalar_tensor_tensor(p2, oh1, -1e6, probs, ALU.mult, ALU.add),
                      reads=[b_rt], writes=[b_rt])
                em.op("vector", lambda e: e.reduce_max(m2, p2, axis=AX.X), reads=[b_rt], writes=[b_rt])
                em.op("vector", lambda e: e.tensor_scalar(eq, p2, m2, None, ALU.is_ge),
                      reads=[b_rt], writes=[b_rt])
                em.op("vector", lambda e: e.scalar_tensor_tensor(u1, eq, -1e6, iotab_sb[:, :], ALU.mult, ALU.add),
                      reads=[b_rt, b_iotab], writes=[b_rt])
                em.op("vector", lambda e: e.tensor_reduce(idx2, u1, AX.X, ALU.min),
                      reads=[b_rt], writes=[b_rt])
                em.op("vector", lambda e: e.tensor_scalar(oh2, iota_sb[:, :], idx2, None, ALU.is_equal),
                      reads=[b_rt, b_iota], writes=[b_rt])
                em.op("vector", lambda e: e.tensor_copy(stk[:, 0:8], oh1), reads=[b_rt], writes=[b_stk])
                em.op("vector", lambda e: e.tensor_copy(stk[:, 8:16], oh2), reads=[b_rt], writes=[b_stk])
                em.op("vector", lambda e: e.tensor_copy(stk[:, 16:17], m1), reads=[b_rt], writes=[b_stk])
                em.op("vector", lambda e: e.tensor_copy(stk[:, 17:18], m2), reads=[b_rt], writes=[b_stk])
                em.op("tensor", lambda e: e.transpose(ps6[0:18, 0:128], stk[:, :], idb_sb[:, :]),
                      reads=[b_stk, b_idb], writes=[b_ps6])
                em.op("vector", lambda e, grp=grp: e.tensor_copy(coefT[:, grp:grp + 128], ps6[0:18, 0:128]),
                      reads=[b_ps6], writes=[b_coefT])
                if debug and st_i == 0 and g == 0 and ti == 0:
                    em.op("sync", lambda e: e.dma_start(out=d_rdbg[:, :], in_=rt[:, :]),
                          reads=[b_rt], q="qs")

        if debug and st_i == 0:
            em.op("vector", lambda e: e.tensor_copy(S.t1[0:18, :], coefT[:, 0:512]),
                  reads=[b_coefT], writes=[S.b_t1])
            em.op("sync", lambda e: e.dma_start(out=d_cdbg[:, 0:512], in_=S.t1[0:18, :]),
                  reads=[S.b_t1], q="qs")

        for k in range(2):
            for q5 in range(NQ5):
                em.op("tensor", lambda e, k=k, q5=q5: e.matmul(
                    ps5[:, :], sel_sb[:, (16 + k) * 128:(17 + k) * 128],
                    coefT[:, q5 * 512:(q5 + 1) * 512], start=True, stop=True),
                    reads=[b_sel, b_coefT], writes=[b_ps5])
                em.op("vector", lambda e, k=k, q5=q5: e.tensor_copy(
                    wkb[:, k * NT + q5 * 512: k * NT + (q5 + 1) * 512], ps5[:, :]),
                    reads=[b_ps5], writes=[b_wkb])

        for e_i in range(8):
            wA, b_wA = rA.next(); wB, b_wB = rB.next(); wC, b_wC = rC.next()
            b13, b_b13 = r13.next(); b2s, b_b2 = r2r.next()
            b1s = b13[:, 0:8]; b3s = b13[:, 8:16]
            em.op("gpsimd", lambda e, e_i=e_i, wA=wA: e.dma_start(
                out=wA[:, :], in_=d_w1[e_i].rearrange("(kc p) n -> p kc n", p=128)),
                writes=[b_wA], q="qg")
            em.op("gpsimd", lambda e, e_i=e_i, wB=wB: e.dma_start(
                out=wB[:, :], in_=d_w3[e_i].rearrange("(kc p) n -> p kc n", p=128)),
                writes=[b_wB], q="qg")
            em.op("gpsimd", lambda e, e_i=e_i, wC=wC: e.dma_start(
                out=wC[:, :], in_=d_w2[e_i].rearrange("(kc p) n -> p kc n", p=128)),
                writes=[b_wC], q="qg")
            em.op("gpsimd", lambda e, e_i=e_i, b2s=b2s: e.dma_start(
                out=b2s[:, :], in_=d_b2[e_i:e_i + 1, :]), writes=[b_b2], q="qg")
            em.op("sync", lambda e, e_i=e_i, b1s=b1s: e.dma_start(
                out=b1s, in_=d_b1[e_i]), writes=[b_b13], q="qs")
            em.op("sync", lambda e, e_i=e_i, b3s=b3s: e.dma_start(
                out=b3s, in_=d_b3[e_i]), writes=[b_b13], q="qs")
            for (wb, b_wb, dst, b_dst) in ((wA, b_wA, X1, b_X1), (wB, b_wB, X3, b_X3)):
                for dd in range(8):
                    for q5 in range(NQ5):
                        for kc in range(8):
                            em.op("tensor", lambda e, wb=wb, dd=dd, q5=q5, kc=kc: e.matmul(
                                ps1[:, :], wsl(wb, kc, dd * 128, 128),
                                xn[:, kc * NT + q5 * 512: kc * NT + q5 * 512 + 512],
                                start=(kc == 0), stop=(kc == 7)),
                                reads=[b_wb, b_xn], writes=[b_ps1], sig=(kc == 7))
                        em.op("scalar", lambda e, dd=dd, q5=q5, dst=dst: e.activation(
                            dst[:, dd * NT + q5 * 512: dd * NT + q5 * 512 + 512], ps1[:, :], AF.Copy),
                            reads=[b_ps1], writes=[b_dst])
            for k in range(2):
                for dd in range(8):
                    for q5 in range(NQ5):
                        em.op("vector", lambda e, dd=dd, q5=q5, k=k: e.tensor_tensor(
                            tsc[:, :], X1[:, dd * NT + q5 * 512: dd * NT + q5 * 512 + 512],
                            wkb[:, k * NT + q5 * 512: k * NT + q5 * 512 + 512], ALU.mult),
                            reads=[b_X1, b_wkb], writes=[b_tsc])
                        em.op("scalar", lambda e, dd=dd, q5=q5, b1s=b1s: e.activation(
                            hb[:, dd * NT + q5 * 512: dd * NT + q5 * 512 + 512], tsc[:, :], AF.Silu,
                            bias=b1s[:, dd:dd + 1]),
                            reads=[b_tsc, b_b13], writes=[b_hb])
                for q5 in range(NQ5):
                    em.op("tensor", lambda e, e_i=e_i, k=k, q5=q5: e.matmul(
                        ps5[:, :], sel_sb[:, (8 * k + e_i) * 128:(8 * k + e_i + 1) * 128],
                        coefT[:, q5 * 512:(q5 + 1) * 512], start=True, stop=True),
                        reads=[b_sel, b_coefT], writes=[b_ps5])
                    for dd in range(8):
                        for kc in range(8):
                            em.op("tensor", lambda e, dd=dd, q5=q5, kc=kc, wC=wC: e.matmul(
                                ps2[:, :], wsl(wC, kc, dd * 128, 128),
                                hb[:, kc * NT + q5 * 512: kc * NT + q5 * 512 + 512],
                                start=(kc == 0), stop=False, skip_group_check=True),
                                reads=[b_wC, b_hb], writes=[b_ps2], sig=False)
                        em.op("tensor", lambda e, dd=dd, q5=q5, b2s=b2s: e.matmul(
                            ps2[:, :], b2s[:, dd * 128:(dd + 1) * 128],
                            C.ones1x512[:, :], start=False, stop=True, skip_group_check=True),
                            reads=[b_b2, C.b_ones1x512], writes=[b_ps2], sig=True)
                        em.op("vector", lambda e, dd=dd, q5=q5, k=k: e.tensor_tensor(
                            t3a[:, :], X3[:, dd * NT + q5 * 512: dd * NT + q5 * 512 + 512],
                            wkb[:, k * NT + q5 * 512: k * NT + q5 * 512 + 512], ALU.mult),
                            reads=[b_X3, b_wkb], writes=[b_t3a])
                        em.op("vector", lambda e, dd=dd, b3s=b3s: e.scalar_tensor_tensor(
                            c1t[:, :], t3a[:, :], b3s[:, dd:dd + 1], ps2[:, :], ALU.add, ALU.mult),
                            reads=[b_t3a, b_b13, b_ps2], writes=[b_c1])
                        em.op("vector", lambda e: e.tensor_tensor(
                            c1t[:, :], c1t[:, :], ps5[:, :], ALU.mult),
                            reads=[b_c1, b_ps5], writes=[b_c1])
                        em.op("vector", lambda e, dd=dd, q5=q5: e.tensor_tensor(
                            y[:, dd * NT + q5 * 512: dd * NT + q5 * 512 + 512],
                            y[:, dd * NT + q5 * 512: dd * NT + q5 * 512 + 512],
                            c1t[:, :], ALU.add),
                            reads=[b_c1], writes=[b_y])

        for ti in range(NQ5):
            co = ti * 512
            for Ih in range(2):
                wA, b_wA = rA.next(); wB, b_wB = rB.next(); wC, b_wC = rC.next()
                em.op("gpsimd", lambda e, Ih=Ih, wA=wA: e.dma_start(
                    out=wA[:, :], in_=d_sw1.rearrange("(kc p) n -> p kc n", p=128)[:, :, Ih * 1024:(Ih + 1) * 1024]),
                    writes=[b_wA], q="qg")
                em.op("gpsimd", lambda e, Ih=Ih, wB=wB: e.dma_start(
                    out=wB[:, :], in_=d_sw2.rearrange("(kc p) n -> p kc n", p=128)[:, :, Ih * 1024:(Ih + 1) * 1024]),
                    writes=[b_wB], q="qg")
                em.op("gpsimd", lambda e, Ih=Ih, wC=wC: e.dma_start(
                    out=wC[:, :], in_=d_sw3.rearrange("(kc p) n -> p kc n", p=128)[:, 8 * Ih:8 * Ih + 8, :]),
                    writes=[b_wC], q="qg")
                for ic in range(8):
                    for kc in range(8):
                        em.op("tensor", lambda e, ic=ic, kc=kc, co=co, wA=wA: e.matmul(
                            ps1[:, :], wsl(wA, kc, ic * 128, 128),
                            xn[:, kc * NT + co: kc * NT + co + 512],
                            start=(kc == 0), stop=(kc == 7)),
                            reads=[b_wA, b_xn], writes=[b_ps1], sig=(kc == 7))
                    em.op("scalar", lambda e, ic=ic: e.activation(
                        hb[:, ic * 512:(ic + 1) * 512], ps1[:, :], AF.Silu),
                        reads=[b_ps1], writes=[b_hb])
                    for kc in range(8):
                        em.op("tensor", lambda e, ic=ic, kc=kc, co=co, wB=wB: e.matmul(
                            ps2[:, :], wsl(wB, kc, ic * 128, 128),
                            xn[:, kc * NT + co: kc * NT + co + 512],
                            start=(kc == 0), stop=(kc == 7)),
                            reads=[b_wB, b_xn], writes=[b_ps2], sig=(kc == 7))
                    em.op("vector", lambda e, ic=ic: e.tensor_tensor(
                        X3[:, ic * 512:(ic + 1) * 512],
                        ps2[:, :], hb[:, ic * 512:(ic + 1) * 512], ALU.mult),
                        reads=[b_ps2, b_hb], writes=[b_X3])
                for dd in range(8):
                    for ic in range(8):
                        em.op("tensor", lambda e, dd=dd, ic=ic, wC=wC: e.matmul(
                            ps1[:, :], wsl(wC, ic, dd * 128, 128),
                            X3[:, ic * 512:(ic + 1) * 512],
                            start=(ic == 0), stop=(ic == 7)),
                            reads=[b_wC, b_X3], writes=[b_ps1], sig=(ic == 7))
                    em.op("vector", lambda e, dd=dd, co=co: e.tensor_tensor(
                        y[:, dd * NT + co: dd * NT + co + 512],
                        y[:, dd * NT + co: dd * NT + co + 512], ps1[:, :], ALU.add),
                        reads=[b_ps1], writes=[b_y])

        if debug and st_i == 0:
            em.op("sync", lambda e: e.dma_start(out=d_ydbg[:, :], in_=y[:, :]), reads=[b_y], q="qs")

        for ti in range(NQ5):
            t0 = tb0 + ti * 512
            co = ti * 512
            em.op("sync", lambda e, b0=b0, t0=t0: e.dma_start(
                out=xt[:, :], in_=d_x1[b0].rearrange("(c p) t -> p c t", p=128)[:, :, t0:t0 + 512]),
                reads=[bd_in], writes=[b_xt], q="qs")
            emit_ln(em, C, S, ps1[0:1, :], ps2[0:1, :], ps3[:, :], ps4[:, :],
                    b_ps1, b_ps2, b_ps3, b_ps4,
                    src_ap=lambda c, co=co: y[:, c * NT + co: c * NT + co + 512], b_src=b_y,
                    out_ap=None, b_out=None,
                    out32_ap=lambda c: xt[:, c * 512:(c + 1) * 512], b_out32=b_xt,
                    res_ap=lambda c: xt[:, c * 512:(c + 1) * 512], b_res=b_xt)
            em.op("sync", lambda e, b0=b0, t0=t0: e.dma_start(
                out=d_x2[b0].rearrange("(c p) t -> p c t", p=128)[:, :, t0:t0 + 512],
                in_=xt[:, :].rearrange("p (c t) -> p c t", c=8)),
                reads=[b_xt], writes=[bd_out], q="qs")

    ctx.close()
    if F is None:
        em.block()
        qs_final = em.cnt["qs"]
        em.op("sync", lambda e: e.nop(), extra=[("qs", qs_final)], sig=False)
        em.close()
        octx.close()
        return nc
    return None


def p2_inputs(inp):
    gw = np.ascontiguousarray(inp["L0_ffn_gate_w"], np.float32)
    gwT = gw.T.reshape(8, 128, 8).transpose(1, 0, 2)
    w1t = np.ascontiguousarray(np.asarray(inp["L0_ffn_w1"], np.float32).transpose(0, 2, 1))
    w2t = np.ascontiguousarray(np.asarray(inp["L0_ffn_w2"], np.float32).transpose(0, 2, 1))
    w3t = np.ascontiguousarray(np.asarray(inp["L0_ffn_w3"], np.float32).transpose(0, 2, 1))
    b1 = np.ascontiguousarray(np.asarray(inp["L0_ffn_b1"], np.float32).reshape(8, 8, 128).transpose(0, 2, 1))
    b3 = np.ascontiguousarray(np.asarray(inp["L0_ffn_b3"], np.float32).reshape(8, 8, 128).transpose(0, 2, 1))
    b2 = np.ascontiguousarray(inp["L0_ffn_b2"], np.float32)
    sw1t = np.ascontiguousarray(np.asarray(inp["L0_ffn_sw1"], np.float32).T)
    sw2t = np.ascontiguousarray(np.asarray(inp["L0_ffn_sw2"], np.float32).T)
    sw3t = np.ascontiguousarray(np.asarray(inp["L0_ffn_sw3"], np.float32).T)
    return {"gw": np.ascontiguousarray(gwT), "w1t": w1t, "w2t": w2t, "w3t": w3t,
            "b1c": b1, "b3c": b3, "b2r": b2, "sw1t": sw1t, "sw2t": sw2t, "sw3t": sw3t}


def run_p2(x1_fm_cores, winp, debug=False):
    nc = build_p2(debug=debug)
    in_maps = []
    for c in range(NCORES):
        m = dict(winp)
        m["x1"] = x1_fm_cores[c]
        in_maps.append(m)
    res = run_bass_kernel_spmd(nc, in_maps, core_ids=list(range(NCORES)))
    if debug:
        return res.results
    return [r["x2"] for r in res.results]



def build_p3(debug=False, F=None):
    if F is None:
        nc = bass.Bass("TRN2", target_bir_lowering=False)
        octx = ExitStack()
        em = Em(nc, octx)
        C = Consts(nc, em, octx)
        S = LnScratch(nc, octx)
        d_x2 = nc.dram_tensor("x2", [BPC, D, T], F32, kind="ExternalInput").ap()
        d_x3 = nc.dram_tensor("x3", [BPC, D, T], F32, kind="ExternalOutput").ap()
        bd_in = Buf("bd_in"); bd_out = Buf("bd_out"); bd_lg = Buf("bd_lg")
    else:
        nc, em, C, S, streams, dbufs = F
        d_x2 = streams["x2"]; d_x3 = streams["x3"]
        bd_in, bd_out = dbufs["x2"], dbufs["x3"]
    ctx = ExitStack()
    d_wq = nc.dram_tensor("wqT", [D, D], F32, kind="ExternalInput").ap()
    d_wk = nc.dram_tensor("wkT", [D, D], F32, kind="ExternalInput").ap()
    d_wv = nc.dram_tensor("wvT", [D, D], F32, kind="ExternalInput").ap()
    d_wrp = nc.dram_tensor("wrp", [16, 64, D], F32, kind="ExternalInput").ap()

    ROT = 32
    THETA = 10000.0
    inv = 1.0 / THETA ** (np.arange(0, ROT, 2, dtype=np.float64) / ROT)
    freq_row = np.repeat(inv, 2)
    c_idx = np.arange(T)
    s_true = 16 * (c_idx % 128) + c_idx // 128
    ang = s_true[None, :] * freq_row[:, None]
    cos_t = np.ones((64, T), dtype=np.float64)
    sin_t = np.zeros((64, T), dtype=np.float64)
    cos_t[0:32] = np.cos(ang)
    sin_t[0:32] = np.sin(ang)
    d_cosq = nc.inline_tensor((cos_t * 0.25).astype(BF), "c_cosq").ap()
    d_sinq = nc.inline_tensor((sin_t * 0.25).astype(BF), "c_sinq").ap()
    d_cosk = nc.inline_tensor(cos_t.astype(BF), "c_cosk").ap()
    d_sink = nc.inline_tensor(sin_t.astype(BF), "c_sink").ap()
    Pt = np.zeros((64, 64), dtype=BF)
    for i in range(16):
        Pt[2 * i + 1, 2 * i] = -1.0
        Pt[2 * i, 2 * i + 1] = 1.0
    d_pt = nc.inline_tensor(Pt, "c3_rotP").ap()
    d_idb = nc.inline_tensor(np.eye(128, dtype=BF), "c3_identbf").ap()

    d_qt = nc.dram_tensor("qt3", [BPC, T, D], BF16, kind="Internal").ap()
    d_kt = nc.dram_tensor("kt3", [BPC, T, D], BF16, kind="Internal").ap()
    d_vt = nc.dram_tensor("vt3", [BPC, T, D], BF16, kind="Internal").ap()
    if debug:
        d_o1dbg = nc.dram_tensor("o1dbg", [128, 16 * 1024], BF16, kind="ExternalOutput").ap()
        d_qpdbg = nc.dram_tensor("qpdbg", [65, T], BF16, kind="ExternalOutput").ap()

    xt = ctx.enter_context(nc.sbuf_tensor("p3_xt", [128, 4096], F32))
    xn = ctx.enter_context(nc.sbuf_tensor("p3_xn", [128, 4096], BF16))
    b_xt = Buf("xt"); b_xn = Buf("xn")

    cosq = ctx.enter_context(nc.sbuf_tensor("p3_cosq", [64, T], BF16))
    sinq = ctx.enter_context(nc.sbuf_tensor("p3_sinq", [64, T], BF16))
    cosk = ctx.enter_context(nc.sbuf_tensor("p3_cosk", [64, T], BF16))
    sink = ctx.enter_context(nc.sbuf_tensor("p3_sink", [64, T], BF16))
    pt_m = ctx.enter_context(nc.sbuf_tensor("p3_pt_m", [64, 64], BF16))
    idb_sb = ctx.enter_context(nc.sbuf_tensor("p3_idb_sb", [128, 128], BF16))
    b_tab = Buf("ropetabs"); b_pt_m = Buf("pt_m"); b_idb = Buf("idb")

    em.block()
    C.load(em)
    for d_t, sb in ((d_cosq, cosq), (d_sinq, sinq), (d_cosk, cosk), (d_sink, sink)):
        em.op("sync", lambda e, d_t=d_t, sb=sb: e.dma_start(out=sb[:, :], in_=d_t[:, :]),
              writes=[b_tab], q="qs")
    em.op("sync", lambda e: e.dma_start(out=pt_m[:, :], in_=d_pt[:, :]), writes=[b_pt_m], q="qs")
    em.op("sync", lambda e: e.dma_start(out=idb_sb[:, :], in_=d_idb[:, :]), writes=[b_idb], q="qs")

    ctxC1 = ExitStack()
    wq_sb = ctxC1.enter_context(nc.sbuf_tensor("p3_wq_sb", [128, 8192], BF16))
    wk_sb = ctxC1.enter_context(nc.sbuf_tensor("p3_wk_sb", [128, 8192], BF16))
    wv_sb = ctxC1.enter_context(nc.sbuf_tensor("p3_wv_sb", [128, 8192], BF16))
    o_sb = ctxC1.enter_context(nc.sbuf_tensor("p3_o_sb", [128, 1024], BF16))
    b_wq = Buf("wq"); b_wk = Buf("wk"); b_wv = Buf("wv"); b_o = Buf("o_sb")
    psq = ctxC1.enter_context(nc.psum_tensor("p3_psq", [128, 512], F32))
    psk = ctxC1.enter_context(nc.psum_tensor("p3_psk", [128, 512], F32))
    psM = ctxC1.enter_context(nc.psum_tensor("p3_psM", [128, 512], F32))
    psR = ctxC1.enter_context(nc.psum_tensor("p3_psR", [128, 512], F32))
    b_psq = Buf("psq"); b_psk = Buf("psk"); b_psM = Buf("psM"); b_psR = Buf("psR")

    for w_sb, d_w, b_w in ((wq_sb, d_wq, b_wq), (wk_sb, d_wk, b_wk), (wv_sb, d_wv, b_wv)):
        em.op("gpsimd", lambda e, w_sb=w_sb, d_w=d_w: e.dma_start(
            out=w_sb[:, :], in_=d_w.rearrange("(kc p) n -> p kc n", p=128)),
            writes=[b_w], q="qg")

    for b in range(BPC):
        for ti in range(4):
            t0 = ti * 512
            em.op("sync", lambda e, b=b, t0=t0: e.dma_start(
                out=xt[:, :], in_=d_x2[b].rearrange("(c p) t -> p c t", p=128)[:, :, t0:t0 + 512]),
                reads=[bd_in], writes=[b_xt], q="qs")
            emit_ln(em, C, S, psq[0:1, :], psk[0:1, :], psM[:, :], psR[:, :],
                    b_psq, b_psk, b_psM, b_psR,
                    src_ap=lambda c: xt[:, c * 512:(c + 1) * 512], b_src=b_xt,
                    out_ap=lambda c: xn[:, c * 512:(c + 1) * 512], b_out=b_xn)
            for (w_sb, b_w, dgt) in ((wq_sb, b_wq, d_qt), (wk_sb, b_wk, d_kt), (wv_sb, b_wv, d_vt)):
                for ts in range(4):
                    for dv in range(2):
                        for kc in range(8):
                            em.op("tensor", lambda e, w_sb=w_sb, ts=ts, dv=dv, kc=kc: e.matmul(
                                psq[:, :], xn[:, kc * 512 + ts * 128: kc * 512 + (ts + 1) * 128],
                                w_sb[:, kc * 1024 + dv * 512: kc * 1024 + (dv + 1) * 512],
                                start=(kc == 0), stop=(kc == 7)),
                                reads=[b_w, b_xn], writes=[b_psq], sig=(kc == 7))
                        em.op("scalar", lambda e, dv=dv: e.activation(
                            o_sb[:, dv * 512:(dv + 1) * 512], psq[:, :], AF.Copy),
                            reads=[b_psq], writes=[b_o])
                    em.op("sync", lambda e, b=b, t0=t0, ts=ts, dgt=dgt: e.dma_start(
                        out=dgt[b][t0 + ts * 128: t0 + (ts + 1) * 128, :], in_=o_sb[:, :]),
                        reads=[b_o], q="qs")
    ctxC1.close()

    em.block()
    qh_sb = ctx.enter_context(nc.sbuf_tensor("p3_qh_sb", [128, 1024], BF16))
    kh_sb = ctx.enter_context(nc.sbuf_tensor("p3_kh_sb", [128, 1024], BF16))
    b_qh = Buf("qh"); b_kh = Buf("kh")
    q_perm = ctx.enter_context(nc.sbuf_tensor("p3_q_perm", [64, T], BF16))
    k_perm = ctx.enter_context(nc.sbuf_tensor("p3_k_perm", [64, T], BF16))
    b_qp = Buf("q_perm"); b_kp = Buf("k_perm")
    q_aug = ctx.enter_context(nc.sbuf_tensor("p3_q_aug", [65, T], BF16))
    k_aug = ctx.enter_context(nc.sbuf_tensor("p3_k_aug", [65, T], BF16))
    b_qa = Buf("q_aug"); b_ka = Buf("k_aug")
    v65 = ctx.enter_context(nc.sbuf_tensor("p3_v65", [128, 16 * 65], BF16))
    b_v65 = Buf("v65")
    rop_a = ctx.enter_context(nc.sbuf_tensor("p3_rop_a", [64, 128], F32))
    b_ropa = Buf("rop_a")
    qsq = ctx.enter_context(nc.sbuf_tensor("p3_qsq", [64, T], BF16))
    b_qsq = Buf("qsq")
    brow = ctx.enter_context(nc.sbuf_tensor("p3_brow", [1, T], F32))
    brow_bf = ctx.enter_context(nc.sbuf_tensor("p3_brow_bf", [1, T], BF16))
    kmx = ctx.enter_context(nc.sbuf_tensor("p3_kmx", [1, 8], F32))
    b_brow = Buf("brow"); b_browbf = Buf("brow_bf"); b_kmx = Buf("kmx")
    pt_sb = ctx.enter_context(nc.sbuf_tensor("p3_pt_sb", [128, T], BF16))
    b_pt = Buf("pt_sb")
    out1 = ctx.enter_context(nc.sbuf_tensor("p3_out1", [128, 16 * 1024], BF16))
    b_out1 = Buf("out1")
    rden = ctx.enter_context(nc.sbuf_tensor("p3_rden", [128, 1], F32))
    b_rden = Buf("rden")

    for b in range(BPC):
        for h in range(NH):
            em.op("sync", lambda e, b=b, h=h: e.dma_start(
                out=qh_sb[:, :], in_=d_qt[b][128 * h:128 * (h + 1), :]),
                writes=[b_qh], q="qs")
            em.op("sync", lambda e, b=b, h=h: e.dma_start(
                out=kh_sb[:, :], in_=d_kt[b][128 * h:128 * (h + 1), :]),
                writes=[b_kh], q="qs")
            em.op("vector", lambda e: e.memset(v65[:, :], 1.0), writes=[b_v65])
            em.op("sync", lambda e, b=b, h=h: e.dma_start(
                out=v65[:, :].rearrange("p (c d) -> p c d", d=65)[:, :, 0:64],
                in_=d_vt[b].rearrange("(c p) d -> p c d", p=128)[128 * h // 128 * 0:128, :, 0:64] if False else
                    d_vt[b][128 * h:128 * (h + 1), :].rearrange("p (c d) -> p c d", d=64)),
                writes=[b_v65], q="qs")

            ctxT = ExitStack()
            psT = ctxT.enter_context(nc.psum_tensor(f"psT_{b}_{h}", [64, 128], BF16))
            psR2 = ctxT.enter_context(nc.psum_tensor(f"psR2_{b}_{h}", [64, 128], F32))
            b_psT = Buf("psT"); b_psR2 = Buf("psR2")
            for (src_sb, b_srcb, perm, b_perm) in ((qh_sb, b_qh, q_perm, b_qp), (kh_sb, b_kh, k_perm, b_kp)):
                for g in range(16):
                    em.op("tensor", lambda e, g=g, src_sb=src_sb: e.transpose(
                        psT[:, :], src_sb[:, 64 * g:64 * (g + 1)], idb_sb[:, :]),
                        reads=[b_srcb, b_idb], writes=[b_psT])
                    em.op("vector", lambda e, g=g, perm=perm: e.tensor_copy(
                        perm[:, 128 * g:128 * (g + 1)], psT[:, :]),
                        reads=[b_psT], writes=[b_perm])
            for (perm, b_perm, aug, b_aug, cos_s, sin_s) in (
                    (q_perm, b_qp, q_aug, b_qa, cosq, sinq),
                    (k_perm, b_kp, k_aug, b_ka, cosk, sink)):
                for g in range(16):
                    sl = slice(128 * g, 128 * (g + 1))
                    em.op("tensor", lambda e, sl=sl, perm=perm: e.matmul(
                        psR2[:, :], pt_m[:, :], perm[:, sl], start=True, stop=True),
                        reads=[b_pt_m, b_perm], writes=[b_psR2])
                    em.op("vector", lambda e, sl=sl, perm=perm, cos_s=cos_s: e.tensor_tensor(
                        rop_a[:, :], perm[:, sl], cos_s[:, sl], ALU.mult),
                        reads=[b_perm, b_tab], writes=[b_ropa])
                    em.op("vector", lambda e, sl=sl, sin_s=sin_s: e.tensor_tensor(
                        psR2[:, :], psR2[:, :], sin_s[:, sl], ALU.mult),
                        reads=[b_psR2, b_tab], writes=[b_psR2])
                    em.op("vector", lambda e, sl=sl, aug=aug: e.tensor_tensor(
                        aug[0:64, sl], rop_a[:, :], psR2[:, :], ALU.add),
                        reads=[b_ropa, b_psR2], writes=[b_aug])
            ctxT.close()
            em.op("vector", lambda e: e.memset(k_aug[64:65, :], 1.0), writes=[b_ka])

            ctxS = ExitStack()
            psS = ctxS.enter_context(nc.psum_tensor(f"psS_{b}_{h}", [128, T], F32))
            psO2 = ctxS.enter_context(nc.psum_tensor(f"psO2_{b}_{h}", [128, T], F32))
            b_psS = Buf("psS"); b_psO2 = Buf("psO2")
            ps_row = psS[0:1, :]
            em.op("scalar", lambda e: e.activation(qsq[:, :], k_aug[0:64, :], AF.Square),
                  reads=[b_ka], writes=[b_qsq])
            for qc in range(4):
                em.op("tensor", lambda e, qc=qc: e.matmul(
                    ps_row[:, qc * 512:(qc + 1) * 512], C.ones128[0:64, :],
                    qsq[:, qc * 512:(qc + 1) * 512], start=True, stop=True),
                    reads=[b_qsq, C.b_ones128], writes=[b_psS], sig=(qc == 3))
            em.op("vector", lambda e: e.reduce_max(kmx[:, 0:1], ps_row, axis=AX.X),
                  reads=[b_psS], writes=[b_kmx])
            em.op("scalar", lambda e: e.activation(kmx[:, 1:2], kmx[:, 0:1], AF.Sqrt),
                  reads=[b_kmx], writes=[b_kmx])
            em.op("scalar", lambda e: e.activation(qsq[:, :], q_aug[0:64, :], AF.Square),
                  reads=[b_qa], writes=[b_qsq])
            for qc in range(4):
                em.op("tensor", lambda e, qc=qc: e.matmul(
                    ps_row[:, qc * 512:(qc + 1) * 512], C.ones128[0:64, :],
                    qsq[:, qc * 512:(qc + 1) * 512], start=True, stop=True),
                    reads=[b_qsq, C.b_ones128], writes=[b_psS], sig=(qc == 3))
            em.op("scalar", lambda e: e.activation(brow[:, :], ps_row, AF.Sqrt),
                  reads=[b_psS], writes=[b_brow])
            em.op("vector", lambda e: e.tensor_scalar(
                brow_bf[:, :], brow[:, :], kmx[:, 1:2], -1.02, ALU.mult, ALU.mult),
                reads=[b_brow, b_kmx], writes=[b_browbf])
            em.op("vector", lambda e: e.tensor_copy(q_aug[64:65, :], brow_bf[:, :]),
                  reads=[b_browbf], writes=[b_qa])
            if debug and b == 0 and h == 0:
                em.op("sync", lambda e: e.dma_start(out=d_qpdbg[:, :], in_=q_aug[:, :]),
                      reads=[b_qa], q="qs")

            for kc in range(16):
                for qc in range(4):
                    em.op("tensor", lambda e, kc=kc, qc=qc: e.matmul(
                        psS[:, qc * 512:(qc + 1) * 512],
                        k_aug[:, kc * 128:(kc + 1) * 128],
                        q_aug[:, qc * 512:(qc + 1) * 512],
                        start=True, stop=True),
                        reads=[b_ka, b_qa], writes=[b_psS], sig=(qc == 3))
                em.op("scalar", lambda e, kc=kc: e.activation(
                    pt_sb[:, :], psS[:, :], AF.Exp),
                    reads=[b_psS], writes=[b_pt])
                for qc2 in range(16):
                    em.op("tensor", lambda e, kc=kc, qc2=qc2: e.matmul(
                        psO2[:, 128 * qc2: 128 * qc2 + 65],
                        pt_sb[:, qc2 * 128:(qc2 + 1) * 128],
                        v65[:, kc * 65:(kc + 1) * 65],
                        start=(kc == 0 and qc2 % 4 == 0), stop=(kc == 15),
                        skip_group_check=True),
                        reads=[b_pt, b_v65], writes=[b_psO2], sig=(kc == 15 and qc2 == 15))
            for qc2 in range(16):
                em.op("vector", lambda e, qc2=qc2: e.reciprocal(
                    rden[:, :], psO2[:, 128 * qc2 + 64: 128 * qc2 + 65]),
                    reads=[b_psO2], writes=[b_rden])
                em.op("vector", lambda e, qc2=qc2, h=h: e.tensor_scalar(
                    out1[:, qc2 * 1024 + 64 * h: qc2 * 1024 + 64 * (h + 1)],
                    psO2[:, 128 * qc2: 128 * qc2 + 64], rden[:, :], None, ALU.mult),
                    reads=[b_psO2, b_rden], writes=[b_out1])
            ctxS.close()

        if debug and b == 0:
            em.op("sync", lambda e: e.dma_start(out=d_o1dbg[:, :], in_=out1[:, :]),
                  reads=[b_out1], q="qs")

        ctxW = ExitStack()
        wrp_sb = ctxW.enter_context(nc.sbuf_tensor(f"wrp_sb_{b}", [128, 16 * 1024], BF16))
        b_wrp = Buf("wrp")
        fin = ctxW.enter_context(nc.sbuf_tensor(f"fin_{b}", [128, 4 * 1024], BF16))
        b_fin = Buf("fin")
        u_f = ctxW.enter_context(nc.sbuf_tensor(f"u_f_{b}", [128, 4096], F32))
        b_uf = Buf("u_f")
        psF = ctxW.enter_context(nc.psum_tensor(f"psF_{b}", [128, 512], F32))
        psT2 = ctxW.enter_context(nc.psum_tensor(f"psT2_{b}", [128, 128], BF16))
        psA2 = ctxW.enter_context(nc.psum_tensor(f"psA2_{b}", [128, 512], F32))
        psB2 = ctxW.enter_context(nc.psum_tensor(f"psB2_{b}", [128, 512], F32))
        psM2 = ctxW.enter_context(nc.psum_tensor(f"psM2_{b}", [128, 512], F32))
        psR3 = ctxW.enter_context(nc.psum_tensor(f"psR3_{b}", [128, 512], F32))
        b_psF = Buf("psF"); b_psT2 = Buf("psT2")
        b_psA2 = Buf("psA2"); b_psB2 = Buf("psB2"); b_psM2 = Buf("psM2"); b_psR3 = Buf("psR3")
        em.op("gpsimd", lambda e: e.dma_start(
            out=wrp_sb[0:64, :].rearrange("t (g n) -> t g n", g=16),
            in_=d_wrp.rearrange("g t n -> t g n")),
            writes=[b_wrp], q="qg")
        em.op("gpsimd", lambda e: e.dma_start(
            out=wrp_sb[64:128, :].rearrange("t (g n) -> t g n", g=16),
            in_=d_wrp.rearrange("g t n -> t g n")),
            writes=[b_wrp], q="qg")
        for u in range(2):
            for tg in range(2):
                for j in range(4):
                    tc = tg * 4 + j
                    for pc in range(2):
                        for g in range(16):
                            em.op("tensor", lambda e, u=u, tc=tc, pc=pc, g=g: e.matmul(
                                psF[:, :], wrp_sb[64 * u:64 * u + 64, g * 1024 + tc * 128:g * 1024 + (tc + 1) * 128],
                                out1[64 * u:64 * u + 64, g * 1024 + pc * 512:g * 1024 + pc * 512 + 512],
                                start=(g == 0), stop=(g == 15)),
                                reads=[b_wrp, b_out1], writes=[b_psF], sig=(g == 15))
                        em.op("vector", lambda e, j=j, pc=pc: e.tensor_copy(
                            fin[:, j * 1024 + pc * 512: j * 1024 + (pc + 1) * 512], psF[:, :]),
                            reads=[b_psF], writes=[b_fin])
                for j in range(4):
                    for cfeat in range(8):
                        em.op("tensor", lambda e, j=j, cfeat=cfeat: e.transpose(
                            psT2[:, :], fin[:, j * 1024 + cfeat * 128:j * 1024 + (cfeat + 1) * 128],
                            idb_sb[:, :]),
                            reads=[b_fin, b_idb], writes=[b_psT2])
                        em.op("vector", lambda e, j=j, cfeat=cfeat: e.tensor_copy(
                            u_f[:, cfeat * 512 + j * 128: cfeat * 512 + (j + 1) * 128], psT2[:, :]),
                            reads=[b_psT2], writes=[b_uf])
                t0 = 1024 * u + 512 * tg
                em.op("sync", lambda e, b=b, t0=t0: e.dma_start(
                    out=xt[:, :], in_=d_x2[b].rearrange("(c p) t -> p c t", p=128)[:, :, t0:t0 + 512]),
                    reads=[bd_in], writes=[b_xt], q="qs")
                emit_ln(em, C, S, psA2[0:1, :], psB2[0:1, :], psM2[:, :], psR3[:, :],
                        b_psA2, b_psB2, b_psM2, b_psR3,
                        src_ap=lambda c: u_f[:, c * 512:(c + 1) * 512], b_src=b_uf,
                        out_ap=None, b_out=None,
                        out32_ap=lambda c: xt[:, c * 512:(c + 1) * 512], b_out32=b_xt,
                        res_ap=lambda c: xt[:, c * 512:(c + 1) * 512], b_res=b_xt)
                em.op("sync", lambda e, b=b, t0=t0: e.dma_start(
                    out=d_x3[b].rearrange("(c p) t -> p c t", p=128)[:, :, t0:t0 + 512],
                    in_=xt[:, :].rearrange("p (c t) -> p c t", c=8)),
                    reads=[b_xt], writes=[bd_out], q="qs")
        ctxW.close()

    ctx.close()
    if F is None:
        em.block()
        qs_final = em.cnt["qs"]
        em.op("sync", lambda e: e.nop(), extra=[("qs", qs_final)], sig=False)
        em.close()
        octx.close()
        return nc
    return None


def p3_inputs(inp):
    wq = np.ascontiguousarray(np.asarray(inp["L1_attn_Wq"], np.float32).T)
    wk = np.ascontiguousarray(np.asarray(inp["L1_attn_Wk"], np.float32).T)
    wv = np.ascontiguousarray(np.asarray(inp["L1_attn_Wv"], np.float32).T)
    wr = np.asarray(inp["L1_attn_Wr"], np.float32)
    wrp = np.ascontiguousarray(wr.T.reshape(64, 16, 1024).transpose(1, 0, 2))
    return {"wqT": wq, "wkT": wk, "wvT": wv, "wrp": wrp}


def run_p3(x2_fm_cores, winp, debug=False):
    nc = build_p3(debug=debug)
    in_maps = []
    for c in range(NCORES):
        m = dict(winp)
        m["x2"] = x2_fm_cores[c]
        in_maps.append(m)
    res = run_bass_kernel_spmd(nc, in_maps, core_ids=list(range(NCORES)))
    if debug:
        return res.results
    return [r["x3"] for r in res.results]



def build_d1(F=None):
    if F is None:
        nc = bass.Bass("TRN2", target_bir_lowering=False)
        octx = ExitStack()
        em = Em(nc, octx)
        C = Consts(nc, em, octx)
        S = LnScratch(nc, octx)
        d_x3 = nc.dram_tensor("x3", [BPC, D, T], F32, kind="ExternalInput").ap()
        d_lg = nc.dram_tensor("lg", [NTOK, 8], F32, kind="ExternalOutput").ap()
        bd_in = Buf("bd_in"); bd_out = Buf("bd_out"); bd_lg = Buf("bd_lg")
    else:
        nc, em, C, S, streams, dbufs = F
        d_x3 = streams["x3"]; d_lg = streams["lg"]
        bd_in, bd_out = dbufs["x3"], dbufs["lg"]
    ctx = ExitStack()
    d_wr = nc.dram_tensor("wrc", [128, 8, 8], F32, kind="ExternalInput").ap()
    d_idf = nc.inline_tensor(np.eye(8, dtype=np.float32), "cd1_id8f").ap()

    xt = ctx.enter_context(nc.sbuf_tensor("d1_xt", [128, 4096], F32))
    xn = ctx.enter_context(nc.sbuf_tensor("d1_xn", [128, 4096], BF16))
    xnlo = ctx.enter_context(nc.sbuf_tensor("d1_xnlo", [128, 4096], BF16))
    wr_f = ctx.enter_context(nc.sbuf_tensor("d1_wr_f", [128, 64], F32))
    wr_hi = ctx.enter_context(nc.sbuf_tensor("d1_wr_hi", [128, 64], BF16))
    wr_lo = ctx.enter_context(nc.sbuf_tensor("d1_wr_lo", [128, 64], BF16))
    id8 = ctx.enter_context(nc.sbuf_tensor("d1_id8", [8, 8], F32))
    lgo = ctx.enter_context(nc.sbuf_tensor("d1_lgo", [128, 8], F32))
    b_xt = Buf("xt"); b_xn = Buf("xn"); b_xnlo = Buf("xnlo")
    b_wrf = Buf("wr_f"); b_wrh = Buf("wr_hi"); b_wrl = Buf("wr_lo")
    b_id8 = Buf("id8"); b_lgo = Buf("lgo")

    ps1 = ctx.enter_context(nc.psum_tensor("d1_ps1", [128, 512], F32))
    ps2 = ctx.enter_context(nc.psum_tensor("d1_ps2", [128, 512], F32))
    ps3 = ctx.enter_context(nc.psum_tensor("d1_ps3", [128, 512], F32))
    ps4 = ctx.enter_context(nc.psum_tensor("d1_ps4", [128, 512], F32))
    b_ps1 = Buf("ps1"); b_ps2 = Buf("ps2"); b_ps3 = Buf("ps3"); b_ps4 = Buf("ps4")

    em.block()
    C.load(em)
    em.op("sync", lambda e: e.dma_start(out=wr_f[:, :], in_=d_wr.rearrange("p c e -> p (c e)")),
          writes=[b_wrf], q="qs")
    em.op("sync", lambda e: e.dma_start(out=id8[:, :], in_=d_idf[:, :]), writes=[b_id8], q="qs")
    em.op("vector", lambda e: e.tensor_copy(wr_hi[:, :], wr_f[:, :]), reads=[b_wrf], writes=[b_wrh])
    em.op("vector", lambda e: e.tensor_tensor(wr_lo[:, :], wr_f[:, :], wr_hi[:, :], ALU.subtract),
          reads=[b_wrf, b_wrh], writes=[b_wrl])

    for b in range(BPC):
        for ti in range(4):
            t0 = ti * 512
            em.op("sync", lambda e, b=b, t0=t0: e.dma_start(
                out=xt[:, :], in_=d_x3[b].rearrange("(c p) t -> p c t", p=128)[:, :, t0:t0 + 512]),
                reads=[bd_in], writes=[b_xt], q="qs")
            emit_ln(em, C, S, ps1[0:1, :], ps2[0:1, :], ps3[:, :], ps4[:, :],
                    b_ps1, b_ps2, b_ps3, b_ps4,
                    src_ap=lambda c: xt[:, c * 512:(c + 1) * 512], b_src=b_xt,
                    out_ap=lambda c: xn[:, c * 512:(c + 1) * 512], b_out=b_xn,
                    lo_ap=lambda c: xnlo[:, c * 512:(c + 1) * 512], b_lo=b_xnlo)
            groups = [(wr_hi, b_wrh, True), (wr_lo, b_wrl, True), (wr_hi, b_wrh, False)]
            for gi, (g_sb, b_g, x_hi) in enumerate(groups):
                for kc in range(8):
                    src = xn[:, kc * 512:(kc + 1) * 512] if x_hi else xnlo[:, kc * 512:(kc + 1) * 512]
                    em.op("tensor", lambda e, kc=kc, g_sb=g_sb, src=src, gi=gi: e.matmul(
                        ps3[0:8, :], g_sb[:, kc * 8:(kc + 1) * 8], src,
                        start=(gi == 0 and kc == 0), stop=(gi == 2 and kc == 7),
                        skip_group_check=True),
                        reads=[b_g, b_xn, b_xnlo], writes=[b_ps3],
                        sig=(gi == 2 and kc == 7))
            em.op("vector", lambda e: e.tensor_copy(S.t1[0:8, :], ps3[0:8, :]),
                  reads=[b_ps3], writes=[S.b_t1])
            for g in range(4):
                em.op("tensor", lambda e, g=g: e.transpose(
                    ps4[0:128, 0:8], S.t1[0:8, g * 128:(g + 1) * 128], id8[:, :]),
                    reads=[S.b_t1, b_id8], writes=[b_ps4])
                em.op("vector", lambda e: e.tensor_copy(lgo[:, :], ps4[0:128, 0:8]),
                      reads=[b_ps4], writes=[b_lgo])
                em.op("sync", lambda e, b=b, t0=t0, g=g: e.dma_start(
                    out=d_lg[b * T + t0 + g * 128: b * T + t0 + (g + 1) * 128, :], in_=lgo[:, :]),
                    reads=[b_lgo], writes=[bd_out], q="qs")

    ctx.close()
    if F is None:
        em.block()
        qs_final = em.cnt["qs"]
        em.op("sync", lambda e: e.nop(), extra=[("qs", qs_final)], sig=False)
        em.close()
        octx.close()
        return nc
    return None



CAP = 4096
RMS_EPS = 1e-6

def build_d2(debug=False, F=None):
    NTOT = 32 * T
    NGRP = NTOT // 128
    NOWN = NTOK // 128
    if F is None:
        nc = bass.Bass("TRN2", target_bir_lowering=False)
        octx = ExitStack()
        em = Em(nc, octx)
        C = Consts(nc, em, octx)
        S = LnScratch(nc, octx)
        d_x3 = nc.dram_tensor("x3", [BPC, D, T], F32, kind="ExternalInput").ap()
        d_x4 = nc.dram_tensor("x4", [BPC, D, T], F32, kind="ExternalOutput").ap()
        d_lgown = nc.dram_tensor("lgown", [NTOK, 8], F32, kind="ExternalInput").ap()
        bd_in = Buf("bd_in"); bd_out = Buf("bd_out"); bd_lg = Buf("bd_lg")
        d_lgall = nc.dram_tensor("lgall", [NTOT, 8], F32, kind="ExternalInput").ap()
        d_gsel = nc.dram_tensor("gsel", [128, NGRP], F32, kind="ExternalInput").ap()
    else:
        nc, em, C, S, streams, dbufs = F
        d_x3 = streams["x3"]; d_x4 = streams["x4"]; d_lgown = streams["lg"]
        bd_in, bd_out, bd_lg = dbufs["x3"], dbufs["x4"], dbufs["lg"]
        d_gselc = nc.dram_tensor("gselc", [128, 32], F32, kind="ExternalInput").ap()
        h_totin = nc.dram_tensor("totin", [1, 512], F32, kind="Internal")
        h_totout = nc.dram_tensor("totout", [8, 512], F32, kind="Internal")
        d_totin = h_totin.ap(); d_totout = h_totout.ap()
    ctx = ExitStack()
    d_nw = nc.dram_tensor("nwc", [8, 128, 8], F32, kind="ExternalInput").ap()
    d_wg = nc.dram_tensor("wg", [8, D, 2048], F32, kind="ExternalInput").ap()
    d_wu = nc.dram_tensor("wu", [8, D, 2048], F32, kind="ExternalInput").ap()
    d_wd = nc.dram_tensor("wd", [8, 2048, D], F32, kind="ExternalInput").ap()
    d_fn = nc.dram_tensor("fnc", [128, 8], F32, kind="ExternalInput").ap()
    d_fwg = nc.dram_tensor("fwg", [D, 512], F32, kind="ExternalInput").ap()
    d_fwu = nc.dram_tensor("fwu", [D, 512], F32, kind="ExternalInput").ap()
    d_fwd = nc.dram_tensor("fwd", [512, D], F32, kind="ExternalInput").ap()

    iota_np = np.broadcast_to(np.arange(8, dtype=np.float32)[None, :], (128, 8)).copy()
    d_iota = nc.inline_tensor(iota_np, "cd2_iota").ap()
    d_iotab = nc.inline_tensor(iota_np + 1e6, "cd2_iotab").ap()
    d_triu = nc.inline_tensor(np.triu(np.ones((128, 128), dtype=BF), 1), "cd2_triu").ap()
    d_idb = nc.inline_tensor(np.eye(128, dtype=BF), "cd2_identbf").ap()
    sel_np = np.zeros((9, 9 * 128), dtype=BF)
    for r_ in range(9):
        sel_np[r_, r_ * 128:(r_ + 1) * 128] = 1
    d_sel = nc.inline_tensor(sel_np, "cd2_sel9").ap()
    if debug:
        d_cedbg = nc.dram_tensor("cedbg", [9, NTOK], F32, kind="ExternalOutput").ap()

    xt = ctx.enter_context(nc.sbuf_tensor("d2_xt", [128, 4096], F32))
    xn = ctx.enter_context(nc.sbuf_tensor("d2_xn", [128, 4096], BF16))
    y = ctx.enter_context(nc.sbuf_tensor("d2_y", [128, 4096], F32))
    xnr = ctx.enter_context(nc.sbuf_tensor("d2_xnr", [128, 4096], BF16))
    gu = ctx.enter_context(nc.sbuf_tensor("d2_gu", [128, 8192], BF16))
    hE = ctx.enter_context(nc.sbuf_tensor("d2_hE", [128, 4096], BF16))
    rW = Roller(nc, ctx, "d2_wbuf", [128, 8192], BF16, 4)
    rnw = Roller(nc, ctx, "d2_nw", [128, 8], F32, 2)
    fn_sb = ctx.enter_context(nc.sbuf_tensor("d2_fn_sb", [128, 8], F32))
    iota_sb = ctx.enter_context(nc.sbuf_tensor("d2_iota_sb", [128, 8], F32))
    iotab_sb = ctx.enter_context(nc.sbuf_tensor("d2_iotab_sb", [128, 8], F32))
    triu_sb = ctx.enter_context(nc.sbuf_tensor("d2_triu_sb", [128, 128], BF16))
    idb_sb = ctx.enter_context(nc.sbuf_tensor("d2_idb_sb", [128, 128], BF16))
    sel_sb = ctx.enter_context(nc.sbuf_tensor("d2_sel_sb", [9, 9 * 128], BF16))
    gsel_sb = ctx.enter_context(nc.sbuf_tensor("d2_gsel_sb", [128, NGRP], F32)) if F is None else None
    lg_sb = ctx.enter_context(nc.sbuf_tensor("d2_lg_sb", [128, 8], F32))
    rt = ctx.enter_context(nc.sbuf_tensor("d2_rt", [128, 80], F32))
    offb = ctx.enter_context(nc.sbuf_tensor("d2_offb", [128, 8], F32))
    totb = ctx.enter_context(nc.sbuf_tensor("d2_totb", [1, 8], BF16))
    ohcb = ctx.enter_context(nc.sbuf_tensor("d2_ohcb", [128, 8], BF16))
    stk = ctx.enter_context(nc.sbuf_tensor("d2_stk", [128, 9], BF16))
    ceT = ctx.enter_context(nc.sbuf_tensor("d2_ceT", [9, NTOK], BF16))
    tsc = ctx.enter_context(nc.sbuf_tensor("d2_tsc", [128, 512], F32))
    tsc2 = ctx.enter_context(nc.sbuf_tensor("d2_tsc2", [128, 512], F32))

    b_xt = Buf("xt"); b_xn = Buf("xn"); b_y = Buf("y"); b_xnr = Buf("xnr")
    b_gu = Buf("gu"); b_hE = Buf("hE")
    b_fn = Buf("fn"); b_iota = Buf("iota"); b_iotab = Buf("iotab")
    b_triu = Buf("triu"); b_idb = Buf("idb"); b_sel = Buf("sel"); b_gsel = Buf("gsel")
    b_lg = Buf("lg"); b_rt = Buf("rt"); b_offb = Buf("offb"); b_totb = Buf("totb")
    b_ohcb = Buf("ohcb"); b_stk = Buf("stk"); b_ceT = Buf("ceT")
    b_tsc = Buf("tsc"); b_tsc2 = Buf("tsc2")

    ps1 = ctx.enter_context(nc.psum_tensor("d2_ps1", [128, 512], F32))
    ps2 = ctx.enter_context(nc.psum_tensor("d2_ps2", [128, 512], F32))
    ps3 = ctx.enter_context(nc.psum_tensor("d2_ps3", [128, 512], F32))
    ps4 = ctx.enter_context(nc.psum_tensor("d2_ps4", [128, 512], F32))
    ps5 = ctx.enter_context(nc.psum_tensor("d2_ps5", [128, 512], F32))
    ps6 = ctx.enter_context(nc.psum_tensor("d2_ps6", [128, 1024], BF16))
    b_ps1 = Buf("ps1"); b_ps2 = Buf("ps2"); b_ps3 = Buf("ps3")
    b_ps4 = Buf("ps4"); b_ps5 = Buf("ps5"); b_ps6 = Buf("ps6")

    em.block()
    C.load(em)
    for sb, dt_, bf in ((iota_sb, d_iota, b_iota), (iotab_sb, d_iotab, b_iotab),
                        (triu_sb, d_triu, b_triu), (idb_sb, d_idb, b_idb),
                        (sel_sb, d_sel, b_sel), (fn_sb, d_fn, b_fn)):
        em.op("sync", lambda e, sb=sb, dt_=dt_: e.dma_start(out=sb[:, :], in_=dt_[:, :]),
              writes=[bf], q="qs")
    if F is None:
        em.op("sync", lambda e: e.dma_start(out=gsel_sb[:, :], in_=d_gsel[:, :]), writes=[b_gsel], q="qs")
    em.op("vector", lambda e: e.memset(offb[:, :], 0.0), writes=[b_offb])

    r = rt
    nmx = r[:, 0:1]; ssum = r[:, 1:2]; rs = r[:, 2:3]
    m1 = r[:, 3:4]; idx1 = r[:, 4:5]; m2 = r[:, 5:6]; idx2 = r[:, 6:7]
    probs = r[:, 8:16]; eq = r[:, 16:24]; u1 = r[:, 24:32]
    oh1 = r[:, 32:40]; oh2 = r[:, 40:48]; p2 = r[:, 48:56]
    ohc = r[:, 56:64]; rank1 = r[:, 64:65]; rank2 = r[:, 65:66]
    keep1 = r[:, 66:67]; keep2 = r[:, 67:68]; ce = r[:, 68:76]; ov = r[:, 76:77]

    def emit_route(src_lg):
        em.op("vector", lambda e: e.reduce_max(nmx, src_lg, axis=AX.X, negate=True),
              reads=[b_lg], writes=[b_rt])
        em.op("scalar", lambda e: e.activation(probs, src_lg, AF.Exp, bias=nmx, accum_out=ssum),
              reads=[b_lg, b_rt], writes=[b_rt])
        em.op("vector", lambda e: e.reciprocal(rs, ssum), reads=[b_rt], writes=[b_rt])
        em.op("vector", lambda e: e.tensor_scalar(probs, probs, rs, None, ALU.mult),
              reads=[b_rt], writes=[b_rt])
        em.op("vector", lambda e: e.reduce_max(m1, probs, axis=AX.X), reads=[b_rt], writes=[b_rt])
        em.op("vector", lambda e: e.tensor_scalar(eq, probs, m1, None, ALU.is_ge),
              reads=[b_rt], writes=[b_rt])
        em.op("vector", lambda e: e.scalar_tensor_tensor(u1, eq, -1e6, iotab_sb[:, :], ALU.mult, ALU.add),
              reads=[b_rt, b_iotab], writes=[b_rt])
        em.op("vector", lambda e: e.tensor_reduce(idx1, u1, AX.X, ALU.min), reads=[b_rt], writes=[b_rt])
        em.op("vector", lambda e: e.tensor_scalar(oh1, iota_sb[:, :], idx1, None, ALU.is_equal),
              reads=[b_rt, b_iota], writes=[b_rt])
        em.op("vector", lambda e: e.scalar_tensor_tensor(p2, oh1, -1e6, probs, ALU.mult, ALU.add),
              reads=[b_rt], writes=[b_rt])
        em.op("vector", lambda e: e.reduce_max(m2, p2, axis=AX.X), reads=[b_rt], writes=[b_rt])
        em.op("vector", lambda e: e.tensor_scalar(eq, p2, m2, None, ALU.is_ge),
              reads=[b_rt], writes=[b_rt])
        em.op("vector", lambda e: e.scalar_tensor_tensor(u1, eq, -1e6, iotab_sb[:, :], ALU.mult, ALU.add),
              reads=[b_rt, b_iotab], writes=[b_rt])
        em.op("vector", lambda e: e.tensor_reduce(idx2, u1, AX.X, ALU.min), reads=[b_rt], writes=[b_rt])
        em.op("vector", lambda e: e.tensor_scalar(oh2, iota_sb[:, :], idx2, None, ALU.is_equal),
              reads=[b_rt, b_iota], writes=[b_rt])
        em.op("vector", lambda e: e.tensor_tensor(ohc, oh1, oh2, ALU.add), reads=[b_rt], writes=[b_rt])
        em.op("vector", lambda e: e.tensor_copy(ohcb[:, :], ohc), reads=[b_rt], writes=[b_ohcb])

    def emit_tot_bcast():
        em.op("tensor", lambda e: e.matmul(ps3[0:1, 0:8], C.ones128[:, :], ohcb[:, :],
                                           start=True, stop=True),
              reads=[C.b_ones128, b_ohcb], writes=[b_ps3])
        em.op("vector", lambda e: e.tensor_copy(totb[:, :], ps3[0:1, 0:8]),
              reads=[b_ps3], writes=[b_totb])
        em.op("tensor", lambda e: e.matmul(ps4[0:128, 0:8], C.ones1x128[:, :], totb[:, :],
                                           start=True, stop=True),
              reads=[C.b_ones1x128, b_totb], writes=[b_ps4])

    if F is None:
        for g in range(NGRP):
            em.op("sync", lambda e, g=g: e.dma_start(
                out=lg_sb[:, :], in_=d_lgall[g * 128:(g + 1) * 128, :]),
                writes=[b_lg], q="qs")
            emit_route(lg_sb[:, :])
            emit_tot_bcast()
            em.op("vector", lambda e, g=g: e.scalar_tensor_tensor(
                offb[:, :], ps4[0:128, 0:8], gsel_sb[:, g:g + 1], offb[:, :], ALU.mult, ALU.add),
                reads=[b_ps4, b_gsel, b_offb], writes=[b_offb])
    else:
        totrow = ctx.enter_context(nc.sbuf_tensor("d2_totrow", [1, 512], F32))
        totg = ctx.enter_context(nc.sbuf_tensor("d2_totg", [128, 32], F32))
        totgb = ctx.enter_context(nc.sbuf_tensor("d2_totgb", [128, 32], BF16))
        gselc_sb = ctx.enter_context(nc.sbuf_tensor("d2_gselc", [128, 32], F32))
        baser = ctx.enter_context(nc.sbuf_tensor("d2_baser", [1, 48], F32))
        baseb = ctx.enter_context(nc.sbuf_tensor("d2_baseb", [1, 16], BF16))
        b_totrow = Buf("totrow"); b_totg = Buf("totg"); b_totgb = Buf("totgb")
        b_gselc = Buf("gselc"); b_baser = Buf("baser"); b_baseb = Buf("baseb")
        b_totind = Buf("totin_d"); b_totoutd = Buf("totout_d")
        em.op("sync", lambda e: e.dma_start(out=gselc_sb[:, :], in_=d_gselc[:, :]),
              writes=[b_gselc], q="qs")
        for g in range(NOWN):
            em.op("sync", lambda e, g=g: e.dma_start(
                out=lg_sb[:, :], in_=d_lgown[g * 128:(g + 1) * 128, :]),
                reads=[bd_lg], writes=[b_lg], q="qs")
            emit_route(lg_sb[:, :])
            em.op("tensor", lambda e: e.matmul(ps3[0:1, 0:8], C.ones128[:, :], ohcb[:, :],
                                               start=True, stop=True),
                  reads=[C.b_ones128, b_ohcb], writes=[b_ps3])
            em.op("vector", lambda e, g=g: e.tensor_copy(totrow[:, g * 8:(g + 1) * 8], ps3[0:1, 0:8]),
                  reads=[b_ps3], writes=[b_totrow])
        em.op("sync", lambda e: e.dma_start(out=d_totin[:, :], in_=totrow[:, :]),
              reads=[b_totrow], writes=[b_totind], q="qs")
        em.op("gpsimd", lambda e: e.collective_compute(
            "AllGather", ALU.bypass, replica_groups=[list(range(NCORES))],
            ins=[h_totin.ap().opt()], outs=[h_totout.ap().opt()]),
            reads=[b_totind], writes=[b_totoutd])
        em.op("sync", lambda e: e.dma_start(
            out=totg[:, :].rearrange("p (ci e) -> p ci e", e=8),
            in_=d_totout.rearrange("r (g e) -> (r g) e", e=8).rearrange("(ci p) e -> p ci e", p=128)),
            reads=[b_totoutd], writes=[b_totg], q="qs")
        em.op("vector", lambda e: e.tensor_tensor(totg[:, :], totg[:, :], gselc_sb[:, :], ALU.mult),
              reads=[b_totg, b_gselc], writes=[b_totg])
        em.op("vector", lambda e: e.tensor_copy(totgb[:, :], totg[:, :]),
              reads=[b_totg], writes=[b_totgb])
        em.op("tensor", lambda e: e.matmul(ps3[0:1, 0:32], C.ones128[:, :], totgb[:, :],
                                           start=True, stop=True),
              reads=[C.b_ones128, b_totgb], writes=[b_ps3])
        em.op("vector", lambda e: e.tensor_copy(baser[:, 0:32], ps3[0:1, 0:32]),
              reads=[b_ps3], writes=[b_baser])
        em.op("vector", lambda e: e.tensor_tensor(baser[:, 32:40], baser[:, 0:8], baser[:, 8:16], ALU.add),
              reads=[b_baser], writes=[b_baser])
        em.op("vector", lambda e: e.tensor_tensor(baser[:, 40:48], baser[:, 16:24], baser[:, 24:32], ALU.add),
              reads=[b_baser], writes=[b_baser])
        em.op("vector", lambda e: e.tensor_tensor(baser[:, 32:40], baser[:, 32:40], baser[:, 40:48], ALU.add),
              reads=[b_baser], writes=[b_baser])
        em.op("vector", lambda e: e.tensor_copy(baseb[:, 0:8], baser[:, 32:40]),
              reads=[b_baser], writes=[b_baseb])
        em.op("vector", lambda e: e.tensor_tensor(baser[:, 40:48], baser[:, 32:40], baseb[:, 0:8], ALU.subtract),
              reads=[b_baser, b_baseb], writes=[b_baser])
        em.op("vector", lambda e: e.tensor_copy(baseb[:, 8:16], baser[:, 40:48]),
              reads=[b_baser], writes=[b_baseb])
        em.op("tensor", lambda e: e.matmul(ps4[0:128, 0:8], C.ones1x128[:, :], baseb[:, 0:8],
                                           start=True, stop=False, skip_group_check=True),
              reads=[C.b_ones1x128, b_baseb], writes=[b_ps4], sig=False)
        em.op("tensor", lambda e: e.matmul(ps4[0:128, 0:8], C.ones1x128[:, :], baseb[:, 8:16],
                                           start=False, stop=True, skip_group_check=True),
              reads=[C.b_ones1x128, b_baseb], writes=[b_ps4])
        em.op("vector", lambda e: e.tensor_copy(offb[:, :], ps4[0:128, 0:8]),
              reads=[b_ps4], writes=[b_offb])

    for g in range(NOWN):
        em.op("sync", lambda e, g=g: e.dma_start(
            out=lg_sb[:, :], in_=d_lgown[g * 128:(g + 1) * 128, :]),
            reads=[bd_lg], writes=[b_lg], q="qs")
        emit_route(lg_sb[:, :])
        em.op("tensor", lambda e: e.matmul(ps5[0:128, 0:8], triu_sb[:, :], ohcb[:, :],
                                           start=True, stop=True),
              reads=[b_triu, b_ohcb], writes=[b_ps5])
        em.op("vector", lambda e: e.tensor_tensor(r[:, 8:16] if False else probs, ps5[0:128, 0:8], offb[:, :], ALU.add),
              reads=[b_ps5, b_offb], writes=[b_rt])
        ranks = probs
        em.op("vector", lambda e: e.tensor_tensor(u1, oh1, ranks, ALU.mult), reads=[b_rt], writes=[b_rt])
        em.op("vector", lambda e: e.reduce_sum(rank1, u1, axis=AX.X), reads=[b_rt], writes=[b_rt])
        em.op("vector", lambda e: e.tensor_tensor(u1, oh2, ranks, ALU.mult), reads=[b_rt], writes=[b_rt])
        em.op("vector", lambda e: e.reduce_sum(rank2, u1, axis=AX.X), reads=[b_rt], writes=[b_rt])
        em.op("vector", lambda e: e.tensor_single_scalar(keep1, rank1, float(CAP), ALU.is_lt),
              reads=[b_rt], writes=[b_rt])
        em.op("vector", lambda e: e.tensor_single_scalar(keep2, rank2, float(CAP), ALU.is_lt),
              reads=[b_rt], writes=[b_rt])
        em.op("vector", lambda e: e.tensor_tensor(rank1, keep1, m1, ALU.mult), reads=[b_rt], writes=[b_rt])
        em.op("vector", lambda e: e.tensor_tensor(rank2, keep2, m2, ALU.mult), reads=[b_rt], writes=[b_rt])
        em.op("vector", lambda e: e.tensor_scalar(ce, oh1, rank1, None, ALU.mult),
              reads=[b_rt], writes=[b_rt])
        em.op("vector", lambda e: e.tensor_scalar(u1, oh2, rank2, None, ALU.mult),
              reads=[b_rt], writes=[b_rt])
        em.op("vector", lambda e: e.tensor_tensor(ce, ce, u1, ALU.add), reads=[b_rt], writes=[b_rt])
        em.op("vector", lambda e: e.tensor_tensor(ov, keep1, keep2, ALU.mult), reads=[b_rt], writes=[b_rt])
        em.op("vector", lambda e: e.tensor_scalar(ov, ov, -1.0, 1.0, ALU.mult, ALU.add),
              reads=[b_rt], writes=[b_rt])
        em.op("vector", lambda e: e.tensor_copy(stk[:, 0:8], ce), reads=[b_rt], writes=[b_stk])
        em.op("vector", lambda e: e.tensor_copy(stk[:, 8:9], ov), reads=[b_rt], writes=[b_stk])
        em.op("tensor", lambda e: e.transpose(ps6[0:9, 0:128], stk[:, :], idb_sb[:, :]),
              reads=[b_stk, b_idb], writes=[b_ps6])
        em.op("vector", lambda e, g=g: e.tensor_copy(ceT[:, g * 128:(g + 1) * 128], ps6[0:9, 0:128]),
              reads=[b_ps6], writes=[b_ceT])
        emit_tot_bcast()
        em.op("vector", lambda e: e.tensor_tensor(offb[:, :], offb[:, :], ps4[0:128, 0:8], ALU.add),
              reads=[b_ps4, b_offb], writes=[b_offb])

    if debug:
        for j in range(2):
            em.op("vector", lambda e, j=j: e.tensor_copy(S.t1[0:9, :], ceT[:, j * 512:(j + 1) * 512]),
                  reads=[b_ceT], writes=[S.b_t1])
            em.op("sync", lambda e, j=j: e.dma_start(out=d_cedbg[:, j * 512:(j + 1) * 512], in_=S.t1[0:9, :]),
                  reads=[S.b_t1], q="qs")

    em.block()
    for sti in range(16):
        b0 = (sti * 512) // T
        t0 = (sti * 512) % T
        co = sti * 512
        em.op("sync", lambda e, b0=b0, t0=t0: e.dma_start(
            out=xt[:, :], in_=d_x3[b0].rearrange("(c p) t -> p c t", p=128)[:, :, t0:t0 + 512]),
            reads=[bd_in], writes=[b_xt], q="qs")
        emit_ln(em, C, S, ps1[0:1, :], ps2[0:1, :], ps3[:, :], ps4[:, :],
                b_ps1, b_ps2, b_ps3, b_ps4,
                src_ap=lambda c: xt[:, c * 512:(c + 1) * 512], b_src=b_xt,
                out_ap=lambda c: xn[:, c * 512:(c + 1) * 512], b_out=b_xn,
                out32_ap=lambda c: y[:, c * 512:(c + 1) * 512], b_out32=b_y)
        for c in range(8):
            em.op("scalar", lambda e, c=c: e.activation(S.sq[:, c * 512:(c + 1) * 512],
                                                        xn[:, c * 512:(c + 1) * 512], AF.Square),
                  reads=[b_xn], writes=[S.b_sq])
        for c in range(8):
            em.op("tensor", lambda e, c=c: e.matmul(ps1[0:1, :], C.ones128[:, :],
                                                    S.sq[:, c * 512:(c + 1) * 512],
                                                    start=(c == 0), stop=(c == 7)),
                  reads=[S.b_sq, C.b_ones128], writes=[b_ps1], sig=(c == 7))
        st_ = S.stat
        em.op("vector", lambda e: e.tensor_scalar(st_[:, 0:512], ps1[0:1, :], 1.0 / D, float(RMS_EPS),
                                                  ALU.mult, ALU.add),
              reads=[b_ps1], writes=[S.b_stat])
        em.op("scalar", lambda e: e.activation(st_[:, 512:1024], st_[:, 0:512], AF.Sqrt),
              reads=[S.b_stat], writes=[S.b_stat])
        em.op("vector", lambda e: e.reciprocal(st_[:, 1024:1536], st_[:, 512:1024]),
              reads=[S.b_stat], writes=[S.b_stat])
        em.op("vector", lambda e: e.tensor_copy(S.statb[:, 0:512], st_[:, 1024:1536]),
              reads=[S.b_stat], writes=[S.b_statb])
        em.op("tensor", lambda e: e.matmul(ps4[:, :], C.ones1x128[:, :], S.statb[:, 0:512],
                                           start=True, stop=True),
              reads=[S.b_statb, C.b_ones1x128], writes=[b_ps4])
        for c in range(8):
            em.op("vector", lambda e, c=c: e.tensor_tensor(
                xnr[:, c * 512:(c + 1) * 512], xn[:, c * 512:(c + 1) * 512], ps4[:, :], ALU.mult),
                reads=[b_xn, b_ps4], writes=[b_xnr])
        for e_i in range(8):
            nw_sb, b_nw = rnw.next()
            em.op("sync", lambda e, e_i=e_i, nw_sb=nw_sb: e.dma_start(out=nw_sb[:, :], in_=d_nw[e_i]),
                  writes=[b_nw], q="qs")
            wg0, b_wg0 = rW.next(); wu0, b_wu0 = rW.next()
            wg1, b_wg1 = rW.next(); wu1, b_wu1 = rW.next()
            gus = ((wg0, b_wg0, wu0, b_wu0), (wg1, b_wg1, wu1, b_wu1))
            for Fh in range(2):
                wg, b_wg, wu, b_wu = gus[Fh]
                em.op("gpsimd", lambda e, e_i=e_i, Fh=Fh, wg=wg: e.dma_start(
                    out=wg[:, :], in_=d_wg[e_i].rearrange("(kc p) n -> p kc n", p=128)[:, :, Fh * 1024:(Fh + 1) * 1024]),
                    writes=[b_wg], q="qg")
                em.op("gpsimd", lambda e, e_i=e_i, Fh=Fh, wu=wu: e.dma_start(
                    out=wu[:, :], in_=d_wu[e_i].rearrange("(kc p) n -> p kc n", p=128)[:, :, Fh * 1024:(Fh + 1) * 1024]),
                    writes=[b_wu], q="qg")
            for c in range(8):
                em.op("vector", lambda e, c=c, nw_sb=nw_sb: e.tensor_scalar(
                    hE[:, c * 512:(c + 1) * 512], xnr[:, c * 512:(c + 1) * 512],
                    nw_sb[:, c:c + 1], None, ALU.mult),
                    reads=[b_xnr, b_nw], writes=[b_hE])
            wd0 = wd1 = b_wd0 = b_wd1 = None
            for Fh in range(2):
                wg, b_wg, wu, b_wu = gus[Fh]
                if Fh == 1:
                    wd0, b_wd0 = rW.next(); wd1, b_wd1 = rW.next()
                    em.op("gpsimd", lambda e, e_i=e_i, wd0=wd0: e.dma_start(
                        out=wd0[:, :], in_=d_wd[e_i].rearrange("(kc p) n -> p kc n", p=128)[:, 0:8, :]),
                        writes=[b_wd0], q="qg")
                    em.op("gpsimd", lambda e, e_i=e_i, wd1=wd1: e.dma_start(
                        out=wd1[:, :], in_=d_wd[e_i].rearrange("(kc p) n -> p kc n", p=128)[:, 8:16, :]),
                        writes=[b_wd1], q="qg")
                for fc in range(8):
                    for kc in range(8):
                        em.op("tensor", lambda e, fc=fc, kc=kc, wg=wg: e.matmul(
                            ps1[:, :], wg[:, kc * 1024 + fc * 128: kc * 1024 + (fc + 1) * 128],
                            hE[:, kc * 512:(kc + 1) * 512],
                            start=(kc == 0), stop=(kc == 7)),
                            reads=[b_wg, b_hE], writes=[b_ps1], sig=(kc == 7))
                    em.op("scalar", lambda e, fc=fc: e.activation(tsc[:, :], ps1[:, :], AF.Gelu),
                          reads=[b_ps1], writes=[b_tsc])
                    for kc in range(8):
                        em.op("tensor", lambda e, fc=fc, kc=kc, wu=wu: e.matmul(
                            ps2[:, :], wu[:, kc * 1024 + fc * 128: kc * 1024 + (fc + 1) * 128],
                            hE[:, kc * 512:(kc + 1) * 512],
                            start=(kc == 0), stop=(kc == 7)),
                            reads=[b_wu, b_hE], writes=[b_ps2], sig=(kc == 7))
                    em.op("vector", lambda e, fc=fc, Fh=Fh: e.tensor_tensor(
                        gu[:, (Fh * 8 + fc) * 512:(Fh * 8 + fc + 1) * 512],
                        ps2[:, :], tsc[:, :], ALU.mult),
                        reads=[b_ps2, b_tsc], writes=[b_gu])
            em.op("tensor", lambda e, e_i=e_i, co=co: e.matmul(
                ps5[:, :], sel_sb[:, e_i * 128:(e_i + 1) * 128], ceT[:, co:co + 512],
                start=True, stop=True),
                reads=[b_sel, b_ceT], writes=[b_ps5])
            for dd in range(8):
                for ic in range(16):
                    wb = wd0 if ic < 8 else wd1
                    b_wb = b_wd0 if ic < 8 else b_wd1
                    em.op("tensor", lambda e, dd=dd, ic=ic, wb=wb: e.matmul(
                        ps1[:, :], wb[:, (ic % 8) * 1024 + dd * 128:(ic % 8) * 1024 + (dd + 1) * 128],
                        gu[:, ic * 512:(ic + 1) * 512],
                        start=(ic == 0), stop=(ic == 15)),
                        reads=[b_wb, b_gu], writes=[b_ps1], sig=(ic == 15))
                em.op("vector", lambda e, dd=dd: e.tensor_tensor(
                    tsc[:, :], ps1[:, :], xn[:, dd * 512:(dd + 1) * 512], ALU.add),
                    reads=[b_ps1, b_xn], writes=[b_tsc])
                em.op("vector", lambda e: e.tensor_tensor(tsc[:, :], tsc[:, :], ps5[:, :], ALU.mult),
                      reads=[b_tsc, b_ps5], writes=[b_tsc])
                em.op("vector", lambda e, dd=dd: e.tensor_tensor(
                    y[:, dd * 512:(dd + 1) * 512], y[:, dd * 512:(dd + 1) * 512], tsc[:, :], ALU.add),
                    reads=[b_tsc], writes=[b_y])
        for c in range(8):
            em.op("vector", lambda e, c=c: e.tensor_scalar(
                hE[:, c * 512:(c + 1) * 512], xnr[:, c * 512:(c + 1) * 512],
                fn_sb[:, c:c + 1], None, ALU.mult),
                reads=[b_xnr, b_fn], writes=[b_hE])
        fwg_sb, b_fwg = rW.next(); fwu_sb, b_fwu = rW.next(); fwd_sb, b_fwd = rW.next()
        em.op("gpsimd", lambda e, fwg_sb=fwg_sb: e.dma_start(
            out=fwg_sb[:, 0:4096], in_=d_fwg.rearrange("(kc p) n -> p kc n", p=128)),
            writes=[b_fwg], q="qg")
        em.op("gpsimd", lambda e, fwu_sb=fwu_sb: e.dma_start(
            out=fwu_sb[:, 0:4096], in_=d_fwu.rearrange("(kc p) n -> p kc n", p=128)),
            writes=[b_fwu], q="qg")
        em.op("gpsimd", lambda e, fwd_sb=fwd_sb: e.dma_start(
            out=fwd_sb[:, 0:4096], in_=d_fwd.rearrange("(kc p) n -> p kc n", p=128)),
            writes=[b_fwd], q="qg")
        for fc in range(4):
            for kc in range(8):
                em.op("tensor", lambda e, fc=fc, kc=kc, fwg_sb=fwg_sb: e.matmul(
                    ps1[:, :], fwg_sb[:, kc * 512 + fc * 128: kc * 512 + (fc + 1) * 128],
                    hE[:, kc * 512:(kc + 1) * 512],
                    start=(kc == 0), stop=(kc == 7)),
                    reads=[b_fwg, b_hE], writes=[b_ps1], sig=(kc == 7))
            em.op("scalar", lambda e, fc=fc: e.activation(tsc[:, :], ps1[:, :], AF.Gelu),
                  reads=[b_ps1], writes=[b_tsc])
            for kc in range(8):
                em.op("tensor", lambda e, fc=fc, kc=kc, fwu_sb=fwu_sb: e.matmul(
                    ps2[:, :], fwu_sb[:, kc * 512 + fc * 128: kc * 512 + (fc + 1) * 128],
                    hE[:, kc * 512:(kc + 1) * 512],
                    start=(kc == 0), stop=(kc == 7)),
                    reads=[b_fwu, b_hE], writes=[b_ps2], sig=(kc == 7))
            em.op("vector", lambda e, fc=fc: e.tensor_tensor(
                gu[:, fc * 512:(fc + 1) * 512], ps2[:, :], tsc[:, :], ALU.mult),
                reads=[b_ps2, b_tsc], writes=[b_gu])
        em.op("tensor", lambda e, co=co: e.matmul(
            ps5[:, :], sel_sb[:, 8 * 128:9 * 128], ceT[:, co:co + 512],
            start=True, stop=True),
            reads=[b_sel, b_ceT], writes=[b_ps5])
        for dd in range(8):
            for ic in range(4):
                em.op("tensor", lambda e, dd=dd, ic=ic, fwd_sb=fwd_sb: e.matmul(
                    ps1[:, :], fwd_sb[:, ic * 1024 + dd * 128: ic * 1024 + (dd + 1) * 128],
                    gu[:, ic * 512:(ic + 1) * 512],
                    start=(ic == 0), stop=(ic == 3)),
                    reads=[b_fwd, b_gu], writes=[b_ps1], sig=(ic == 3))
            em.op("vector", lambda e, dd=dd: e.tensor_tensor(
                tsc[:, :], ps1[:, :], xn[:, dd * 512:(dd + 1) * 512], ALU.add),
                reads=[b_ps1, b_xn], writes=[b_tsc])
            em.op("vector", lambda e, dd=dd: e.tensor_tensor(
                tsc2[:, :], tsc[:, :], y[:, dd * 512:(dd + 1) * 512], ALU.subtract),
                reads=[b_tsc, b_y], writes=[b_tsc2])
            em.op("vector", lambda e: e.tensor_tensor(tsc2[:, :], tsc2[:, :], ps5[:, :], ALU.mult),
                  reads=[b_tsc2, b_ps5], writes=[b_tsc2])
            em.op("vector", lambda e, dd=dd: e.tensor_tensor(
                y[:, dd * 512:(dd + 1) * 512], y[:, dd * 512:(dd + 1) * 512], tsc2[:, :], ALU.add),
                reads=[b_tsc2], writes=[b_y])
        emit_ln(em, C, S, ps1[0:1, :], ps2[0:1, :], ps3[:, :], ps4[:, :],
                b_ps1, b_ps2, b_ps3, b_ps4,
                src_ap=lambda c: y[:, c * 512:(c + 1) * 512], b_src=b_y,
                out_ap=None, b_out=None,
                out32_ap=lambda c: xt[:, c * 512:(c + 1) * 512], b_out32=b_xt,
                res_ap=lambda c: xt[:, c * 512:(c + 1) * 512], b_res=b_xt)
        em.op("sync", lambda e, b0=b0, t0=t0: e.dma_start(
            out=d_x4[b0].rearrange("(c p) t -> p c t", p=128)[:, :, t0:t0 + 512],
            in_=xt[:, :].rearrange("p (c t) -> p c t", c=8)),
            reads=[b_xt], writes=[bd_out], q="qs")

    ctx.close()
    if F is None:
        em.block()
        qs_final = em.cnt["qs"]
        em.op("sync", lambda e: e.nop(), extra=[("qs", qs_final)], sig=False)
        em.close()
        octx.close()
        return nc
    return None


def d1_inputs(inp):
    wr = np.asarray(inp["L1_ffn_Wr"], np.float32)
    wrc = np.ascontiguousarray(wr.reshape(8, 128, 8).transpose(1, 0, 2))
    return {"wrc": wrc}


def d2_inputs(inp):
    nw = np.asarray(inp["L1_ffn_norm_w"], np.float32)
    nwc = np.ascontiguousarray(nw.reshape(8, 8, 128).transpose(0, 2, 1))
    fn = np.asarray(inp["L1_ffn_f_norm"], np.float32)
    fnc = np.ascontiguousarray(fn.reshape(8, 128).T)
    return {"nwc": nwc,
            "wg": np.ascontiguousarray(inp["L1_ffn_Wg"], np.float32),
            "wu": np.ascontiguousarray(inp["L1_ffn_Wu"], np.float32),
            "wd": np.ascontiguousarray(inp["L1_ffn_Wd"], np.float32),
            "fnc": fnc,
            "fwg": np.ascontiguousarray(inp["L1_ffn_fWg"], np.float32),
            "fwu": np.ascontiguousarray(inp["L1_ffn_fWu"], np.float32),
            "fwd": np.ascontiguousarray(inp["L1_ffn_fWd"], np.float32)}


def run_d1(x3_fm_cores, winp):
    nc = build_d1()
    in_maps = []
    for c in range(NCORES):
        m = dict(winp)
        m["x3"] = x3_fm_cores[c]
        in_maps.append(m)
    res = run_bass_kernel_spmd(nc, in_maps, core_ids=list(range(NCORES)))
    return [r["lg"] for r in res.results]


def run_d2(x3_fm_cores, lg_full, winp, debug=False):
    nc = build_d2(debug=debug)
    in_maps = []
    for c in range(NCORES):
        m = dict(winp)
        m["x3"] = x3_fm_cores[c]
        m["lgall"] = lg_full
        m["lgown"] = np.ascontiguousarray(lg_full[c * NTOK:(c + 1) * NTOK])
        gsel = np.zeros((1, lg_full.shape[0] // 128), np.float32)
        gsel[0, : c * (NTOK // 128)] = 1.0
        m["gsel"] = np.ascontiguousarray(np.broadcast_to(gsel, (128, gsel.shape[1])))
        in_maps.append(m)
    res = run_bass_kernel_spmd(nc, in_maps, core_ids=list(range(NCORES)))
    if debug:
        return res.results
    return [r["x4"] for r in res.results]



def _to_fm(x):
    out = []
    for c in range(NCORES):
        xc = x[c * BPC:(c + 1) * BPC]
        out.append(np.ascontiguousarray(xc.transpose(0, 2, 1), dtype=np.float32))
    return out


def _from_fm(parts):
    full = np.concatenate([p.transpose(0, 2, 1) for p in parts], axis=0)
    return np.ascontiguousarray(full)


def build_fused():
    nc = bass.Bass("TRN2", target_bir_lowering=False)
    octx = ExitStack()
    em = Em(nc, octx)
    C = Consts(nc, em, octx)
    S = LnScratch(nc, octx)
    streams = {
        "x": nc.dram_tensor("x", [BPC, D, T], F32, kind="ExternalInput").ap(),
        "x1": nc.dram_tensor("x1s", [BPC, D, T], F32, kind="Internal").ap(),
        "x2": nc.dram_tensor("x2s", [BPC, D, T], F32, kind="Internal").ap(),
        "x3": nc.dram_tensor("x3s", [BPC, D, T], F32, kind="Internal").ap(),
        "lg": nc.dram_tensor("lgs", [NTOK, 8], F32, kind="Internal").ap(),
        "x4": nc.dram_tensor("x4", [BPC, D, T], F32, kind="ExternalOutput").ap(),
    }
    dbufs = {k: Buf("d_" + k) for k in streams}
    F = (nc, em, C, S, streams, dbufs)

    def drain():
        for eng in ("sync", "gpsimd", "vector", "scalar", "tensor"):
            em.op(eng, lambda e: e.nop(),
                  extra=[("qs", em.cnt["qs"]), ("qg", em.cnt["qg"])], sig=False)

    build_p1(F=F)
    drain()
    build_p2(F=F)
    drain()
    build_p3(F=F)
    drain()
    build_d1(F=F)
    drain()
    build_d2(F=F)
    em.block()
    qs_final = em.cnt["qs"]
    em.op("sync", lambda e: e.nop(), extra=[("qs", qs_final)], sig=False)
    em.close()
    octx.close()
    return nc


def fused_inputs(inp):
    m = {}
    m["wq"] = np.ascontiguousarray(np.asarray(inp["L0_attn_Wq"], np.float32))
    m["wk"] = np.ascontiguousarray(np.asarray(inp["L0_attn_Wk"], np.float32))
    m["wv"] = np.ascontiguousarray(np.asarray(inp["L0_attn_Wv"], np.float32))
    m["wo"] = np.ascontiguousarray(np.asarray(inp["L0_attn_Wo"], np.float32))
    m.update(p2_inputs(inp))
    m.update(p3_inputs(inp))
    m.update(d1_inputs(inp))
    m.update(d2_inputs(inp))
    return m


def kernel(**inputs):
    inp = inputs
    x = np.ascontiguousarray(np.asarray(inp["L0_attn_x"], dtype=np.float32))
    x_fm = _to_fm(x)
    wm = fused_inputs(inp)
    nc = build_fused()
    in_maps = []
    for c in range(NCORES):
        m = dict(wm)
        m["x"] = x_fm[c]
        gsel = np.zeros((128, 32), np.float32)
        glob = (np.arange(32)[None, :] // 8) * 128 + np.arange(128)[:, None]
        gsel[:, :] = (glob < 64 * c)
        m["gselc"] = np.ascontiguousarray(gsel)
        in_maps.append(m)
    res = run_bass_kernel_spmd(nc, in_maps, core_ids=list(range(NCORES)))
    return _from_fm([r["x4"] for r in res.results])


def kernel_unfused(**inputs):
    inp = inputs
    x = np.ascontiguousarray(np.asarray(inp["L0_attn_x"], dtype=np.float32))
    x_fm = _to_fm(x)

    w1 = {"wq": np.ascontiguousarray(np.asarray(inp["L0_attn_Wq"], np.float32)),
          "wk": np.ascontiguousarray(np.asarray(inp["L0_attn_Wk"], np.float32)),
          "wv": np.ascontiguousarray(np.asarray(inp["L0_attn_Wv"], np.float32)),
          "wo": np.ascontiguousarray(np.asarray(inp["L0_attn_Wo"], np.float32))}
    x1_fm = run_p1(x_fm, w1)

    x2_fm = run_p2(x1_fm, p2_inputs(inp))
    x3_fm = run_p3(x2_fm, p3_inputs(inp))
    lg_parts = run_d1(x3_fm, d1_inputs(inp))
    lg_full = np.ascontiguousarray(np.concatenate(lg_parts, axis=0))
    x4_fm = run_d2(x3_fm, lg_full, d2_inputs(inp))
    return _from_fm(x4_fm)


# version 06 (final)
# speedup vs baseline: 1.0480x; 1.0480x over previous
import numpy as np
import ml_dtypes
from contextlib import ExitStack

import concourse.bass as bass
import concourse.mybir as mybir
from concourse.bass_utils import run_bass_kernel_spmd

F32 = mybir.dt.float32
BF16 = mybir.dt.bfloat16
AF = mybir.ActivationFunctionType
ALU = mybir.AluOpType
AX = mybir.AxisListType
BF = ml_dtypes.bfloat16

NCORES = 8
BPC = 4
T = 2048
D = 1024
NTOK = BPC * T
NH = 16
HD = 64
LN_EPS = 1e-5

NEG_BIG = -1e30



class Buf:
    __slots__ = ("name", "w", "r")

    def __init__(self, name=""):
        self.name = name
        self.w = None
        self.r = []


class Em:

    def __init__(self, nc, ctx):
        self.nc = nc
        self.ctx = ctx
        self.blk = None
        self.sems = {}
        self.cnt = {}
        self.wm = {}
        self.pend = {}
        self.bodies = {}
        for name in ("tensor", "vector", "scalar", "gpsimd", "sync", "qs", "qg"):
            self.sems[name] = ctx.enter_context(nc.semaphore(name))
            self.cnt[name] = 0

    def _flush(self):
        if self.blk is None:
            return
        for eng, fns in self.bodies.items():
            if not fns:
                continue

            def body(engine, fns=tuple(fns)):
                for f in fns:
                    f(engine)
            getattr(self.blk, eng)(body)
        self.bodies = {}

    def block(self):
        self._flush()
        if self.blk is not None:
            self.blk.__exit__(None, None, None)
        self.blk = self.nc.Block().__enter__()
        return self.blk

    def close(self):
        self._flush()
        if self.blk is not None:
            self.blk.__exit__(None, None, None)
            self.blk = None

    def op(self, eng, fn, reads=(), writes=(), sig=True, q=None, extra=()):
        waits = {}

        def add(tok):
            if tok is None:
                return
            s, v = tok
            if eng == "tensor" and s == "tensor":
                return
            if waits.get(s, 0) < v:
                waits[s] = v

        for b in reads:
            add(b.w)
        for b in writes:
            add(b.w)
            for t in b.r:
                add(t)
        for t in extra:
            add(t)
        wm = self.wm.setdefault(eng, {})
        wlist = sorted((s, v) for s, v in waits.items() if wm.get(s, -1) < v)

        signame = q if q is not None else eng
        amt = 16 if q is not None else 1
        sem = self.sems[signame]
        if sig:
            self.cnt[signame] += amt
        cval = self.cnt[signame]
        sems = self.sems

        def body(engine):
            for s, v in wlist:
                engine.wait_ge(sems[s], v)
            inst = fn(engine)
            if sig:
                inst.then_inc(sem, amt)

        self.bodies.setdefault(eng, []).append(body)
        for s, v in wlist:
            wm[s] = max(wm.get(s, -1), v)

        p_r, p_w = self.pend.pop(eng, ((), ()))
        all_r = list(p_r) + list(reads)
        all_w = list(p_w) + list(writes)
        if sig:
            tok = (signame, cval)
            for b in all_w:
                b.w = tok
                b.r = []
            for b in all_r:
                b.r.append(tok)
            return tok
        else:
            self.pend[eng] = (tuple(all_r), tuple(all_w))
            return None

    def barrier_token(self, eng="vector"):
        return None



class Roller:

    def __init__(self, nc, ctx, name, shape, dtype, n):
        self.bufs = [ctx.enter_context(nc.sbuf_tensor(f"{name}{i}", shape, dtype))
                     for i in range(n)]
        self.tok = [Buf(f"{name}{i}") for i in range(n)]
        self.i = 0

    def next(self):
        j = self.i % len(self.bufs)
        self.i += 1
        return self.bufs[j], self.tok[j]


class Consts:

    def __init__(self, nc, em, ctx, need_masks=False, need_ident=False):
        self.em = em
        one128 = np.ones((128, 1), dtype=BF)
        one1x128 = np.ones((1, 128), dtype=BF)
        one1x512 = np.ones((1, 512), dtype=BF)
        self.d_ones128 = nc.inline_tensor(one128, "c_ones128").ap()
        self.d_ones1x128 = nc.inline_tensor(one1x128, "c_ones1x128").ap()
        self.d_ones1x512 = nc.inline_tensor(one1x512, "c_ones1x512").ap()
        self.ones128 = ctx.enter_context(nc.sbuf_tensor("ones128", [128, 1], BF16))
        self.ones1x128 = ctx.enter_context(nc.sbuf_tensor("ones1x128", [1, 128], BF16))
        self.ones1x512 = ctx.enter_context(nc.sbuf_tensor("ones1x512", [1, 512], BF16))
        self.b_ones128 = Buf("ones128")
        self.b_ones1x128 = Buf("ones1x128")
        self.b_ones1x512 = Buf("ones1x512")
        self.loaded = False

    def load(self, em):
        if self.loaded:
            return
        self.loaded = True
        em.op("sync", lambda e: e.dma_start(out=self.ones128[:, :], in_=self.d_ones128[:, :]),
              writes=[self.b_ones128], q="qs")
        em.op("sync", lambda e: e.dma_start(out=self.ones1x128[:, :], in_=self.d_ones1x128[:, :]),
              writes=[self.b_ones1x128], q="qs")
        em.op("sync", lambda e: e.dma_start(out=self.ones1x512[:, :], in_=self.d_ones1x512[:, :]),
              writes=[self.b_ones1x512], q="qs")


class LnScratch:

    def __init__(self, nc, ctx, W=512):
        self.W = W
        self.sq = ctx.enter_context(nc.sbuf_tensor("ln_sq", [128, 8 * W], BF16))
        self.ub = ctx.enter_context(nc.sbuf_tensor("ln_ub", [128, 8 * W], BF16))
        self.t1 = ctx.enter_context(nc.sbuf_tensor("ln_t1", [128, W], F32))
        self.stat = ctx.enter_context(nc.sbuf_tensor("ln_stat", [1, 5 * W], F32))
        self.statb = ctx.enter_context(nc.sbuf_tensor("ln_statb", [1, 2 * W], BF16))
        self.sbR = ctx.enter_context(nc.sbuf_tensor("ln_sbR", [128, W], F32))
        self.b_sq = Buf("ln_sq"); self.b_ub = Buf("ln_ub"); self.b_t1 = Buf("ln_t1")
        self.b_stat = Buf("ln_stat"); self.b_statb = Buf("ln_statb"); self.b_sbR = Buf("ln_sbR")


def emit_ln(em, C, S, psA, psB, psM, psR, b_psA, b_psB, b_psM, b_psR,
            src_ap, b_src, out_ap=None, b_out=None,
            res_ap=None, b_res=None, out32_ap=None, b_out32=None,
            lo_ap=None, b_lo=None,
            nchunks=8, W=512, Dred=1024, eps=LN_EPS):
    def srcb(c):
        return b_src[c] if isinstance(b_src, (list, tuple)) else b_src

    W2 = W
    for c in range(nchunks):
        em.op("scalar", lambda e, c=c: e.activation(S.sq[:, c * W2:(c + 1) * W2], src_ap(c), AF.Square),
              reads=[srcb(c)], writes=[S.b_sq])
        em.op("vector", lambda e, c=c: e.tensor_copy(S.ub[:, c * W2:(c + 1) * W2], src_ap(c)),
              reads=[srcb(c)], writes=[S.b_ub])
    for c in range(nchunks):
        em.op("tensor", lambda e, c=c: e.matmul(psA, C.ones128[:, :], S.ub[:, c * W2:(c + 1) * W2],
                                                start=(c == 0), stop=(c == nchunks - 1)),
              reads=[S.b_ub, C.b_ones128], writes=[b_psA], sig=(c == nchunks - 1))
    for c in range(nchunks):
        em.op("tensor", lambda e, c=c: e.matmul(psB, C.ones128[:, :], S.sq[:, c * W2:(c + 1) * W2],
                                                start=(c == 0), stop=(c == nchunks - 1)),
              reads=[S.b_sq, C.b_ones128], writes=[b_psB], sig=(c == nchunks - 1))
    st = S.stat
    mean = st[:, 0:W]; msq = st[:, W:2 * W]; var = st[:, 2 * W:3 * W]; sd = st[:, 3 * W:4 * W]
    r = st[:, 4 * W:5 * W]
    em.op("vector", lambda e: e.tensor_scalar_mul(mean, psA, 1.0 / Dred),
          reads=[b_psA], writes=[S.b_stat])
    em.op("vector", lambda e: e.tensor_scalar_mul(msq, psB, 1.0 / Dred),
          reads=[b_psB], writes=[S.b_stat])
    em.op("vector", lambda e: e.tensor_tensor(var, mean, mean, ALU.mult),
          reads=[S.b_stat], writes=[S.b_stat])
    em.op("vector", lambda e: e.tensor_tensor(var, msq, var, ALU.subtract),
          reads=[S.b_stat], writes=[S.b_stat])
    em.op("vector", lambda e: e.tensor_scalar_add(var, var, float(eps)),
          reads=[S.b_stat], writes=[S.b_stat])
    em.op("scalar", lambda e: e.activation(sd, var, AF.Sqrt),
          reads=[S.b_stat], writes=[S.b_stat])
    em.op("vector", lambda e: e.reciprocal(r, sd),
          reads=[S.b_stat], writes=[S.b_stat])
    mb = S.statb[:, 0:W]; rb = S.statb[:, W:2 * W]
    em.op("vector", lambda e: e.tensor_copy(mb, mean), reads=[S.b_stat], writes=[S.b_statb])
    em.op("vector", lambda e: e.tensor_copy(rb, r), reads=[S.b_stat], writes=[S.b_statb])
    em.op("tensor", lambda e: e.matmul(psM, C.ones1x128[:, :], mb, start=True, stop=True),
          reads=[S.b_statb, C.b_ones1x128], writes=[b_psM])
    em.op("tensor", lambda e: e.matmul(psR, C.ones1x128[:, :], rb, start=True, stop=True),
          reads=[S.b_statb, C.b_ones1x128], writes=[b_psR])
    em.op("vector", lambda e: e.tensor_copy(S.sbR[:, :], psR), reads=[b_psR], writes=[S.b_sbR])
    for c in range(nchunks):
        em.op("vector", lambda e, c=c: e.tensor_tensor(S.t1[:, :], src_ap(c), psM, ALU.subtract),
              reads=[srcb(c), b_psM], writes=[S.b_t1])
        if out32_ap is not None:
            em.op("vector", lambda e, c=c: e.tensor_tensor(S.t1[:, :], S.t1[:, :], S.sbR[:, :], ALU.mult),
                  reads=[S.b_t1, S.b_sbR], writes=[S.b_t1])
            if res_ap is not None:
                em.op("vector", lambda e, c=c: e.tensor_tensor(out32_ap(c), S.t1[:, :], res_ap(c), ALU.add),
                      reads=[S.b_t1, b_res[c] if isinstance(b_res, (list, tuple)) else b_res],
                      writes=[b_out32[c] if isinstance(b_out32, (list, tuple)) else b_out32])
            else:
                em.op("vector", lambda e, c=c: e.tensor_copy(out32_ap(c), S.t1[:, :]),
                      reads=[S.b_t1],
                      writes=[b_out32[c] if isinstance(b_out32, (list, tuple)) else b_out32])
            if out_ap is not None:
                em.op("vector", lambda e, c=c: e.tensor_copy(out_ap(c), S.t1[:, :]),
                      reads=[S.b_t1],
                      writes=[b_out[c] if isinstance(b_out, (list, tuple)) else b_out])
        elif lo_ap is not None:
            em.op("vector", lambda e, c=c: e.tensor_tensor(S.t1[:, :], S.t1[:, :], S.sbR[:, :], ALU.mult),
                  reads=[S.b_t1, S.b_sbR], writes=[S.b_t1])
            em.op("vector", lambda e, c=c: e.tensor_copy(out_ap(c), S.t1[:, :]),
                  reads=[S.b_t1], writes=[b_out])
            em.op("vector", lambda e, c=c: e.tensor_tensor(lo_ap(c), S.t1[:, :], out_ap(c), ALU.subtract),
                  reads=[S.b_t1, b_out], writes=[b_lo])
        else:
            em.op("vector", lambda e, c=c: e.tensor_tensor(out_ap(c), S.t1[:, :], S.sbR[:, :], ALU.mult),
                  reads=[S.b_t1, S.b_sbR],
                  writes=[b_out[c] if isinstance(b_out, (list, tuple)) else b_out])



def build_p1(debug=False, F=None):
    if F is None:
        nc = bass.Bass("TRN2", target_bir_lowering=False)
        octx = ExitStack()
        em = Em(nc, octx)
        C = Consts(nc, em, octx)
        S = LnScratch(nc, octx)
        d_x = nc.dram_tensor("x", [BPC, D, T], F32, kind="ExternalInput").ap()
        d_x1 = nc.dram_tensor("x1", [BPC, D, T], F32, kind="ExternalOutput").ap()
        bd_in = Buf("bd_in"); bd_out = Buf("bd_out"); bd_lg = Buf("bd_lg")
    else:
        nc, em, C, S, streams, dbufs = F
        d_x = streams["x"]; d_x1 = streams["x1"]
        bd_in, bd_out = dbufs["x"], dbufs["x1"]
    ctx = ExitStack()
    d_wq = nc.dram_tensor("wq", [D, D], F32, kind="ExternalInput").ap()
    d_wk = nc.dram_tensor("wk", [D, D], F32, kind="ExternalInput").ap()
    d_wv = nc.dram_tensor("wv", [D, D], F32, kind="ExternalInput").ap()
    d_wo = nc.dram_tensor("wo", [D, D], F32, kind="ExternalInput").ap()

    masks_np = np.zeros((4, 128, 512), dtype=np.float32)
    for v in range(4):
        p = np.arange(128)[:, None]
        c = np.arange(512)[None, :]
        masks_np[v] = np.where(p <= c - 128 * v, 0.0, NEG_BIG)
    d_masks = nc.inline_tensor(masks_np, "c_masks").ap()

    def slopes(n):
        import math
        start = 2 ** (-2 ** (-(math.log2(n) - 3)))
        return [start * start ** i for i in range(n)]
    sl = np.array(slopes(NH), dtype=np.float64)
    ali = (sl[:, None] * (np.arange(T, dtype=np.float64) - (T - 1))[None, :])
    ali_hi = ali.astype(BF)
    ali_lo = (ali - ali_hi.astype(np.float64)).astype(BF)
    ali_hilo = np.stack([ali_hi, ali_lo], axis=1)
    d_ali = nc.inline_tensor(ali_hilo, "c_alibi").ap()

    kind = "ExternalOutput" if debug else "Internal"
    d_qf = nc.dram_tensor("qf", [BPC, D, T], BF16, kind=kind).ap()
    d_kf = nc.dram_tensor("kf", [BPC, D, T], BF16, kind=kind).ap()
    d_vt = nc.dram_tensor("vt", [BPC, T, D], BF16, kind=kind).ap()
    if debug:
        d_af = nc.dram_tensor("af", [128, 8 * T], BF16, kind="ExternalOutput").ap()
        d_bdbg = nc.dram_tensor("bdbg", [1, T], BF16, kind="ExternalOutput").ap()
        d_sdbg = nc.dram_tensor("sdbg", [1, T], F32, kind="ExternalOutput").ap()
        d_ptdbg = nc.dram_tensor("ptdbg", [128, T], BF16, kind="ExternalOutput").ap()

    xt = ctx.enter_context(nc.sbuf_tensor("xt", [128, 4096], F32))
    xn = ctx.enter_context(nc.sbuf_tensor("xn", [128, 4096], BF16))
    b_xt = Buf("xt"); b_xn = Buf("xn")

    ctxA = ExitStack()
    wq_sb = ctxA.enter_context(nc.sbuf_tensor("wq_sb", [128, 8192], BF16))
    wk_sb = ctxA.enter_context(nc.sbuf_tensor("wk_sb", [128, 8192], BF16))
    wv_sb = ctxA.enter_context(nc.sbuf_tensor("wv_sb", [128, 8192], BF16))
    b_wq = Buf("wq"); b_wk = Buf("wk"); b_wv = Buf("wv")
    q_sb = ctxA.enter_context(nc.sbuf_tensor("q_sb", [128, 4096], BF16))
    k_sb = ctxA.enter_context(nc.sbuf_tensor("k_sb", [128, 4096], BF16))
    v_sb = ctxA.enter_context(nc.sbuf_tensor("v_sb", [128, 1024], BF16))
    b_q = Buf("q_sb"); b_k = Buf("k_sb"); b_v = Buf("v_sb")
    psq = ctxA.enter_context(nc.psum_tensor("psq", [128, 512], F32))
    psk = ctxA.enter_context(nc.psum_tensor("psk", [128, 512], F32))
    psv = ctxA.enter_context(nc.psum_tensor("psv", [128, 512], F32))
    psM = ctxA.enter_context(nc.psum_tensor("psM", [128, 512], F32))
    psR = ctxA.enter_context(nc.psum_tensor("psR", [128, 512], F32))
    b_psq = Buf("psq"); b_psk = Buf("psk"); b_psv = Buf("psv")
    b_psM = Buf("psM"); b_psR = Buf("psR")
    psA = psq[0:1, :]; psB = psk[0:1, :]

    em.block()
    C.load(em)
    for w_sb, d_w, b_w in ((wq_sb, d_wq, b_wq), (wk_sb, d_wk, b_wk), (wv_sb, d_wv, b_wv)):
        em.op("gpsimd", lambda e, w_sb=w_sb, d_w=d_w: e.dma_start(
            out=w_sb[:, :], in_=d_w.rearrange("(kc p) n -> p kc n", p=128)),
            writes=[b_w], q="qg")

    for b in range(BPC):
        for ti in range(4):
            t0 = ti * 512
            em.op("sync", lambda e, b=b, t0=t0: e.dma_start(
                out=xt[:, :], in_=d_x[b].rearrange("(c p) t -> p c t", p=128)[:, :, t0:t0 + 512]),
                reads=[bd_in], writes=[b_xt], q="qs")
            emit_ln(em, C, S, psA, psB, psM[:, :], psR[:, :], b_psq, b_psk, b_psM, b_psR,
                    src_ap=lambda c: xt[:, c * 512:(c + 1) * 512], b_src=b_xt,
                    out_ap=lambda c: xn[:, c * 512:(c + 1) * 512], b_out=b_xn)
            for (w_sb, b_w, o_sb, b_o, ps, b_ps, scale) in (
                    (wq_sb, b_wq, q_sb, b_q, psq, b_psq, 0.125),
                    (wk_sb, b_wk, k_sb, b_k, psk, b_psk, 1.0)):
                for dd in range(8):
                    for kc in range(8):
                        em.op("tensor", lambda e, w_sb=w_sb, dd=dd, kc=kc, ps=ps: e.matmul(
                            ps[:, :], w_sb[:, kc * 1024 + dd * 128: kc * 1024 + (dd + 1) * 128],
                            xn[:, kc * 512:(kc + 1) * 512],
                            start=(kc == 0), stop=(kc == 7)),
                            reads=[b_w, b_xn], writes=[b_ps], sig=(kc == 7))
                    if scale == 1.0:
                        em.op("scalar", lambda e, dd=dd, o_sb=o_sb, ps=ps: e.activation(
                            o_sb[:, dd * 512:(dd + 1) * 512], ps[:, :], AF.Copy),
                            reads=[b_ps], writes=[b_o])
                    else:
                        em.op("vector", lambda e, dd=dd, o_sb=o_sb, ps=ps, scale=scale: e.tensor_scalar_mul(
                            o_sb[:, dd * 512:(dd + 1) * 512], ps[:, :], float(scale)),
                            reads=[b_ps], writes=[b_o])
                em.op("sync", lambda e, b=b, t0=t0, o_sb=o_sb, dgt=(d_qf if o_sb is q_sb else d_kf): e.dma_start(
                    out=dgt[b].rearrange("(c p) t -> p c t", p=128)[:, :, t0:t0 + 512],
                    in_=o_sb[:, :].rearrange("p (c t) -> p c t", c=8)),
                    reads=[b_o], q="qs")
            for ts in range(4):
                for dv in range(2):
                    for kc in range(8):
                        em.op("tensor", lambda e, ts=ts, dv=dv, kc=kc: e.matmul(
                            psv[:, :], xn[:, kc * 512 + ts * 128: kc * 512 + (ts + 1) * 128],
                            wv_sb[:, kc * 1024 + dv * 512: kc * 1024 + (dv + 1) * 512],
                            start=(kc == 0), stop=(kc == 7)),
                            reads=[b_wv, b_xn], writes=[b_psv], sig=(kc == 7))
                    em.op("scalar", lambda e, dv=dv: e.activation(
                        v_sb[:, dv * 512:(dv + 1) * 512], psv[:, :], AF.Copy),
                        reads=[b_psv], writes=[b_v])
                em.op("sync", lambda e, b=b, t0=t0, ts=ts: e.dma_start(
                    out=d_vt[b][t0 + ts * 128: t0 + (ts + 1) * 128, :], in_=v_sb[:, :]),
                    reads=[b_v], q="qs")

    ctxA.close()

    wo_sb = ctx.enter_context(nc.sbuf_tensor("wo_sb", [128, 8192], BF16))
    b_wo = Buf("wo")
    q_aug = ctx.enter_context(nc.sbuf_tensor("q_aug", [66, T], BF16))
    k_aug = ctx.enter_context(nc.sbuf_tensor("k_aug", [66, T], BF16))
    b_qa = Buf("q_aug"); b_ka = Buf("k_aug")
    v_h = ctx.enter_context(nc.sbuf_tensor("v_h", [128, 16 * 65], BF16))
    b_vh = Buf("v_h")
    qsq = ctx.enter_context(nc.sbuf_tensor("qsq", [64, T], BF16))
    b_qsq = Buf("qsq")
    alirow = ctx.enter_context(nc.sbuf_tensor("alirow", [1, T], BF16))
    b_ali = Buf("alirow")
    pt_sb = ctx.enter_context(nc.sbuf_tensor("pt_sb", [128, T], BF16))
    b_pt = Buf("pt_sb")
    attn_f = ctx.enter_context(nc.sbuf_tensor("attn_f", [128, 8 * T], BF16))
    b_attn = Buf("attn_f")
    mask_sb = ctx.enter_context(nc.sbuf_tensor("mask_sb", [128, 2048], F32))
    b_mask = Buf("mask")
    brow = ctx.enter_context(nc.sbuf_tensor("brow", [1, T], F32))
    brow_bf = ctx.enter_context(nc.sbuf_tensor("brow_bf", [1, T], BF16))
    kmx = ctx.enter_context(nc.sbuf_tensor("kmx", [1, 8], F32))
    b_brow = Buf("brow"); b_browbf = Buf("brow_bf"); b_kmx = Buf("kmx")
    u_sb = ctx.enter_context(nc.sbuf_tensor("u_sb", [128, 4096], F32))
    b_u = Buf("u_sb")
    x1t = ctx.enter_context(nc.sbuf_tensor("x1t", [128, 4096], F32))
    b_x1 = Buf("x1t")

    rsum = ctx.enter_context(nc.sbuf_tensor("rsum", [1, T], F32))
    rsum_bf = ctx.enter_context(nc.sbuf_tensor("rsum_bf", [1, T], BF16))
    bc_sb = ctx.enter_context(nc.sbuf_tensor("bc_sb", [64, T], F32))
    b_rsum = Buf("rsum"); b_rsumbf = Buf("rsum_bf"); b_bc = Buf("bc_sb")

    ctxB = ExitStack()
    psS = ctxB.enter_context(nc.psum_tensor("psS", [128, T], F32))
    psO = ctxB.enter_context(nc.psum_tensor("psO", [65, T], F32))
    b_psS = Buf("psS"); b_psO = Buf("psO")
    ps_row = psS[0:1, :]

    em.block()
    em.op("gpsimd", lambda e: e.dma_start(
        out=wo_sb[:, :], in_=d_wo.rearrange("(kc p) n -> p kc n", p=128)),
        writes=[b_wo], q="qg")
    em.op("sync", lambda e: e.dma_start(
        out=mask_sb[:, :], in_=d_masks.rearrange("v p t -> p v t")),
        writes=[b_mask], q="qs")

    for b in range(BPC):
        for h in range(NH):
            em.op("sync", lambda e, b=b, h=h: e.dma_start(
                out=q_aug[0:64, :], in_=d_qf[b][64 * h:64 * (h + 1), :]),
                writes=[b_qa], q="qs")
            em.op("sync", lambda e, b=b, h=h: e.dma_start(
                out=k_aug[0:64, :], in_=d_kf[b][64 * h:64 * (h + 1), :]),
                writes=[b_ka], q="qs")
            em.op("sync", lambda e, h=h: e.dma_start(
                out=k_aug[64:66, :], in_=d_ali[h]),
                writes=[b_ka], q="qs")
            em.op("vector", lambda e: e.memset(q_aug[64:66, :], 1.0), writes=[b_qa])
            em.op("vector", lambda e: e.memset(v_h[:, :], 1.0), writes=[b_vh])
            em.op("sync", lambda e, b=b, h=h: e.dma_start(
                out=v_h[:, :].rearrange("p (c d) -> p c d", d=65)[:, :, 0:64],
                in_=d_vt[b].rearrange("(c p) d -> p c d", p=128)[:, :, 64 * h:64 * (h + 1)]),
                writes=[b_vh], q="qs")
            em.op("scalar", lambda e: e.activation(qsq[:, :], k_aug[0:64, :], AF.Square),
                  reads=[b_ka], writes=[b_qsq])
            for qc in range(4):
                em.op("tensor", lambda e, qc=qc: e.matmul(
                    ps_row[:, qc * 512:(qc + 1) * 512], C.ones128[0:64, :],
                    qsq[:, qc * 512:(qc + 1) * 512], start=True, stop=True),
                    reads=[b_qsq, C.b_ones128], writes=[b_psS], sig=(qc == 3))
            em.op("vector", lambda e: e.reduce_max(kmx[:, 0:1], ps_row, axis=AX.X),
                  reads=[b_psS], writes=[b_kmx])
            em.op("scalar", lambda e: e.activation(kmx[:, 1:2], kmx[:, 0:1], AF.Sqrt),
                  reads=[b_kmx], writes=[b_kmx])
            em.op("scalar", lambda e: e.activation(qsq[:, :], q_aug[0:64, :], AF.Square),
                  reads=[b_qa], writes=[b_qsq])
            for qc in range(4):
                em.op("tensor", lambda e, qc=qc: e.matmul(
                    ps_row[:, qc * 512:(qc + 1) * 512], C.ones128[0:64, :],
                    qsq[:, qc * 512:(qc + 1) * 512], start=True, stop=True),
                    reads=[b_qsq, C.b_ones128], writes=[b_psS], sig=(qc == 3))
            em.op("scalar", lambda e: e.activation(brow[:, :], ps_row, AF.Sqrt),
                  reads=[b_psS], writes=[b_brow])
            em.op("sync", lambda e, h=h: e.dma_start(out=alirow[:, :], in_=d_ali[h, 0:1, :]),
                  writes=[b_ali], q="qs")
            em.op("vector", lambda e: e.tensor_scalar(
                brow[:, :], brow[:, :], kmx[:, 1:2], -1.02, ALU.mult, ALU.mult),
                reads=[b_brow, b_kmx], writes=[b_brow])
            em.op("vector", lambda e: e.tensor_tensor(
                brow_bf[:, :], brow[:, :], alirow[:, :], ALU.subtract),
                reads=[b_brow, b_ali], writes=[b_browbf])

            for kc in range(16):
                qc0 = kc // 4
                for qc in range(qc0, 4):
                    em.op("tensor", lambda e, kc=kc, qc=qc: e.matmul(
                        psS[:, qc * 512:(qc + 1) * 512],
                        k_aug[:, kc * 128:(kc + 1) * 128],
                        q_aug[:, qc * 512:(qc + 1) * 512],
                        start=True, stop=False),
                        reads=[b_ka, b_qa], writes=[b_psS], sig=False)
                    em.op("tensor", lambda e, kc=kc, qc=qc: e.matmul(
                        psS[:, qc * 512:(qc + 1) * 512],
                        C.ones1x128[:, :],
                        brow_bf[:, qc * 512:(qc + 1) * 512],
                        start=False, stop=True),
                        reads=[C.b_ones1x128, b_browbf], writes=[b_psS],
                        sig=(qc == 3))
                v_id = kc % 4
                em.op("vector", lambda e, kc=kc, qc0=qc0, v_id=v_id: e.tensor_tensor(
                    psS[:, qc0 * 512:(qc0 + 1) * 512], psS[:, qc0 * 512:(qc0 + 1) * 512],
                    mask_sb[:, v_id * 512:(v_id + 1) * 512], ALU.add),
                    reads=[b_psS, b_mask], writes=[b_psS])
                em.op("scalar", lambda e, kc=kc, qc0=qc0: e.activation(
                    pt_sb[:, qc0 * 512:T], psS[:, qc0 * 512:T], AF.Exp),
                    reads=[b_psS], writes=[b_pt])
                for qc in range(qc0, 4):
                    em.op("tensor", lambda e, kc=kc, qc=qc: e.matmul(
                        psO[:, qc * 512:(qc + 1) * 512],
                        v_h[:, kc * 65:(kc + 1) * 65],
                        pt_sb[:, qc * 512:(qc + 1) * 512],
                        start=(kc == 0), stop=(kc == 4 * qc + 3)),
                        reads=[b_vh, b_pt], writes=[b_psO], sig=(kc == 4 * qc + 3))
            if debug and b == 0 and h == 15:
                em.op("vector", lambda e: e.tensor_copy(rsum[:, :], psO[64:65, :]),
                      reads=[b_psO], writes=[b_rsum])
                em.op("sync", lambda e: e.dma_start(out=d_sdbg[:, :], in_=rsum[:, :]),
                      reads=[b_rsum], q="qs")
                em.op("sync", lambda e: e.dma_start(out=d_ptdbg[:, :], in_=pt_sb[:, :]),
                      reads=[b_pt], q="qs")
            em.op("vector", lambda e: e.reciprocal(rsum[:, :], psO[64:65, :]),
                  reads=[b_psO], writes=[b_rsum])
            em.op("vector", lambda e: e.tensor_copy(rsum_bf[:, :], rsum[:, :]),
                  reads=[b_rsum], writes=[b_rsumbf])
            for qc in range(4):
                em.op("tensor", lambda e, qc=qc: e.matmul(
                    psS[0:64, qc * 512:(qc + 1) * 512], C.ones1x128[:, 0:64],
                    rsum_bf[:, qc * 512:(qc + 1) * 512], start=True, stop=True),
                    reads=[C.b_ones1x128, b_rsumbf], writes=[b_psS], sig=(qc == 3))
            em.op("vector", lambda e: e.tensor_copy(bc_sb[:, :], psS[0:64, :]),
                  reads=[b_psS], writes=[b_bc])
            ci, r0 = h // 2, 64 * (h % 2)
            for qc in range(4):
                em.op("vector", lambda e, qc=qc, ci=ci, r0=r0: e.tensor_tensor(
                    attn_f[r0:r0 + 64, ci * T + qc * 512: ci * T + (qc + 1) * 512],
                    psO[0:64, qc * 512:(qc + 1) * 512],
                    bc_sb[:, qc * 512:(qc + 1) * 512], ALU.mult),
                    reads=[b_psO, b_bc], writes=[b_attn])

        if debug and b == 0:
            em.op("sync", lambda e: e.dma_start(out=d_af[:, :], in_=attn_f[:, :]),
                  reads=[b_attn], q="qs")
            em.op("sync", lambda e: e.dma_start(out=d_bdbg[:, :], in_=brow_bf[:, :]),
                  reads=[b_browbf], q="qs")
        for ti in range(4):
            t0 = ti * 512
            for dd in range(8):
                for kc in range(8):
                    em.op("tensor", lambda e, dd=dd, kc=kc, t0=t0: e.matmul(
                        psS[:, 0:512],
                        wo_sb[:, kc * 1024 + dd * 128: kc * 1024 + (dd + 1) * 128],
                        attn_f[:, kc * T + t0: kc * T + t0 + 512],
                        start=(kc == 0), stop=(kc == 7)),
                        reads=[b_wo, b_attn], writes=[b_psS], sig=(kc == 7))
                em.op("vector", lambda e, dd=dd: e.tensor_copy(
                    u_sb[:, dd * 512:(dd + 1) * 512], psS[:, 0:512]),
                    reads=[b_psS], writes=[b_u])
            em.op("sync", lambda e, b=b, t0=t0: e.dma_start(
                out=xt[:, :], in_=d_x[b].rearrange("(c p) t -> p c t", p=128)[:, :, t0:t0 + 512]),
                reads=[bd_in], writes=[b_xt], q="qs")
            emit_ln(em, C, S,
                    psS[0:1, 1024:1536], psS[0:1, 1536:2048],
                    psS[:, 0:512], psS[:, 512:1024],
                    b_psS, b_psS, b_psS, b_psS,
                    src_ap=lambda c: u_sb[:, c * 512:(c + 1) * 512], b_src=b_u,
                    out_ap=None, b_out=None,
                    out32_ap=lambda c: x1t[:, c * 512:(c + 1) * 512], b_out32=b_x1,
                    res_ap=lambda c: xt[:, c * 512:(c + 1) * 512], b_res=b_xt)
            em.op("sync", lambda e, b=b, t0=t0: e.dma_start(
                out=d_x1[b].rearrange("(c p) t -> p c t", p=128)[:, :, t0:t0 + 512],
                in_=x1t[:, :].rearrange("p (c t) -> p c t", c=8)),
                reads=[b_x1], writes=[bd_out], q="qs")

    ctxB.close()
    ctx.close()
    if F is None:
        em.block()
        qs_final = em.cnt["qs"]
        em.op("sync", lambda e: e.nop(), extra=[("qs", qs_final)], sig=False)
        em.close()
        octx.close()
        return nc
    return None


def run_p1(x_fm_cores, weights, debug=False):
    nc = build_p1(debug=debug)
    in_maps = []
    for c in range(NCORES):
        in_maps.append({
            "x": x_fm_cores[c],
            "wq": weights["wq"], "wk": weights["wk"],
            "wv": weights["wv"], "wo": weights["wo"],
        })
    res = run_bass_kernel_spmd(nc, in_maps, core_ids=list(range(NCORES)))
    if debug:
        return res.results
    return [r["x1"] for r in res.results]



def build_p2(debug=False, F=None):
    if F is None:
        nc = bass.Bass("TRN2", target_bir_lowering=False)
        octx = ExitStack()
        em = Em(nc, octx)
        C = Consts(nc, em, octx)
        S = LnScratch(nc, octx)
        d_x1 = nc.dram_tensor("x1", [BPC, D, T], F32, kind="ExternalInput").ap()
        d_x2 = nc.dram_tensor("x2", [BPC, D, T], F32, kind="ExternalOutput").ap()
        bd_in = Buf("bd_in"); bd_out = Buf("bd_out"); bd_lg = Buf("bd_lg")
    else:
        nc, em, C, S, streams, dbufs = F
        d_x1 = streams["x1"]; d_x2 = streams["x2"]
        bd_in, bd_out = dbufs["x1"], dbufs["x2"]
    ctx = ExitStack()
    d_gw = nc.dram_tensor("gw", [128, 8, 8], F32, kind="ExternalInput").ap()
    d_w1 = nc.dram_tensor("w1t", [8, D, D], F32, kind="ExternalInput").ap()
    d_w2 = nc.dram_tensor("w2t", [8, D, D], F32, kind="ExternalInput").ap()
    d_w3 = nc.dram_tensor("w3t", [8, D, D], F32, kind="ExternalInput").ap()
    d_b1 = nc.dram_tensor("b1c", [8, 128, 8], F32, kind="ExternalInput").ap()
    d_b3 = nc.dram_tensor("b3c", [8, 128, 8], F32, kind="ExternalInput").ap()
    d_b2 = nc.dram_tensor("b2r", [8, D], F32, kind="ExternalInput").ap()
    d_sw1 = nc.dram_tensor("sw1t", [D, 2048], F32, kind="ExternalInput").ap()
    d_sw2 = nc.dram_tensor("sw2t", [D, 2048], F32, kind="ExternalInput").ap()
    d_sw3 = nc.dram_tensor("sw3t", [2048, D], F32, kind="ExternalInput").ap()

    iota_np = np.broadcast_to(np.arange(8, dtype=np.float32)[None, :], (128, 8)).copy()
    d_iota = nc.inline_tensor(iota_np, "c2_iota").ap()
    d_iotab = nc.inline_tensor(iota_np + 1e6, "c2_iotab").ap()
    d_idf = nc.inline_tensor(np.eye(128, dtype=np.float32), "c2_identf32").ap()
    d_idb = nc.inline_tensor(np.eye(128, dtype=BF), "c2_identbf").ap()
    sel_np = np.zeros((18, 18 * 128), dtype=BF)
    for r_ in range(18):
        sel_np[r_, r_ * 128:(r_ + 1) * 128] = 1
    d_sel = nc.inline_tensor(sel_np, "c2_sel").ap()

    if debug:
        d_rdbg = nc.dram_tensor("rdbg", [128, 64], F32, kind="ExternalOutput").ap()
        d_cdbg = nc.dram_tensor("cdbg", [18, 512], F32, kind="ExternalOutput").ap()
        d_ydbg = nc.dram_tensor("ydbg", [128, 8192], F32, kind="ExternalOutput").ap()

    NT = 512
    NST = NTOK // NT
    NQ5 = NT // 512

    xt = ctx.enter_context(nc.sbuf_tensor("p2_xt", [128, 4096], F32))
    xn = ctx.enter_context(nc.sbuf_tensor("p2_xn", [128, 8 * NT], BF16))
    xnlo = ctx.enter_context(nc.sbuf_tensor("p2_xnlo", [128, 4096], BF16))
    y = ctx.enter_context(nc.sbuf_tensor("p2_y", [128, 8 * NT], F32))
    X1 = ctx.enter_context(nc.sbuf_tensor("p2_X1", [128, 8 * NT], BF16))
    X3 = ctx.enter_context(nc.sbuf_tensor("p2_X3", [128, 8 * NT], BF16))
    hb = ctx.enter_context(nc.sbuf_tensor("p2_hb", [128, 8 * NT], BF16))
    rA = Roller(nc, ctx, "p2_wbufA", [128, 8192], BF16, 1)
    rB = Roller(nc, ctx, "p2_wbufB", [128, 8192], BF16, 1)
    rC = Roller(nc, ctx, "p2_wbufC", [128, 8192], BF16, 2)
    r13 = Roller(nc, ctx, "p2_b13", [128, 16], F32, 2)
    r2r = Roller(nc, ctx, "p2_b2r", [1, D], BF16, 2)
    gw_hi = ctx.enter_context(nc.sbuf_tensor("p2_gw_hi", [128, 64], BF16))
    gw_lo = ctx.enter_context(nc.sbuf_tensor("p2_gw_lo", [128, 64], BF16))
    gw_f = ctx.enter_context(nc.sbuf_tensor("p2_gw_f", [128, 64], F32))
    iota_sb = ctx.enter_context(nc.sbuf_tensor("p2_iota_sb", [128, 8], F32))
    iotab_sb = ctx.enter_context(nc.sbuf_tensor("p2_iotab_sb", [128, 8], F32))
    idf_sb = ctx.enter_context(nc.sbuf_tensor("p2_idf_sb", [128, 128], F32))
    idb_sb = ctx.enter_context(nc.sbuf_tensor("p2_idb_sb", [128, 128], BF16))
    sel_sb = ctx.enter_context(nc.sbuf_tensor("p2_sel_sb", [18, 18 * 128], BF16))
    lg_sb = ctx.enter_context(nc.sbuf_tensor("p2_lg_sb", [128, 8], F32))
    rt = ctx.enter_context(nc.sbuf_tensor("p2_rt", [128, 64], F32))
    stk = ctx.enter_context(nc.sbuf_tensor("p2_stk", [128, 18], BF16))
    coefT = ctx.enter_context(nc.sbuf_tensor("p2_coefT", [18, NT], BF16))
    wkb = ctx.enter_context(nc.sbuf_tensor("p2_wkb", [128, 2 * NT], F32))
    t3a = ctx.enter_context(nc.sbuf_tensor("p2_t3a", [128, 512], F32))
    c1t = ctx.enter_context(nc.sbuf_tensor("p2_c1t", [128, 512], F32))
    tsc = ctx.enter_context(nc.sbuf_tensor("p2_tsc", [128, 512], F32))

    b_xt = Buf("xt"); b_xn = Buf("xn"); b_xnlo = Buf("xnlo"); b_y = Buf("y")
    b_X1 = Buf("X1"); b_X3 = Buf("X3"); b_hb = Buf("hb")
    b_gwh = Buf("gw_hi"); b_gwl = Buf("gw_lo"); b_gwf = Buf("gw_f")
    b_iota = Buf("iota"); b_iotab = Buf("iotab"); b_idf = Buf("idf"); b_idb = Buf("idb")
    b_sel = Buf("sel"); b_lg = Buf("lg"); b_rt = Buf("rt"); b_stk = Buf("stk")
    b_coefT = Buf("coefT"); b_wkb = Buf("wkb")
    b_t3a = Buf("t3a"); b_c1 = Buf("c1t"); b_tsc = Buf("tsc")

    ps1 = ctx.enter_context(nc.psum_tensor("p2_ps1", [128, 512], F32))
    ps2 = ctx.enter_context(nc.psum_tensor("p2_ps2", [128, 512], F32))
    ps3 = ctx.enter_context(nc.psum_tensor("p2_ps3", [128, 512], F32))
    ps4 = ctx.enter_context(nc.psum_tensor("p2_ps4", [128, 512], F32))
    ps5 = ctx.enter_context(nc.psum_tensor("p2_ps5", [128, 512], F32))
    ps6 = ctx.enter_context(nc.psum_tensor("p2_ps6", [128, 1024], BF16))
    b_ps1 = Buf("ps1"); b_ps2 = Buf("ps2"); b_ps3 = Buf("ps3")
    b_ps4 = Buf("ps4"); b_ps5 = Buf("ps5"); b_ps6 = Buf("ps6")

    em.block()
    C.load(em)
    em.op("sync", lambda e: e.dma_start(out=gw_f[:, :], in_=d_gw.rearrange("p c e -> p (c e)")),
          writes=[b_gwf], q="qs")
    em.op("sync", lambda e: e.dma_start(out=iota_sb[:, :], in_=d_iota[:, :]), writes=[b_iota], q="qs")
    em.op("sync", lambda e: e.dma_start(out=iotab_sb[:, :], in_=d_iotab[:, :]), writes=[b_iotab], q="qs")
    em.op("sync", lambda e: e.dma_start(out=idf_sb[:, :], in_=d_idf[:, :]), writes=[b_idf], q="qs")
    em.op("sync", lambda e: e.dma_start(out=idb_sb[:, :], in_=d_idb[:, :]), writes=[b_idb], q="qs")
    em.op("sync", lambda e: e.dma_start(out=sel_sb[:, :], in_=d_sel[:, :]), writes=[b_sel], q="qs")
    em.op("vector", lambda e: e.tensor_copy(gw_hi[:, :], gw_f[:, :]), reads=[b_gwf], writes=[b_gwh])
    em.op("vector", lambda e: e.tensor_tensor(gw_lo[:, :], gw_f[:, :], gw_hi[:, :], ALU.subtract),
          reads=[b_gwf, b_gwh], writes=[b_gwl])

    def wsl(role, kc, j0, width):
        return role[:, kc * 1024 + j0: kc * 1024 + j0 + width]

    for st_i in range(NST):
        b0 = (st_i * NT) // T
        tb0 = (st_i * NT) % T

        em.op("vector", lambda e: e.memset(y[:, :], 0.0), writes=[b_y])

        for ti in range(NQ5):
            t0 = tb0 + ti * 512
            co = ti * 512
            em.op("sync", lambda e, b0=b0, t0=t0: e.dma_start(
                out=xt[:, :], in_=d_x1[b0].rearrange("(c p) t -> p c t", p=128)[:, :, t0:t0 + 512]),
                reads=[bd_in], writes=[b_xt], q="qs")

            def xn_out(c, co=co):
                return xn[:, c * NT + co: c * NT + co + 512]
            emit_ln(em, C, S, ps1[0:1, :], ps2[0:1, :], ps3[:, :], ps4[:, :],
                    b_ps1, b_ps2, b_ps3, b_ps4,
                    src_ap=lambda c: xt[:, c * 512:(c + 1) * 512], b_src=b_xt,
                    out_ap=xn_out, b_out=b_xn,
                    lo_ap=lambda c: xnlo[:, c * 512:(c + 1) * 512], b_lo=b_xnlo)
            groups = [(gw_hi, b_gwh, True), (gw_lo, b_gwl, True), (gw_hi, b_gwh, False)]
            for gi, (g_sb, b_g, x_hi) in enumerate(groups):
                for kc in range(8):
                    src = xn_out(kc) if x_hi else xnlo[:, kc * 512:(kc + 1) * 512]
                    em.op("tensor", lambda e, kc=kc, g_sb=g_sb, src=src, gi=gi: e.matmul(
                        ps3[0:8, :], g_sb[:, kc * 8:(kc + 1) * 8], src,
                        start=(gi == 0 and kc == 0), stop=(gi == 2 and kc == 7),
                        skip_group_check=True),
                        reads=[b_g, b_xn, b_xnlo], writes=[b_ps3],
                        sig=(gi == 2 and kc == 7))
            em.op("vector", lambda e: e.tensor_copy(S.t1[0:8, :], ps3[0:8, :]),
                  reads=[b_ps3], writes=[S.b_t1])
            for g in range(4):
                grp = co + g * 128
                em.op("tensor", lambda e, g=g: e.transpose(
                    ps4[0:128, 0:8], S.t1[0:8, g * 128:(g + 1) * 128], idf_sb[0:8, 0:8]),
                    reads=[S.b_t1, b_idf], writes=[b_ps4])
                em.op("vector", lambda e: e.tensor_copy(lg_sb[:, :], ps4[0:128, 0:8]),
                      reads=[b_ps4], writes=[b_lg])
                r = rt
                nmx = r[:, 0:1]; ssum = r[:, 1:2]; rs = r[:, 2:3]
                m1 = r[:, 3:4]; idx1 = r[:, 4:5]; m2 = r[:, 5:6]; idx2 = r[:, 6:7]
                probs = r[:, 8:16]; eq = r[:, 16:24]; u1 = r[:, 24:32]
                oh1 = r[:, 32:40]; oh2 = r[:, 40:48]; p2 = r[:, 48:56]
                em.op("vector", lambda e: e.reduce_max(nmx, lg_sb[:, :], axis=AX.X, negate=True),
                      reads=[b_lg], writes=[b_rt])
                em.op("scalar", lambda e: e.activation(probs, lg_sb[:, :], AF.Exp, bias=nmx, accum_out=ssum),
                      reads=[b_lg, b_rt], writes=[b_rt])
                em.op("vector", lambda e: e.reciprocal(rs, ssum), reads=[b_rt], writes=[b_rt])
                em.op("vector", lambda e: e.tensor_scalar(probs, probs, rs, None, ALU.mult),
                      reads=[b_rt], writes=[b_rt])
                em.op("vector", lambda e: e.reduce_max(m1, probs, axis=AX.X), reads=[b_rt], writes=[b_rt])
                em.op("vector", lambda e: e.tensor_scalar(eq, probs, m1, None, ALU.is_ge),
                      reads=[b_rt], writes=[b_rt])
                em.op("vector", lambda e: e.scalar_tensor_tensor(u1, eq, -1e6, iotab_sb[:, :], ALU.mult, ALU.add),
                      reads=[b_rt, b_iotab], writes=[b_rt])
                em.op("vector", lambda e: e.tensor_reduce(idx1, u1, AX.X, ALU.min),
                      reads=[b_rt], writes=[b_rt])
                em.op("vector", lambda e: e.tensor_scalar(oh1, iota_sb[:, :], idx1, None, ALU.is_equal),
                      reads=[b_rt, b_iota], writes=[b_rt])
                em.op("vector", lambda e: e.scalar_tensor_tensor(p2, oh1, -1e6, probs, ALU.mult, ALU.add),
                      reads=[b_rt], writes=[b_rt])
                em.op("vector", lambda e: e.reduce_max(m2, p2, axis=AX.X), reads=[b_rt], writes=[b_rt])
                em.op("vector", lambda e: e.tensor_scalar(eq, p2, m2, None, ALU.is_ge),
                      reads=[b_rt], writes=[b_rt])
                em.op("vector", lambda e: e.scalar_tensor_tensor(u1, eq, -1e6, iotab_sb[:, :], ALU.mult, ALU.add),
                      reads=[b_rt, b_iotab], writes=[b_rt])
                em.op("vector", lambda e: e.tensor_reduce(idx2, u1, AX.X, ALU.min),
                      reads=[b_rt], writes=[b_rt])
                em.op("vector", lambda e: e.tensor_scalar(oh2, iota_sb[:, :], idx2, None, ALU.is_equal),
                      reads=[b_rt, b_iota], writes=[b_rt])
                em.op("vector", lambda e: e.tensor_copy(stk[:, 0:8], oh1), reads=[b_rt], writes=[b_stk])
                em.op("vector", lambda e: e.tensor_copy(stk[:, 8:16], oh2), reads=[b_rt], writes=[b_stk])
                em.op("vector", lambda e: e.tensor_copy(stk[:, 16:17], m1), reads=[b_rt], writes=[b_stk])
                em.op("vector", lambda e: e.tensor_copy(stk[:, 17:18], m2), reads=[b_rt], writes=[b_stk])
                em.op("tensor", lambda e: e.transpose(ps6[0:18, 0:128], stk[:, :], idb_sb[:, :]),
                      reads=[b_stk, b_idb], writes=[b_ps6])
                em.op("vector", lambda e, grp=grp: e.tensor_copy(coefT[:, grp:grp + 128], ps6[0:18, 0:128]),
                      reads=[b_ps6], writes=[b_coefT])
                if debug and st_i == 0 and g == 0 and ti == 0:
                    em.op("sync", lambda e: e.dma_start(out=d_rdbg[:, :], in_=rt[:, :]),
                          reads=[b_rt], q="qs")

        if debug and st_i == 0:
            em.op("vector", lambda e: e.tensor_copy(S.t1[0:18, :], coefT[:, 0:512]),
                  reads=[b_coefT], writes=[S.b_t1])
            em.op("sync", lambda e: e.dma_start(out=d_cdbg[:, 0:512], in_=S.t1[0:18, :]),
                  reads=[S.b_t1], q="qs")

        for k in range(2):
            for q5 in range(NQ5):
                em.op("tensor", lambda e, k=k, q5=q5: e.matmul(
                    ps5[:, :], sel_sb[:, (16 + k) * 128:(17 + k) * 128],
                    coefT[:, q5 * 512:(q5 + 1) * 512], start=True, stop=True),
                    reads=[b_sel, b_coefT], writes=[b_ps5])
                em.op("vector", lambda e, k=k, q5=q5: e.tensor_copy(
                    wkb[:, k * NT + q5 * 512: k * NT + (q5 + 1) * 512], ps5[:, :]),
                    reads=[b_ps5], writes=[b_wkb])

        for e_i in range(8):
            wA, b_wA = rA.next(); wB, b_wB = rB.next(); wC, b_wC = rC.next()
            b13, b_b13 = r13.next(); b2s, b_b2 = r2r.next()
            b1s = b13[:, 0:8]; b3s = b13[:, 8:16]
            em.op("gpsimd", lambda e, e_i=e_i, wA=wA: e.dma_start(
                out=wA[:, :], in_=d_w1[e_i].rearrange("(kc p) n -> p kc n", p=128)),
                writes=[b_wA], q="qg")
            em.op("gpsimd", lambda e, e_i=e_i, wB=wB: e.dma_start(
                out=wB[:, :], in_=d_w3[e_i].rearrange("(kc p) n -> p kc n", p=128)),
                writes=[b_wB], q="qg")
            em.op("gpsimd", lambda e, e_i=e_i, wC=wC: e.dma_start(
                out=wC[:, :], in_=d_w2[e_i].rearrange("(kc p) n -> p kc n", p=128)),
                writes=[b_wC], q="qg")
            em.op("gpsimd", lambda e, e_i=e_i, b2s=b2s: e.dma_start(
                out=b2s[:, :], in_=d_b2[e_i:e_i + 1, :]), writes=[b_b2], q="qg")
            em.op("sync", lambda e, e_i=e_i, b1s=b1s: e.dma_start(
                out=b1s, in_=d_b1[e_i]), writes=[b_b13], q="qs")
            em.op("sync", lambda e, e_i=e_i, b3s=b3s: e.dma_start(
                out=b3s, in_=d_b3[e_i]), writes=[b_b13], q="qs")
            for (wb, b_wb, dst, b_dst) in ((wA, b_wA, X1, b_X1), (wB, b_wB, X3, b_X3)):
                for dd in range(8):
                    psx, b_psx = (ps1, b_ps1) if dd % 2 == 0 else (ps3, b_ps3)
                    for q5 in range(NQ5):
                        for kc in range(8):
                            em.op("tensor", lambda e, wb=wb, dd=dd, q5=q5, kc=kc, psx=psx: e.matmul(
                                psx[:, :], wsl(wb, kc, dd * 128, 128),
                                xn[:, kc * NT + q5 * 512: kc * NT + q5 * 512 + 512],
                                start=(kc == 0), stop=(kc == 7)),
                                reads=[b_wb, b_xn], writes=[b_psx], sig=(kc == 7))
                        em.op("scalar", lambda e, dd=dd, q5=q5, dst=dst, psx=psx, b_psx=b_psx: e.activation(
                            dst[:, dd * NT + q5 * 512: dd * NT + q5 * 512 + 512], psx[:, :], AF.Copy),
                            reads=[b_psx], writes=[b_dst])
            for k in range(2):
                for dd in range(8):
                    for q5 in range(NQ5):
                        em.op("vector", lambda e, dd=dd, q5=q5, k=k: e.tensor_tensor(
                            tsc[:, :], X1[:, dd * NT + q5 * 512: dd * NT + q5 * 512 + 512],
                            wkb[:, k * NT + q5 * 512: k * NT + q5 * 512 + 512], ALU.mult),
                            reads=[b_X1, b_wkb], writes=[b_tsc])
                        em.op("scalar", lambda e, dd=dd, q5=q5, b1s=b1s: e.activation(
                            hb[:, dd * NT + q5 * 512: dd * NT + q5 * 512 + 512], tsc[:, :], AF.Silu,
                            bias=b1s[:, dd:dd + 1]),
                            reads=[b_tsc, b_b13], writes=[b_hb])
                for q5 in range(NQ5):
                    em.op("tensor", lambda e, e_i=e_i, k=k, q5=q5: e.matmul(
                        ps5[:, :], sel_sb[:, (8 * k + e_i) * 128:(8 * k + e_i + 1) * 128],
                        coefT[:, q5 * 512:(q5 + 1) * 512], start=True, stop=True),
                        reads=[b_sel, b_coefT], writes=[b_ps5])
                    for dd in range(8):
                        psx, b_psx = (ps2, b_ps2) if dd % 2 == 0 else (ps4, b_ps4)
                        for kc in range(8):
                            em.op("tensor", lambda e, dd=dd, q5=q5, kc=kc, wC=wC, psx=psx: e.matmul(
                                psx[:, :], wsl(wC, kc, dd * 128, 128),
                                hb[:, kc * NT + q5 * 512: kc * NT + q5 * 512 + 512],
                                start=(kc == 0), stop=False, skip_group_check=True),
                                reads=[b_wC, b_hb], writes=[b_psx], sig=False)
                        em.op("tensor", lambda e, dd=dd, q5=q5, b2s=b2s, psx=psx, b_psx=b_psx: e.matmul(
                            psx[:, :], b2s[:, dd * 128:(dd + 1) * 128],
                            C.ones1x512[:, :], start=False, stop=True, skip_group_check=True),
                            reads=[b_b2, C.b_ones1x512], writes=[b_psx], sig=True)
                        em.op("vector", lambda e, dd=dd, q5=q5, k=k: e.tensor_tensor(
                            t3a[:, :], X3[:, dd * NT + q5 * 512: dd * NT + q5 * 512 + 512],
                            wkb[:, k * NT + q5 * 512: k * NT + q5 * 512 + 512], ALU.mult),
                            reads=[b_X3, b_wkb], writes=[b_t3a])
                        em.op("vector", lambda e, dd=dd, b3s=b3s, psx=psx, b_psx=b_psx: e.scalar_tensor_tensor(
                            c1t[:, :], t3a[:, :], b3s[:, dd:dd + 1], psx[:, :], ALU.add, ALU.mult),
                            reads=[b_t3a, b_b13, b_psx], writes=[b_c1])
                        em.op("vector", lambda e: e.tensor_tensor(
                            c1t[:, :], c1t[:, :], ps5[:, :], ALU.mult),
                            reads=[b_c1, b_ps5], writes=[b_c1])
                        em.op("vector", lambda e, dd=dd, q5=q5: e.tensor_tensor(
                            y[:, dd * NT + q5 * 512: dd * NT + q5 * 512 + 512],
                            y[:, dd * NT + q5 * 512: dd * NT + q5 * 512 + 512],
                            c1t[:, :], ALU.add),
                            reads=[b_c1], writes=[b_y])

        for ti in range(NQ5):
            co = ti * 512
            for Ih in range(2):
                wA, b_wA = rA.next(); wB, b_wB = rB.next(); wC, b_wC = rC.next()
                em.op("gpsimd", lambda e, Ih=Ih, wA=wA: e.dma_start(
                    out=wA[:, :], in_=d_sw1.rearrange("(kc p) n -> p kc n", p=128)[:, :, Ih * 1024:(Ih + 1) * 1024]),
                    writes=[b_wA], q="qg")
                em.op("gpsimd", lambda e, Ih=Ih, wB=wB: e.dma_start(
                    out=wB[:, :], in_=d_sw2.rearrange("(kc p) n -> p kc n", p=128)[:, :, Ih * 1024:(Ih + 1) * 1024]),
                    writes=[b_wB], q="qg")
                em.op("gpsimd", lambda e, Ih=Ih, wC=wC: e.dma_start(
                    out=wC[:, :], in_=d_sw3.rearrange("(kc p) n -> p kc n", p=128)[:, 8 * Ih:8 * Ih + 8, :]),
                    writes=[b_wC], q="qg")
                for ic in range(8):
                    for kc in range(8):
                        em.op("tensor", lambda e, ic=ic, kc=kc, co=co, wA=wA: e.matmul(
                            ps1[:, :], wsl(wA, kc, ic * 128, 128),
                            xn[:, kc * NT + co: kc * NT + co + 512],
                            start=(kc == 0), stop=(kc == 7)),
                            reads=[b_wA, b_xn], writes=[b_ps1], sig=(kc == 7))
                    em.op("scalar", lambda e, ic=ic: e.activation(
                        hb[:, ic * 512:(ic + 1) * 512], ps1[:, :], AF.Silu),
                        reads=[b_ps1], writes=[b_hb])
                    for kc in range(8):
                        em.op("tensor", lambda e, ic=ic, kc=kc, co=co, wB=wB: e.matmul(
                            ps2[:, :], wsl(wB, kc, ic * 128, 128),
                            xn[:, kc * NT + co: kc * NT + co + 512],
                            start=(kc == 0), stop=(kc == 7)),
                            reads=[b_wB, b_xn], writes=[b_ps2], sig=(kc == 7))
                    em.op("vector", lambda e, ic=ic: e.tensor_tensor(
                        X3[:, ic * 512:(ic + 1) * 512],
                        ps2[:, :], hb[:, ic * 512:(ic + 1) * 512], ALU.mult),
                        reads=[b_ps2, b_hb], writes=[b_X3])
                for dd in range(8):
                    for ic in range(8):
                        em.op("tensor", lambda e, dd=dd, ic=ic, wC=wC: e.matmul(
                            ps1[:, :], wsl(wC, ic, dd * 128, 128),
                            X3[:, ic * 512:(ic + 1) * 512],
                            start=(ic == 0), stop=(ic == 7)),
                            reads=[b_wC, b_X3], writes=[b_ps1], sig=(ic == 7))
                    em.op("vector", lambda e, dd=dd, co=co: e.tensor_tensor(
                        y[:, dd * NT + co: dd * NT + co + 512],
                        y[:, dd * NT + co: dd * NT + co + 512], ps1[:, :], ALU.add),
                        reads=[b_ps1], writes=[b_y])

        if debug and st_i == 0:
            em.op("sync", lambda e: e.dma_start(out=d_ydbg[:, :], in_=y[:, :]), reads=[b_y], q="qs")

        for ti in range(NQ5):
            t0 = tb0 + ti * 512
            co = ti * 512
            em.op("sync", lambda e, b0=b0, t0=t0: e.dma_start(
                out=xt[:, :], in_=d_x1[b0].rearrange("(c p) t -> p c t", p=128)[:, :, t0:t0 + 512]),
                reads=[bd_in], writes=[b_xt], q="qs")
            emit_ln(em, C, S, ps1[0:1, :], ps2[0:1, :], ps3[:, :], ps4[:, :],
                    b_ps1, b_ps2, b_ps3, b_ps4,
                    src_ap=lambda c, co=co: y[:, c * NT + co: c * NT + co + 512], b_src=b_y,
                    out_ap=None, b_out=None,
                    out32_ap=lambda c: xt[:, c * 512:(c + 1) * 512], b_out32=b_xt,
                    res_ap=lambda c: xt[:, c * 512:(c + 1) * 512], b_res=b_xt)
            em.op("sync", lambda e, b0=b0, t0=t0: e.dma_start(
                out=d_x2[b0].rearrange("(c p) t -> p c t", p=128)[:, :, t0:t0 + 512],
                in_=xt[:, :].rearrange("p (c t) -> p c t", c=8)),
                reads=[b_xt], writes=[bd_out], q="qs")

    ctx.close()
    if F is None:
        em.block()
        qs_final = em.cnt["qs"]
        em.op("sync", lambda e: e.nop(), extra=[("qs", qs_final)], sig=False)
        em.close()
        octx.close()
        return nc
    return None


def p2_inputs(inp):
    gw = np.ascontiguousarray(inp["L0_ffn_gate_w"], np.float32)
    gwT = gw.T.reshape(8, 128, 8).transpose(1, 0, 2)
    w1t = np.ascontiguousarray(np.asarray(inp["L0_ffn_w1"], np.float32).transpose(0, 2, 1))
    w2t = np.ascontiguousarray(np.asarray(inp["L0_ffn_w2"], np.float32).transpose(0, 2, 1))
    w3t = np.ascontiguousarray(np.asarray(inp["L0_ffn_w3"], np.float32).transpose(0, 2, 1))
    b1 = np.ascontiguousarray(np.asarray(inp["L0_ffn_b1"], np.float32).reshape(8, 8, 128).transpose(0, 2, 1))
    b3 = np.ascontiguousarray(np.asarray(inp["L0_ffn_b3"], np.float32).reshape(8, 8, 128).transpose(0, 2, 1))
    b2 = np.ascontiguousarray(inp["L0_ffn_b2"], np.float32)
    sw1t = np.ascontiguousarray(np.asarray(inp["L0_ffn_sw1"], np.float32).T)
    sw2t = np.ascontiguousarray(np.asarray(inp["L0_ffn_sw2"], np.float32).T)
    sw3t = np.ascontiguousarray(np.asarray(inp["L0_ffn_sw3"], np.float32).T)
    return {"gw": np.ascontiguousarray(gwT), "w1t": w1t, "w2t": w2t, "w3t": w3t,
            "b1c": b1, "b3c": b3, "b2r": b2, "sw1t": sw1t, "sw2t": sw2t, "sw3t": sw3t}


def run_p2(x1_fm_cores, winp, debug=False):
    nc = build_p2(debug=debug)
    in_maps = []
    for c in range(NCORES):
        m = dict(winp)
        m["x1"] = x1_fm_cores[c]
        in_maps.append(m)
    res = run_bass_kernel_spmd(nc, in_maps, core_ids=list(range(NCORES)))
    if debug:
        return res.results
    return [r["x2"] for r in res.results]



def build_p3(debug=False, F=None):
    if F is None:
        nc = bass.Bass("TRN2", target_bir_lowering=False)
        octx = ExitStack()
        em = Em(nc, octx)
        C = Consts(nc, em, octx)
        S = LnScratch(nc, octx)
        d_x2 = nc.dram_tensor("x2", [BPC, D, T], F32, kind="ExternalInput").ap()
        d_x3 = nc.dram_tensor("x3", [BPC, D, T], F32, kind="ExternalOutput").ap()
        bd_in = Buf("bd_in"); bd_out = Buf("bd_out"); bd_lg = Buf("bd_lg")
    else:
        nc, em, C, S, streams, dbufs = F
        d_x2 = streams["x2"]; d_x3 = streams["x3"]
        bd_in, bd_out = dbufs["x2"], dbufs["x3"]
    ctx = ExitStack()
    d_wq = nc.dram_tensor("wqT", [D, D], F32, kind="ExternalInput").ap()
    d_wk = nc.dram_tensor("wkT", [D, D], F32, kind="ExternalInput").ap()
    d_wv = nc.dram_tensor("wvT", [D, D], F32, kind="ExternalInput").ap()
    d_wrp = nc.dram_tensor("wrp", [16, 64, D], F32, kind="ExternalInput").ap()

    ROT = 32
    THETA = 10000.0
    inv = 1.0 / THETA ** (np.arange(0, ROT, 2, dtype=np.float64) / ROT)
    freq_row = np.repeat(inv, 2)
    c_idx = np.arange(T)
    s_true = 16 * (c_idx % 128) + c_idx // 128
    ang = s_true[None, :] * freq_row[:, None]
    cos_t = np.ones((64, T), dtype=np.float64)
    sin_t = np.zeros((64, T), dtype=np.float64)
    cos_t[0:32] = np.cos(ang)
    sin_t[0:32] = np.sin(ang)
    d_cosq = nc.inline_tensor((cos_t * 0.25).astype(BF), "c_cosq").ap()
    d_sinq = nc.inline_tensor((sin_t * 0.25).astype(BF), "c_sinq").ap()
    d_cosk = nc.inline_tensor(cos_t.astype(BF), "c_cosk").ap()
    d_sink = nc.inline_tensor(sin_t.astype(BF), "c_sink").ap()
    Pt = np.zeros((64, 64), dtype=BF)
    for i in range(16):
        Pt[2 * i + 1, 2 * i] = -1.0
        Pt[2 * i, 2 * i + 1] = 1.0
    d_pt = nc.inline_tensor(Pt, "c3_rotP").ap()
    d_idb = nc.inline_tensor(np.eye(128, dtype=BF), "c3_identbf").ap()

    d_qt = nc.dram_tensor("qt3", [BPC, T, D], BF16, kind="Internal").ap()
    d_kt = nc.dram_tensor("kt3", [BPC, T, D], BF16, kind="Internal").ap()
    d_vt = nc.dram_tensor("vt3", [BPC, T, D], BF16, kind="Internal").ap()
    if debug:
        d_o1dbg = nc.dram_tensor("o1dbg", [128, 16 * 1024], BF16, kind="ExternalOutput").ap()
        d_qpdbg = nc.dram_tensor("qpdbg", [65, T], BF16, kind="ExternalOutput").ap()

    xt = ctx.enter_context(nc.sbuf_tensor("p3_xt", [128, 4096], F32))
    xn = ctx.enter_context(nc.sbuf_tensor("p3_xn", [128, 4096], BF16))
    b_xt = Buf("xt"); b_xn = Buf("xn")

    cosq = ctx.enter_context(nc.sbuf_tensor("p3_cosq", [64, T], BF16))
    sinq = ctx.enter_context(nc.sbuf_tensor("p3_sinq", [64, T], BF16))
    cosk = ctx.enter_context(nc.sbuf_tensor("p3_cosk", [64, T], BF16))
    sink = ctx.enter_context(nc.sbuf_tensor("p3_sink", [64, T], BF16))
    pt_m = ctx.enter_context(nc.sbuf_tensor("p3_pt_m", [64, 64], BF16))
    idb_sb = ctx.enter_context(nc.sbuf_tensor("p3_idb_sb", [128, 128], BF16))
    b_tab = Buf("ropetabs"); b_pt_m = Buf("pt_m"); b_idb = Buf("idb")

    em.block()
    C.load(em)
    for d_t, sb in ((d_cosq, cosq), (d_sinq, sinq), (d_cosk, cosk), (d_sink, sink)):
        em.op("sync", lambda e, d_t=d_t, sb=sb: e.dma_start(out=sb[:, :], in_=d_t[:, :]),
              writes=[b_tab], q="qs")
    em.op("sync", lambda e: e.dma_start(out=pt_m[:, :], in_=d_pt[:, :]), writes=[b_pt_m], q="qs")
    em.op("sync", lambda e: e.dma_start(out=idb_sb[:, :], in_=d_idb[:, :]), writes=[b_idb], q="qs")

    ctxC1 = ExitStack()
    wq_sb = ctxC1.enter_context(nc.sbuf_tensor("p3_wq_sb", [128, 8192], BF16))
    wk_sb = ctxC1.enter_context(nc.sbuf_tensor("p3_wk_sb", [128, 8192], BF16))
    wv_sb = ctxC1.enter_context(nc.sbuf_tensor("p3_wv_sb", [128, 8192], BF16))
    o_sb = ctxC1.enter_context(nc.sbuf_tensor("p3_o_sb", [128, 1024], BF16))
    b_wq = Buf("wq"); b_wk = Buf("wk"); b_wv = Buf("wv"); b_o = Buf("o_sb")
    psq = ctxC1.enter_context(nc.psum_tensor("p3_psq", [128, 512], F32))
    psk = ctxC1.enter_context(nc.psum_tensor("p3_psk", [128, 512], F32))
    psM = ctxC1.enter_context(nc.psum_tensor("p3_psM", [128, 512], F32))
    psR = ctxC1.enter_context(nc.psum_tensor("p3_psR", [128, 512], F32))
    b_psq = Buf("psq"); b_psk = Buf("psk"); b_psM = Buf("psM"); b_psR = Buf("psR")

    for w_sb, d_w, b_w in ((wq_sb, d_wq, b_wq), (wk_sb, d_wk, b_wk), (wv_sb, d_wv, b_wv)):
        em.op("gpsimd", lambda e, w_sb=w_sb, d_w=d_w: e.dma_start(
            out=w_sb[:, :], in_=d_w.rearrange("(kc p) n -> p kc n", p=128)),
            writes=[b_w], q="qg")

    for b in range(BPC):
        for ti in range(4):
            t0 = ti * 512
            em.op("sync", lambda e, b=b, t0=t0: e.dma_start(
                out=xt[:, :], in_=d_x2[b].rearrange("(c p) t -> p c t", p=128)[:, :, t0:t0 + 512]),
                reads=[bd_in], writes=[b_xt], q="qs")
            emit_ln(em, C, S, psq[0:1, :], psk[0:1, :], psM[:, :], psR[:, :],
                    b_psq, b_psk, b_psM, b_psR,
                    src_ap=lambda c: xt[:, c * 512:(c + 1) * 512], b_src=b_xt,
                    out_ap=lambda c: xn[:, c * 512:(c + 1) * 512], b_out=b_xn)
            for (w_sb, b_w, dgt) in ((wq_sb, b_wq, d_qt), (wk_sb, b_wk, d_kt), (wv_sb, b_wv, d_vt)):
                for ts in range(4):
                    for dv in range(2):
                        for kc in range(8):
                            em.op("tensor", lambda e, w_sb=w_sb, ts=ts, dv=dv, kc=kc: e.matmul(
                                psq[:, :], xn[:, kc * 512 + ts * 128: kc * 512 + (ts + 1) * 128],
                                w_sb[:, kc * 1024 + dv * 512: kc * 1024 + (dv + 1) * 512],
                                start=(kc == 0), stop=(kc == 7)),
                                reads=[b_w, b_xn], writes=[b_psq], sig=(kc == 7))
                        em.op("scalar", lambda e, dv=dv: e.activation(
                            o_sb[:, dv * 512:(dv + 1) * 512], psq[:, :], AF.Copy),
                            reads=[b_psq], writes=[b_o])
                    em.op("sync", lambda e, b=b, t0=t0, ts=ts, dgt=dgt: e.dma_start(
                        out=dgt[b][t0 + ts * 128: t0 + (ts + 1) * 128, :], in_=o_sb[:, :]),
                        reads=[b_o], q="qs")
    ctxC1.close()

    em.block()
    qh_sb = ctx.enter_context(nc.sbuf_tensor("p3_qh_sb", [128, 1024], BF16))
    kh_sb = ctx.enter_context(nc.sbuf_tensor("p3_kh_sb", [128, 1024], BF16))
    b_qh = Buf("qh"); b_kh = Buf("kh")
    q_perm = ctx.enter_context(nc.sbuf_tensor("p3_q_perm", [64, T], BF16))
    k_perm = ctx.enter_context(nc.sbuf_tensor("p3_k_perm", [64, T], BF16))
    b_qp = Buf("q_perm"); b_kp = Buf("k_perm")
    q_aug = ctx.enter_context(nc.sbuf_tensor("p3_q_aug", [65, T], BF16))
    k_aug = ctx.enter_context(nc.sbuf_tensor("p3_k_aug", [65, T], BF16))
    b_qa = Buf("q_aug"); b_ka = Buf("k_aug")
    v65 = ctx.enter_context(nc.sbuf_tensor("p3_v65", [128, 16 * 65], BF16))
    b_v65 = Buf("v65")
    rop_a = ctx.enter_context(nc.sbuf_tensor("p3_rop_a", [64, 128], F32))
    b_ropa = Buf("rop_a")
    qsq = ctx.enter_context(nc.sbuf_tensor("p3_qsq", [64, T], BF16))
    b_qsq = Buf("qsq")
    brow = ctx.enter_context(nc.sbuf_tensor("p3_brow", [1, T], F32))
    brow_bf = ctx.enter_context(nc.sbuf_tensor("p3_brow_bf", [1, T], BF16))
    kmx = ctx.enter_context(nc.sbuf_tensor("p3_kmx", [1, 8], F32))
    b_brow = Buf("brow"); b_browbf = Buf("brow_bf"); b_kmx = Buf("kmx")
    pt_sb = ctx.enter_context(nc.sbuf_tensor("p3_pt_sb", [128, T], BF16))
    b_pt = Buf("pt_sb")
    out1 = ctx.enter_context(nc.sbuf_tensor("p3_out1", [128, 16 * 1024], BF16))
    b_out1 = Buf("out1")
    rden = ctx.enter_context(nc.sbuf_tensor("p3_rden", [128, 1], F32))
    b_rden = Buf("rden")

    for b in range(BPC):
        for h in range(NH):
            em.op("sync", lambda e, b=b, h=h: e.dma_start(
                out=qh_sb[:, :], in_=d_qt[b][128 * h:128 * (h + 1), :]),
                writes=[b_qh], q="qs")
            em.op("sync", lambda e, b=b, h=h: e.dma_start(
                out=kh_sb[:, :], in_=d_kt[b][128 * h:128 * (h + 1), :]),
                writes=[b_kh], q="qs")
            em.op("vector", lambda e: e.memset(v65[:, :], 1.0), writes=[b_v65])
            em.op("sync", lambda e, b=b, h=h: e.dma_start(
                out=v65[:, :].rearrange("p (c d) -> p c d", d=65)[:, :, 0:64],
                in_=d_vt[b].rearrange("(c p) d -> p c d", p=128)[128 * h // 128 * 0:128, :, 0:64] if False else
                    d_vt[b][128 * h:128 * (h + 1), :].rearrange("p (c d) -> p c d", d=64)),
                writes=[b_v65], q="qs")

            ctxT = ExitStack()
            psT = ctxT.enter_context(nc.psum_tensor(f"psT_{b}_{h}", [64, 128], BF16))
            psR2 = ctxT.enter_context(nc.psum_tensor(f"psR2_{b}_{h}", [64, 128], F32))
            b_psT = Buf("psT"); b_psR2 = Buf("psR2")
            for (src_sb, b_srcb, perm, b_perm) in ((qh_sb, b_qh, q_perm, b_qp), (kh_sb, b_kh, k_perm, b_kp)):
                for g in range(16):
                    em.op("tensor", lambda e, g=g, src_sb=src_sb: e.transpose(
                        psT[:, :], src_sb[:, 64 * g:64 * (g + 1)], idb_sb[:, :]),
                        reads=[b_srcb, b_idb], writes=[b_psT])
                    em.op("vector", lambda e, g=g, perm=perm: e.tensor_copy(
                        perm[:, 128 * g:128 * (g + 1)], psT[:, :]),
                        reads=[b_psT], writes=[b_perm])
            for (perm, b_perm, aug, b_aug, cos_s, sin_s) in (
                    (q_perm, b_qp, q_aug, b_qa, cosq, sinq),
                    (k_perm, b_kp, k_aug, b_ka, cosk, sink)):
                for g in range(16):
                    sl = slice(128 * g, 128 * (g + 1))
                    em.op("tensor", lambda e, sl=sl, perm=perm: e.matmul(
                        psR2[:, :], pt_m[:, :], perm[:, sl], start=True, stop=True),
                        reads=[b_pt_m, b_perm], writes=[b_psR2])
                    em.op("vector", lambda e, sl=sl, perm=perm, cos_s=cos_s: e.tensor_tensor(
                        rop_a[:, :], perm[:, sl], cos_s[:, sl], ALU.mult),
                        reads=[b_perm, b_tab], writes=[b_ropa])
                    em.op("vector", lambda e, sl=sl, sin_s=sin_s: e.tensor_tensor(
                        psR2[:, :], psR2[:, :], sin_s[:, sl], ALU.mult),
                        reads=[b_psR2, b_tab], writes=[b_psR2])
                    em.op("vector", lambda e, sl=sl, aug=aug: e.tensor_tensor(
                        aug[0:64, sl], rop_a[:, :], psR2[:, :], ALU.add),
                        reads=[b_ropa, b_psR2], writes=[b_aug])
            ctxT.close()
            em.op("vector", lambda e: e.memset(k_aug[64:65, :], 1.0), writes=[b_ka])

            ctxS = ExitStack()
            psS = ctxS.enter_context(nc.psum_tensor(f"psS_{b}_{h}", [128, T], F32))
            psO2 = ctxS.enter_context(nc.psum_tensor(f"psO2_{b}_{h}", [128, T], F32))
            b_psS = Buf("psS"); b_psO2 = Buf("psO2")
            ps_row = psS[0:1, :]
            em.op("scalar", lambda e: e.activation(qsq[:, :], k_aug[0:64, :], AF.Square),
                  reads=[b_ka], writes=[b_qsq])
            for qc in range(4):
                em.op("tensor", lambda e, qc=qc: e.matmul(
                    ps_row[:, qc * 512:(qc + 1) * 512], C.ones128[0:64, :],
                    qsq[:, qc * 512:(qc + 1) * 512], start=True, stop=True),
                    reads=[b_qsq, C.b_ones128], writes=[b_psS], sig=(qc == 3))
            em.op("vector", lambda e: e.reduce_max(kmx[:, 0:1], ps_row, axis=AX.X),
                  reads=[b_psS], writes=[b_kmx])
            em.op("scalar", lambda e: e.activation(kmx[:, 1:2], kmx[:, 0:1], AF.Sqrt),
                  reads=[b_kmx], writes=[b_kmx])
            em.op("scalar", lambda e: e.activation(qsq[:, :], q_aug[0:64, :], AF.Square),
                  reads=[b_qa], writes=[b_qsq])
            for qc in range(4):
                em.op("tensor", lambda e, qc=qc: e.matmul(
                    ps_row[:, qc * 512:(qc + 1) * 512], C.ones128[0:64, :],
                    qsq[:, qc * 512:(qc + 1) * 512], start=True, stop=True),
                    reads=[b_qsq, C.b_ones128], writes=[b_psS], sig=(qc == 3))
            em.op("scalar", lambda e: e.activation(brow[:, :], ps_row, AF.Sqrt),
                  reads=[b_psS], writes=[b_brow])
            em.op("vector", lambda e: e.tensor_scalar(
                brow_bf[:, :], brow[:, :], kmx[:, 1:2], -1.02, ALU.mult, ALU.mult),
                reads=[b_brow, b_kmx], writes=[b_browbf])
            em.op("vector", lambda e: e.tensor_copy(q_aug[64:65, :], brow_bf[:, :]),
                  reads=[b_browbf], writes=[b_qa])
            if debug and b == 0 and h == 0:
                em.op("sync", lambda e: e.dma_start(out=d_qpdbg[:, :], in_=q_aug[:, :]),
                      reads=[b_qa], q="qs")

            for kc in range(16):
                for qc in range(4):
                    em.op("tensor", lambda e, kc=kc, qc=qc: e.matmul(
                        psS[:, qc * 512:(qc + 1) * 512],
                        k_aug[:, kc * 128:(kc + 1) * 128],
                        q_aug[:, qc * 512:(qc + 1) * 512],
                        start=True, stop=True),
                        reads=[b_ka, b_qa], writes=[b_psS], sig=(qc == 3))
                em.op("scalar", lambda e, kc=kc: e.activation(
                    pt_sb[:, :], psS[:, :], AF.Exp),
                    reads=[b_psS], writes=[b_pt])
                for qc2 in range(16):
                    em.op("tensor", lambda e, kc=kc, qc2=qc2: e.matmul(
                        psO2[:, 128 * qc2: 128 * qc2 + 65],
                        pt_sb[:, qc2 * 128:(qc2 + 1) * 128],
                        v65[:, kc * 65:(kc + 1) * 65],
                        start=(kc == 0 and qc2 % 4 == 0), stop=(kc == 15),
                        skip_group_check=True),
                        reads=[b_pt, b_v65], writes=[b_psO2], sig=(kc == 15 and qc2 == 15))
            for qc2 in range(16):
                em.op("vector", lambda e, qc2=qc2: e.reciprocal(
                    rden[:, :], psO2[:, 128 * qc2 + 64: 128 * qc2 + 65]),
                    reads=[b_psO2], writes=[b_rden])
                em.op("vector", lambda e, qc2=qc2, h=h: e.tensor_scalar(
                    out1[:, qc2 * 1024 + 64 * h: qc2 * 1024 + 64 * (h + 1)],
                    psO2[:, 128 * qc2: 128 * qc2 + 64], rden[:, :], None, ALU.mult),
                    reads=[b_psO2, b_rden], writes=[b_out1])
            ctxS.close()

        if debug and b == 0:
            em.op("sync", lambda e: e.dma_start(out=d_o1dbg[:, :], in_=out1[:, :]),
                  reads=[b_out1], q="qs")

        ctxW = ExitStack()
        wrp_sb = ctxW.enter_context(nc.sbuf_tensor(f"wrp_sb_{b}", [128, 16 * 1024], BF16))
        b_wrp = Buf("wrp")
        fin = ctxW.enter_context(nc.sbuf_tensor(f"fin_{b}", [128, 4 * 1024], BF16))
        b_fin = Buf("fin")
        u_f = ctxW.enter_context(nc.sbuf_tensor(f"u_f_{b}", [128, 4096], F32))
        b_uf = Buf("u_f")
        psF = ctxW.enter_context(nc.psum_tensor(f"psF_{b}", [128, 512], F32))
        psT2 = ctxW.enter_context(nc.psum_tensor(f"psT2_{b}", [128, 128], BF16))
        psA2 = ctxW.enter_context(nc.psum_tensor(f"psA2_{b}", [128, 512], F32))
        psB2 = ctxW.enter_context(nc.psum_tensor(f"psB2_{b}", [128, 512], F32))
        psM2 = ctxW.enter_context(nc.psum_tensor(f"psM2_{b}", [128, 512], F32))
        psR3 = ctxW.enter_context(nc.psum_tensor(f"psR3_{b}", [128, 512], F32))
        b_psF = Buf("psF"); b_psT2 = Buf("psT2")
        b_psA2 = Buf("psA2"); b_psB2 = Buf("psB2"); b_psM2 = Buf("psM2"); b_psR3 = Buf("psR3")
        em.op("gpsimd", lambda e: e.dma_start(
            out=wrp_sb[0:64, :].rearrange("t (g n) -> t g n", g=16),
            in_=d_wrp.rearrange("g t n -> t g n")),
            writes=[b_wrp], q="qg")
        em.op("gpsimd", lambda e: e.dma_start(
            out=wrp_sb[64:128, :].rearrange("t (g n) -> t g n", g=16),
            in_=d_wrp.rearrange("g t n -> t g n")),
            writes=[b_wrp], q="qg")
        for u in range(2):
            for tg in range(2):
                for j in range(4):
                    tc = tg * 4 + j
                    for pc in range(2):
                        for g in range(16):
                            em.op("tensor", lambda e, u=u, tc=tc, pc=pc, g=g: e.matmul(
                                psF[:, :], wrp_sb[64 * u:64 * u + 64, g * 1024 + tc * 128:g * 1024 + (tc + 1) * 128],
                                out1[64 * u:64 * u + 64, g * 1024 + pc * 512:g * 1024 + pc * 512 + 512],
                                start=(g == 0), stop=(g == 15)),
                                reads=[b_wrp, b_out1], writes=[b_psF], sig=(g == 15))
                        em.op("vector", lambda e, j=j, pc=pc: e.tensor_copy(
                            fin[:, j * 1024 + pc * 512: j * 1024 + (pc + 1) * 512], psF[:, :]),
                            reads=[b_psF], writes=[b_fin])
                for j in range(4):
                    for cfeat in range(8):
                        em.op("tensor", lambda e, j=j, cfeat=cfeat: e.transpose(
                            psT2[:, :], fin[:, j * 1024 + cfeat * 128:j * 1024 + (cfeat + 1) * 128],
                            idb_sb[:, :]),
                            reads=[b_fin, b_idb], writes=[b_psT2])
                        em.op("vector", lambda e, j=j, cfeat=cfeat: e.tensor_copy(
                            u_f[:, cfeat * 512 + j * 128: cfeat * 512 + (j + 1) * 128], psT2[:, :]),
                            reads=[b_psT2], writes=[b_uf])
                t0 = 1024 * u + 512 * tg
                em.op("sync", lambda e, b=b, t0=t0: e.dma_start(
                    out=xt[:, :], in_=d_x2[b].rearrange("(c p) t -> p c t", p=128)[:, :, t0:t0 + 512]),
                    reads=[bd_in], writes=[b_xt], q="qs")
                emit_ln(em, C, S, psA2[0:1, :], psB2[0:1, :], psM2[:, :], psR3[:, :],
                        b_psA2, b_psB2, b_psM2, b_psR3,
                        src_ap=lambda c: u_f[:, c * 512:(c + 1) * 512], b_src=b_uf,
                        out_ap=None, b_out=None,
                        out32_ap=lambda c: xt[:, c * 512:(c + 1) * 512], b_out32=b_xt,
                        res_ap=lambda c: xt[:, c * 512:(c + 1) * 512], b_res=b_xt)
                em.op("sync", lambda e, b=b, t0=t0: e.dma_start(
                    out=d_x3[b].rearrange("(c p) t -> p c t", p=128)[:, :, t0:t0 + 512],
                    in_=xt[:, :].rearrange("p (c t) -> p c t", c=8)),
                    reads=[b_xt], writes=[bd_out], q="qs")
        ctxW.close()

    ctx.close()
    if F is None:
        em.block()
        qs_final = em.cnt["qs"]
        em.op("sync", lambda e: e.nop(), extra=[("qs", qs_final)], sig=False)
        em.close()
        octx.close()
        return nc
    return None


def p3_inputs(inp):
    wq = np.ascontiguousarray(np.asarray(inp["L1_attn_Wq"], np.float32).T)
    wk = np.ascontiguousarray(np.asarray(inp["L1_attn_Wk"], np.float32).T)
    wv = np.ascontiguousarray(np.asarray(inp["L1_attn_Wv"], np.float32).T)
    wr = np.asarray(inp["L1_attn_Wr"], np.float32)
    wrp = np.ascontiguousarray(wr.T.reshape(64, 16, 1024).transpose(1, 0, 2))
    return {"wqT": wq, "wkT": wk, "wvT": wv, "wrp": wrp}


def run_p3(x2_fm_cores, winp, debug=False):
    nc = build_p3(debug=debug)
    in_maps = []
    for c in range(NCORES):
        m = dict(winp)
        m["x2"] = x2_fm_cores[c]
        in_maps.append(m)
    res = run_bass_kernel_spmd(nc, in_maps, core_ids=list(range(NCORES)))
    if debug:
        return res.results
    return [r["x3"] for r in res.results]



def build_d1(F=None):
    if F is None:
        nc = bass.Bass("TRN2", target_bir_lowering=False)
        octx = ExitStack()
        em = Em(nc, octx)
        C = Consts(nc, em, octx)
        S = LnScratch(nc, octx)
        d_x3 = nc.dram_tensor("x3", [BPC, D, T], F32, kind="ExternalInput").ap()
        d_lg = nc.dram_tensor("lg", [NTOK, 8], F32, kind="ExternalOutput").ap()
        bd_in = Buf("bd_in"); bd_out = Buf("bd_out"); bd_lg = Buf("bd_lg")
    else:
        nc, em, C, S, streams, dbufs = F
        d_x3 = streams["x3"]; d_lg = streams["lg"]
        bd_in, bd_out = dbufs["x3"], dbufs["lg"]
    ctx = ExitStack()
    d_wr = nc.dram_tensor("wrc", [128, 8, 8], F32, kind="ExternalInput").ap()
    d_idf = nc.inline_tensor(np.eye(8, dtype=np.float32), "cd1_id8f").ap()

    xt = ctx.enter_context(nc.sbuf_tensor("d1_xt", [128, 4096], F32))
    xn = ctx.enter_context(nc.sbuf_tensor("d1_xn", [128, 4096], BF16))
    xnlo = ctx.enter_context(nc.sbuf_tensor("d1_xnlo", [128, 4096], BF16))
    wr_f = ctx.enter_context(nc.sbuf_tensor("d1_wr_f", [128, 64], F32))
    wr_hi = ctx.enter_context(nc.sbuf_tensor("d1_wr_hi", [128, 64], BF16))
    wr_lo = ctx.enter_context(nc.sbuf_tensor("d1_wr_lo", [128, 64], BF16))
    id8 = ctx.enter_context(nc.sbuf_tensor("d1_id8", [8, 8], F32))
    lgo = ctx.enter_context(nc.sbuf_tensor("d1_lgo", [128, 8], F32))
    b_xt = Buf("xt"); b_xn = Buf("xn"); b_xnlo = Buf("xnlo")
    b_wrf = Buf("wr_f"); b_wrh = Buf("wr_hi"); b_wrl = Buf("wr_lo")
    b_id8 = Buf("id8"); b_lgo = Buf("lgo")

    ps1 = ctx.enter_context(nc.psum_tensor("d1_ps1", [128, 512], F32))
    ps2 = ctx.enter_context(nc.psum_tensor("d1_ps2", [128, 512], F32))
    ps3 = ctx.enter_context(nc.psum_tensor("d1_ps3", [128, 512], F32))
    ps4 = ctx.enter_context(nc.psum_tensor("d1_ps4", [128, 512], F32))
    b_ps1 = Buf("ps1"); b_ps2 = Buf("ps2"); b_ps3 = Buf("ps3"); b_ps4 = Buf("ps4")

    em.block()
    C.load(em)
    em.op("sync", lambda e: e.dma_start(out=wr_f[:, :], in_=d_wr.rearrange("p c e -> p (c e)")),
          writes=[b_wrf], q="qs")
    em.op("sync", lambda e: e.dma_start(out=id8[:, :], in_=d_idf[:, :]), writes=[b_id8], q="qs")
    em.op("vector", lambda e: e.tensor_copy(wr_hi[:, :], wr_f[:, :]), reads=[b_wrf], writes=[b_wrh])
    em.op("vector", lambda e: e.tensor_tensor(wr_lo[:, :], wr_f[:, :], wr_hi[:, :], ALU.subtract),
          reads=[b_wrf, b_wrh], writes=[b_wrl])

    for b in range(BPC):
        for ti in range(4):
            t0 = ti * 512
            em.op("sync", lambda e, b=b, t0=t0: e.dma_start(
                out=xt[:, :], in_=d_x3[b].rearrange("(c p) t -> p c t", p=128)[:, :, t0:t0 + 512]),
                reads=[bd_in], writes=[b_xt], q="qs")
            emit_ln(em, C, S, ps1[0:1, :], ps2[0:1, :], ps3[:, :], ps4[:, :],
                    b_ps1, b_ps2, b_ps3, b_ps4,
                    src_ap=lambda c: xt[:, c * 512:(c + 1) * 512], b_src=b_xt,
                    out_ap=lambda c: xn[:, c * 512:(c + 1) * 512], b_out=b_xn,
                    lo_ap=lambda c: xnlo[:, c * 512:(c + 1) * 512], b_lo=b_xnlo)
            groups = [(wr_hi, b_wrh, True), (wr_lo, b_wrl, True), (wr_hi, b_wrh, False)]
            for gi, (g_sb, b_g, x_hi) in enumerate(groups):
                for kc in range(8):
                    src = xn[:, kc * 512:(kc + 1) * 512] if x_hi else xnlo[:, kc * 512:(kc + 1) * 512]
                    em.op("tensor", lambda e, kc=kc, g_sb=g_sb, src=src, gi=gi: e.matmul(
                        ps3[0:8, :], g_sb[:, kc * 8:(kc + 1) * 8], src,
                        start=(gi == 0 and kc == 0), stop=(gi == 2 and kc == 7),
                        skip_group_check=True),
                        reads=[b_g, b_xn, b_xnlo], writes=[b_ps3],
                        sig=(gi == 2 and kc == 7))
            em.op("vector", lambda e: e.tensor_copy(S.t1[0:8, :], ps3[0:8, :]),
                  reads=[b_ps3], writes=[S.b_t1])
            for g in range(4):
                em.op("tensor", lambda e, g=g: e.transpose(
                    ps4[0:128, 0:8], S.t1[0:8, g * 128:(g + 1) * 128], id8[:, :]),
                    reads=[S.b_t1, b_id8], writes=[b_ps4])
                em.op("vector", lambda e: e.tensor_copy(lgo[:, :], ps4[0:128, 0:8]),
                      reads=[b_ps4], writes=[b_lgo])
                em.op("sync", lambda e, b=b, t0=t0, g=g: e.dma_start(
                    out=d_lg[b * T + t0 + g * 128: b * T + t0 + (g + 1) * 128, :], in_=lgo[:, :]),
                    reads=[b_lgo], writes=[bd_out], q="qs")

    ctx.close()
    if F is None:
        em.block()
        qs_final = em.cnt["qs"]
        em.op("sync", lambda e: e.nop(), extra=[("qs", qs_final)], sig=False)
        em.close()
        octx.close()
        return nc
    return None



CAP = 4096
RMS_EPS = 1e-6

def build_d2(debug=False, F=None):
    NTOT = 32 * T
    NGRP = NTOT // 128
    NOWN = NTOK // 128
    if F is None:
        nc = bass.Bass("TRN2", target_bir_lowering=False)
        octx = ExitStack()
        em = Em(nc, octx)
        C = Consts(nc, em, octx)
        S = LnScratch(nc, octx)
        d_x3 = nc.dram_tensor("x3", [BPC, D, T], F32, kind="ExternalInput").ap()
        d_x4 = nc.dram_tensor("x4", [BPC, D, T], F32, kind="ExternalOutput").ap()
        d_lgown = nc.dram_tensor("lgown", [NTOK, 8], F32, kind="ExternalInput").ap()
        bd_in = Buf("bd_in"); bd_out = Buf("bd_out"); bd_lg = Buf("bd_lg")
        d_lgall = nc.dram_tensor("lgall", [NTOT, 8], F32, kind="ExternalInput").ap()
        d_gsel = nc.dram_tensor("gsel", [128, NGRP], F32, kind="ExternalInput").ap()
    else:
        nc, em, C, S, streams, dbufs = F
        d_x3 = streams["x3"]; d_x4 = streams["x4"]; d_lgown = streams["lg"]
        bd_in, bd_out, bd_lg = dbufs["x3"], dbufs["x4"], dbufs["lg"]
        d_gselc = nc.dram_tensor("gselc", [128, 32], F32, kind="ExternalInput").ap()
        h_totin = nc.dram_tensor("totin", [1, 512], F32, kind="Internal")
        h_totout = nc.dram_tensor("totout", [8, 512], F32, kind="Internal")
        d_totin = h_totin.ap(); d_totout = h_totout.ap()
    ctx = ExitStack()
    d_nw = nc.dram_tensor("nwc", [8, 128, 8], F32, kind="ExternalInput").ap()
    d_wg = nc.dram_tensor("wg", [8, D, 2048], F32, kind="ExternalInput").ap()
    d_wu = nc.dram_tensor("wu", [8, D, 2048], F32, kind="ExternalInput").ap()
    d_wd = nc.dram_tensor("wd", [8, 2048, D], F32, kind="ExternalInput").ap()
    d_fn = nc.dram_tensor("fnc", [128, 8], F32, kind="ExternalInput").ap()
    d_fwg = nc.dram_tensor("fwg", [D, 512], F32, kind="ExternalInput").ap()
    d_fwu = nc.dram_tensor("fwu", [D, 512], F32, kind="ExternalInput").ap()
    d_fwd = nc.dram_tensor("fwd", [512, D], F32, kind="ExternalInput").ap()

    iota_np = np.broadcast_to(np.arange(8, dtype=np.float32)[None, :], (128, 8)).copy()
    d_iota = nc.inline_tensor(iota_np, "cd2_iota").ap()
    d_iotab = nc.inline_tensor(iota_np + 1e6, "cd2_iotab").ap()
    d_triu = nc.inline_tensor(np.triu(np.ones((128, 128), dtype=BF), 1), "cd2_triu").ap()
    d_idb = nc.inline_tensor(np.eye(128, dtype=BF), "cd2_identbf").ap()
    sel_np = np.zeros((9, 9 * 128), dtype=BF)
    for r_ in range(9):
        sel_np[r_, r_ * 128:(r_ + 1) * 128] = 1
    d_sel = nc.inline_tensor(sel_np, "cd2_sel9").ap()
    if debug:
        d_cedbg = nc.dram_tensor("cedbg", [9, NTOK], F32, kind="ExternalOutput").ap()

    xt = ctx.enter_context(nc.sbuf_tensor("d2_xt", [128, 4096], F32))
    xn = ctx.enter_context(nc.sbuf_tensor("d2_xn", [128, 4096], BF16))
    y = ctx.enter_context(nc.sbuf_tensor("d2_y", [128, 4096], F32))
    xnr = ctx.enter_context(nc.sbuf_tensor("d2_xnr", [128, 4096], BF16))
    gu = ctx.enter_context(nc.sbuf_tensor("d2_gu", [128, 8192], BF16))
    hE = ctx.enter_context(nc.sbuf_tensor("d2_hE", [128, 4096], BF16))
    rW = Roller(nc, ctx, "d2_wbuf", [128, 8192], BF16, 4)
    rnw = Roller(nc, ctx, "d2_nw", [128, 8], F32, 2)
    fn_sb = ctx.enter_context(nc.sbuf_tensor("d2_fn_sb", [128, 8], F32))
    iota_sb = ctx.enter_context(nc.sbuf_tensor("d2_iota_sb", [128, 8], F32))
    iotab_sb = ctx.enter_context(nc.sbuf_tensor("d2_iotab_sb", [128, 8], F32))
    triu_sb = ctx.enter_context(nc.sbuf_tensor("d2_triu_sb", [128, 128], BF16))
    idb_sb = ctx.enter_context(nc.sbuf_tensor("d2_idb_sb", [128, 128], BF16))
    sel_sb = ctx.enter_context(nc.sbuf_tensor("d2_sel_sb", [9, 9 * 128], BF16))
    gsel_sb = ctx.enter_context(nc.sbuf_tensor("d2_gsel_sb", [128, NGRP], F32)) if F is None else None
    lg_sb = ctx.enter_context(nc.sbuf_tensor("d2_lg_sb", [128, 8], F32))
    rt = ctx.enter_context(nc.sbuf_tensor("d2_rt", [128, 80], F32))
    offb = ctx.enter_context(nc.sbuf_tensor("d2_offb", [128, 8], F32))
    totb = ctx.enter_context(nc.sbuf_tensor("d2_totb", [1, 8], BF16))
    ohcb = ctx.enter_context(nc.sbuf_tensor("d2_ohcb", [128, 8], BF16))
    stk = ctx.enter_context(nc.sbuf_tensor("d2_stk", [128, 9], BF16))
    ceT = ctx.enter_context(nc.sbuf_tensor("d2_ceT", [9, NTOK], BF16))
    tsc = ctx.enter_context(nc.sbuf_tensor("d2_tsc", [128, 512], F32))
    tsc2 = ctx.enter_context(nc.sbuf_tensor("d2_tsc2", [128, 512], F32))

    b_xt = Buf("xt"); b_xn = Buf("xn"); b_y = Buf("y"); b_xnr = Buf("xnr")
    b_gu = Buf("gu"); b_hE = Buf("hE")
    b_fn = Buf("fn"); b_iota = Buf("iota"); b_iotab = Buf("iotab")
    b_triu = Buf("triu"); b_idb = Buf("idb"); b_sel = Buf("sel"); b_gsel = Buf("gsel")
    b_lg = Buf("lg"); b_rt = Buf("rt"); b_offb = Buf("offb"); b_totb = Buf("totb")
    b_ohcb = Buf("ohcb"); b_stk = Buf("stk"); b_ceT = Buf("ceT")
    b_tsc = Buf("tsc"); b_tsc2 = Buf("tsc2")

    ps1 = ctx.enter_context(nc.psum_tensor("d2_ps1", [128, 512], F32))
    ps2 = ctx.enter_context(nc.psum_tensor("d2_ps2", [128, 512], F32))
    ps3 = ctx.enter_context(nc.psum_tensor("d2_ps3", [128, 512], F32))
    ps4 = ctx.enter_context(nc.psum_tensor("d2_ps4", [128, 512], F32))
    ps5 = ctx.enter_context(nc.psum_tensor("d2_ps5", [128, 512], F32))
    ps6 = ctx.enter_context(nc.psum_tensor("d2_ps6", [128, 1024], BF16))
    b_ps1 = Buf("ps1"); b_ps2 = Buf("ps2"); b_ps3 = Buf("ps3")
    b_ps4 = Buf("ps4"); b_ps5 = Buf("ps5"); b_ps6 = Buf("ps6")

    em.block()
    C.load(em)
    for sb, dt_, bf in ((iota_sb, d_iota, b_iota), (iotab_sb, d_iotab, b_iotab),
                        (triu_sb, d_triu, b_triu), (idb_sb, d_idb, b_idb),
                        (sel_sb, d_sel, b_sel), (fn_sb, d_fn, b_fn)):
        em.op("sync", lambda e, sb=sb, dt_=dt_: e.dma_start(out=sb[:, :], in_=dt_[:, :]),
              writes=[bf], q="qs")
    if F is None:
        em.op("sync", lambda e: e.dma_start(out=gsel_sb[:, :], in_=d_gsel[:, :]), writes=[b_gsel], q="qs")
    em.op("vector", lambda e: e.memset(offb[:, :], 0.0), writes=[b_offb])

    r = rt
    nmx = r[:, 0:1]; ssum = r[:, 1:2]; rs = r[:, 2:3]
    m1 = r[:, 3:4]; idx1 = r[:, 4:5]; m2 = r[:, 5:6]; idx2 = r[:, 6:7]
    probs = r[:, 8:16]; eq = r[:, 16:24]; u1 = r[:, 24:32]
    oh1 = r[:, 32:40]; oh2 = r[:, 40:48]; p2 = r[:, 48:56]
    ohc = r[:, 56:64]; rank1 = r[:, 64:65]; rank2 = r[:, 65:66]
    keep1 = r[:, 66:67]; keep2 = r[:, 67:68]; ce = r[:, 68:76]; ov = r[:, 76:77]

    def emit_route(src_lg):
        em.op("vector", lambda e: e.reduce_max(nmx, src_lg, axis=AX.X, negate=True),
              reads=[b_lg], writes=[b_rt])
        em.op("scalar", lambda e: e.activation(probs, src_lg, AF.Exp, bias=nmx, accum_out=ssum),
              reads=[b_lg, b_rt], writes=[b_rt])
        em.op("vector", lambda e: e.reciprocal(rs, ssum), reads=[b_rt], writes=[b_rt])
        em.op("vector", lambda e: e.tensor_scalar(probs, probs, rs, None, ALU.mult),
              reads=[b_rt], writes=[b_rt])
        em.op("vector", lambda e: e.reduce_max(m1, probs, axis=AX.X), reads=[b_rt], writes=[b_rt])
        em.op("vector", lambda e: e.tensor_scalar(eq, probs, m1, None, ALU.is_ge),
              reads=[b_rt], writes=[b_rt])
        em.op("vector", lambda e: e.scalar_tensor_tensor(u1, eq, -1e6, iotab_sb[:, :], ALU.mult, ALU.add),
              reads=[b_rt, b_iotab], writes=[b_rt])
        em.op("vector", lambda e: e.tensor_reduce(idx1, u1, AX.X, ALU.min), reads=[b_rt], writes=[b_rt])
        em.op("vector", lambda e: e.tensor_scalar(oh1, iota_sb[:, :], idx1, None, ALU.is_equal),
              reads=[b_rt, b_iota], writes=[b_rt])
        em.op("vector", lambda e: e.scalar_tensor_tensor(p2, oh1, -1e6, probs, ALU.mult, ALU.add),
              reads=[b_rt], writes=[b_rt])
        em.op("vector", lambda e: e.reduce_max(m2, p2, axis=AX.X), reads=[b_rt], writes=[b_rt])
        em.op("vector", lambda e: e.tensor_scalar(eq, p2, m2, None, ALU.is_ge),
              reads=[b_rt], writes=[b_rt])
        em.op("vector", lambda e: e.scalar_tensor_tensor(u1, eq, -1e6, iotab_sb[:, :], ALU.mult, ALU.add),
              reads=[b_rt, b_iotab], writes=[b_rt])
        em.op("vector", lambda e: e.tensor_reduce(idx2, u1, AX.X, ALU.min), reads=[b_rt], writes=[b_rt])
        em.op("vector", lambda e: e.tensor_scalar(oh2, iota_sb[:, :], idx2, None, ALU.is_equal),
              reads=[b_rt, b_iota], writes=[b_rt])
        em.op("vector", lambda e: e.tensor_tensor(ohc, oh1, oh2, ALU.add), reads=[b_rt], writes=[b_rt])
        em.op("vector", lambda e: e.tensor_copy(ohcb[:, :], ohc), reads=[b_rt], writes=[b_ohcb])

    def emit_tot_bcast():
        em.op("tensor", lambda e: e.matmul(ps3[0:1, 0:8], C.ones128[:, :], ohcb[:, :],
                                           start=True, stop=True),
              reads=[C.b_ones128, b_ohcb], writes=[b_ps3])
        em.op("vector", lambda e: e.tensor_copy(totb[:, :], ps3[0:1, 0:8]),
              reads=[b_ps3], writes=[b_totb])
        em.op("tensor", lambda e: e.matmul(ps4[0:128, 0:8], C.ones1x128[:, :], totb[:, :],
                                           start=True, stop=True),
              reads=[C.b_ones1x128, b_totb], writes=[b_ps4])

    if F is None:
        for g in range(NGRP):
            em.op("sync", lambda e, g=g: e.dma_start(
                out=lg_sb[:, :], in_=d_lgall[g * 128:(g + 1) * 128, :]),
                writes=[b_lg], q="qs")
            emit_route(lg_sb[:, :])
            emit_tot_bcast()
            em.op("vector", lambda e, g=g: e.scalar_tensor_tensor(
                offb[:, :], ps4[0:128, 0:8], gsel_sb[:, g:g + 1], offb[:, :], ALU.mult, ALU.add),
                reads=[b_ps4, b_gsel, b_offb], writes=[b_offb])
    else:
        totrow = ctx.enter_context(nc.sbuf_tensor("d2_totrow", [1, 512], F32))
        totg = ctx.enter_context(nc.sbuf_tensor("d2_totg", [128, 32], F32))
        totgb = ctx.enter_context(nc.sbuf_tensor("d2_totgb", [128, 32], BF16))
        gselc_sb = ctx.enter_context(nc.sbuf_tensor("d2_gselc", [128, 32], F32))
        baser = ctx.enter_context(nc.sbuf_tensor("d2_baser", [1, 48], F32))
        baseb = ctx.enter_context(nc.sbuf_tensor("d2_baseb", [1, 16], BF16))
        b_totrow = Buf("totrow"); b_totg = Buf("totg"); b_totgb = Buf("totgb")
        b_gselc = Buf("gselc"); b_baser = Buf("baser"); b_baseb = Buf("baseb")
        b_totind = Buf("totin_d"); b_totoutd = Buf("totout_d")
        em.op("sync", lambda e: e.dma_start(out=gselc_sb[:, :], in_=d_gselc[:, :]),
              writes=[b_gselc], q="qs")
        for g in range(NOWN):
            em.op("sync", lambda e, g=g: e.dma_start(
                out=lg_sb[:, :], in_=d_lgown[g * 128:(g + 1) * 128, :]),
                reads=[bd_lg], writes=[b_lg], q="qs")
            emit_route(lg_sb[:, :])
            em.op("tensor", lambda e: e.matmul(ps3[0:1, 0:8], C.ones128[:, :], ohcb[:, :],
                                               start=True, stop=True),
                  reads=[C.b_ones128, b_ohcb], writes=[b_ps3])
            em.op("vector", lambda e, g=g: e.tensor_copy(totrow[:, g * 8:(g + 1) * 8], ps3[0:1, 0:8]),
                  reads=[b_ps3], writes=[b_totrow])
        em.op("sync", lambda e: e.dma_start(out=d_totin[:, :], in_=totrow[:, :]),
              reads=[b_totrow], writes=[b_totind], q="qs")
        em.op("gpsimd", lambda e: e.collective_compute(
            "AllGather", ALU.bypass, replica_groups=[list(range(NCORES))],
            ins=[h_totin.ap().opt()], outs=[h_totout.ap().opt()]),
            reads=[b_totind], writes=[b_totoutd])
        em.op("sync", lambda e: e.dma_start(
            out=totg[:, :].rearrange("p (ci e) -> p ci e", e=8),
            in_=d_totout.rearrange("r (g e) -> (r g) e", e=8).rearrange("(ci p) e -> p ci e", p=128)),
            reads=[b_totoutd], writes=[b_totg], q="qs")
        em.op("vector", lambda e: e.tensor_tensor(totg[:, :], totg[:, :], gselc_sb[:, :], ALU.mult),
              reads=[b_totg, b_gselc], writes=[b_totg])
        em.op("vector", lambda e: e.tensor_copy(totgb[:, :], totg[:, :]),
              reads=[b_totg], writes=[b_totgb])
        em.op("tensor", lambda e: e.matmul(ps3[0:1, 0:32], C.ones128[:, :], totgb[:, :],
                                           start=True, stop=True),
              reads=[C.b_ones128, b_totgb], writes=[b_ps3])
        em.op("vector", lambda e: e.tensor_copy(baser[:, 0:32], ps3[0:1, 0:32]),
              reads=[b_ps3], writes=[b_baser])
        em.op("vector", lambda e: e.tensor_tensor(baser[:, 32:40], baser[:, 0:8], baser[:, 8:16], ALU.add),
              reads=[b_baser], writes=[b_baser])
        em.op("vector", lambda e: e.tensor_tensor(baser[:, 40:48], baser[:, 16:24], baser[:, 24:32], ALU.add),
              reads=[b_baser], writes=[b_baser])
        em.op("vector", lambda e: e.tensor_tensor(baser[:, 32:40], baser[:, 32:40], baser[:, 40:48], ALU.add),
              reads=[b_baser], writes=[b_baser])
        em.op("vector", lambda e: e.tensor_copy(baseb[:, 0:8], baser[:, 32:40]),
              reads=[b_baser], writes=[b_baseb])
        em.op("vector", lambda e: e.tensor_tensor(baser[:, 40:48], baser[:, 32:40], baseb[:, 0:8], ALU.subtract),
              reads=[b_baser, b_baseb], writes=[b_baser])
        em.op("vector", lambda e: e.tensor_copy(baseb[:, 8:16], baser[:, 40:48]),
              reads=[b_baser], writes=[b_baseb])
        em.op("tensor", lambda e: e.matmul(ps4[0:128, 0:8], C.ones1x128[:, :], baseb[:, 0:8],
                                           start=True, stop=False, skip_group_check=True),
              reads=[C.b_ones1x128, b_baseb], writes=[b_ps4], sig=False)
        em.op("tensor", lambda e: e.matmul(ps4[0:128, 0:8], C.ones1x128[:, :], baseb[:, 8:16],
                                           start=False, stop=True, skip_group_check=True),
              reads=[C.b_ones1x128, b_baseb], writes=[b_ps4])
        em.op("vector", lambda e: e.tensor_copy(offb[:, :], ps4[0:128, 0:8]),
              reads=[b_ps4], writes=[b_offb])

    for g in range(NOWN):
        em.op("sync", lambda e, g=g: e.dma_start(
            out=lg_sb[:, :], in_=d_lgown[g * 128:(g + 1) * 128, :]),
            reads=[bd_lg], writes=[b_lg], q="qs")
        emit_route(lg_sb[:, :])
        em.op("tensor", lambda e: e.matmul(ps5[0:128, 0:8], triu_sb[:, :], ohcb[:, :],
                                           start=True, stop=True),
              reads=[b_triu, b_ohcb], writes=[b_ps5])
        em.op("vector", lambda e: e.tensor_tensor(r[:, 8:16] if False else probs, ps5[0:128, 0:8], offb[:, :], ALU.add),
              reads=[b_ps5, b_offb], writes=[b_rt])
        ranks = probs
        em.op("vector", lambda e: e.tensor_tensor(u1, oh1, ranks, ALU.mult), reads=[b_rt], writes=[b_rt])
        em.op("vector", lambda e: e.reduce_sum(rank1, u1, axis=AX.X), reads=[b_rt], writes=[b_rt])
        em.op("vector", lambda e: e.tensor_tensor(u1, oh2, ranks, ALU.mult), reads=[b_rt], writes=[b_rt])
        em.op("vector", lambda e: e.reduce_sum(rank2, u1, axis=AX.X), reads=[b_rt], writes=[b_rt])
        em.op("vector", lambda e: e.tensor_single_scalar(keep1, rank1, float(CAP), ALU.is_lt),
              reads=[b_rt], writes=[b_rt])
        em.op("vector", lambda e: e.tensor_single_scalar(keep2, rank2, float(CAP), ALU.is_lt),
              reads=[b_rt], writes=[b_rt])
        em.op("vector", lambda e: e.tensor_tensor(rank1, keep1, m1, ALU.mult), reads=[b_rt], writes=[b_rt])
        em.op("vector", lambda e: e.tensor_tensor(rank2, keep2, m2, ALU.mult), reads=[b_rt], writes=[b_rt])
        em.op("vector", lambda e: e.tensor_scalar(ce, oh1, rank1, None, ALU.mult),
              reads=[b_rt], writes=[b_rt])
        em.op("vector", lambda e: e.tensor_scalar(u1, oh2, rank2, None, ALU.mult),
              reads=[b_rt], writes=[b_rt])
        em.op("vector", lambda e: e.tensor_tensor(ce, ce, u1, ALU.add), reads=[b_rt], writes=[b_rt])
        em.op("vector", lambda e: e.tensor_tensor(ov, keep1, keep2, ALU.mult), reads=[b_rt], writes=[b_rt])
        em.op("vector", lambda e: e.tensor_scalar(ov, ov, -1.0, 1.0, ALU.mult, ALU.add),
              reads=[b_rt], writes=[b_rt])
        em.op("vector", lambda e: e.tensor_copy(stk[:, 0:8], ce), reads=[b_rt], writes=[b_stk])
        em.op("vector", lambda e: e.tensor_copy(stk[:, 8:9], ov), reads=[b_rt], writes=[b_stk])
        em.op("tensor", lambda e: e.transpose(ps6[0:9, 0:128], stk[:, :], idb_sb[:, :]),
              reads=[b_stk, b_idb], writes=[b_ps6])
        em.op("vector", lambda e, g=g: e.tensor_copy(ceT[:, g * 128:(g + 1) * 128], ps6[0:9, 0:128]),
              reads=[b_ps6], writes=[b_ceT])
        emit_tot_bcast()
        em.op("vector", lambda e: e.tensor_tensor(offb[:, :], offb[:, :], ps4[0:128, 0:8], ALU.add),
              reads=[b_ps4, b_offb], writes=[b_offb])

    if debug:
        for j in range(2):
            em.op("vector", lambda e, j=j: e.tensor_copy(S.t1[0:9, :], ceT[:, j * 512:(j + 1) * 512]),
                  reads=[b_ceT], writes=[S.b_t1])
            em.op("sync", lambda e, j=j: e.dma_start(out=d_cedbg[:, j * 512:(j + 1) * 512], in_=S.t1[0:9, :]),
                  reads=[S.b_t1], q="qs")

    em.block()
    for sti in range(16):
        b0 = (sti * 512) // T
        t0 = (sti * 512) % T
        co = sti * 512
        em.op("sync", lambda e, b0=b0, t0=t0: e.dma_start(
            out=xt[:, :], in_=d_x3[b0].rearrange("(c p) t -> p c t", p=128)[:, :, t0:t0 + 512]),
            reads=[bd_in], writes=[b_xt], q="qs")
        emit_ln(em, C, S, ps1[0:1, :], ps2[0:1, :], ps3[:, :], ps4[:, :],
                b_ps1, b_ps2, b_ps3, b_ps4,
                src_ap=lambda c: xt[:, c * 512:(c + 1) * 512], b_src=b_xt,
                out_ap=lambda c: xn[:, c * 512:(c + 1) * 512], b_out=b_xn,
                out32_ap=lambda c: y[:, c * 512:(c + 1) * 512], b_out32=b_y)
        for c in range(8):
            em.op("scalar", lambda e, c=c: e.activation(S.sq[:, c * 512:(c + 1) * 512],
                                                        xn[:, c * 512:(c + 1) * 512], AF.Square),
                  reads=[b_xn], writes=[S.b_sq])
        for c in range(8):
            em.op("tensor", lambda e, c=c: e.matmul(ps1[0:1, :], C.ones128[:, :],
                                                    S.sq[:, c * 512:(c + 1) * 512],
                                                    start=(c == 0), stop=(c == 7)),
                  reads=[S.b_sq, C.b_ones128], writes=[b_ps1], sig=(c == 7))
        st_ = S.stat
        em.op("vector", lambda e: e.tensor_scalar(st_[:, 0:512], ps1[0:1, :], 1.0 / D, float(RMS_EPS),
                                                  ALU.mult, ALU.add),
              reads=[b_ps1], writes=[S.b_stat])
        em.op("scalar", lambda e: e.activation(st_[:, 512:1024], st_[:, 0:512], AF.Sqrt),
              reads=[S.b_stat], writes=[S.b_stat])
        em.op("vector", lambda e: e.reciprocal(st_[:, 1024:1536], st_[:, 512:1024]),
              reads=[S.b_stat], writes=[S.b_stat])
        em.op("vector", lambda e: e.tensor_copy(S.statb[:, 0:512], st_[:, 1024:1536]),
              reads=[S.b_stat], writes=[S.b_statb])
        em.op("tensor", lambda e: e.matmul(ps4[:, :], C.ones1x128[:, :], S.statb[:, 0:512],
                                           start=True, stop=True),
              reads=[S.b_statb, C.b_ones1x128], writes=[b_ps4])
        for c in range(8):
            em.op("vector", lambda e, c=c: e.tensor_tensor(
                xnr[:, c * 512:(c + 1) * 512], xn[:, c * 512:(c + 1) * 512], ps4[:, :], ALU.mult),
                reads=[b_xn, b_ps4], writes=[b_xnr])
        for e_i in range(8):
            nw_sb, b_nw = rnw.next()
            em.op("sync", lambda e, e_i=e_i, nw_sb=nw_sb: e.dma_start(out=nw_sb[:, :], in_=d_nw[e_i]),
                  writes=[b_nw], q="qs")
            wg0, b_wg0 = rW.next(); wu0, b_wu0 = rW.next()
            wg1, b_wg1 = rW.next(); wu1, b_wu1 = rW.next()
            gus = ((wg0, b_wg0, wu0, b_wu0), (wg1, b_wg1, wu1, b_wu1))
            for Fh in range(2):
                wg, b_wg, wu, b_wu = gus[Fh]
                em.op("gpsimd", lambda e, e_i=e_i, Fh=Fh, wg=wg: e.dma_start(
                    out=wg[:, :], in_=d_wg[e_i].rearrange("(kc p) n -> p kc n", p=128)[:, :, Fh * 1024:(Fh + 1) * 1024]),
                    writes=[b_wg], q="qg")
                em.op("gpsimd", lambda e, e_i=e_i, Fh=Fh, wu=wu: e.dma_start(
                    out=wu[:, :], in_=d_wu[e_i].rearrange("(kc p) n -> p kc n", p=128)[:, :, Fh * 1024:(Fh + 1) * 1024]),
                    writes=[b_wu], q="qg")
            for c in range(8):
                em.op("vector", lambda e, c=c, nw_sb=nw_sb: e.tensor_scalar(
                    hE[:, c * 512:(c + 1) * 512], xnr[:, c * 512:(c + 1) * 512],
                    nw_sb[:, c:c + 1], None, ALU.mult),
                    reads=[b_xnr, b_nw], writes=[b_hE])
            wd0 = wd1 = b_wd0 = b_wd1 = None
            for Fh in range(2):
                wg, b_wg, wu, b_wu = gus[Fh]
                if Fh == 1:
                    wd0, b_wd0 = rW.next(); wd1, b_wd1 = rW.next()
                    em.op("gpsimd", lambda e, e_i=e_i, wd0=wd0: e.dma_start(
                        out=wd0[:, :], in_=d_wd[e_i].rearrange("(kc p) n -> p kc n", p=128)[:, 0:8, :]),
                        writes=[b_wd0], q="qg")
                    em.op("gpsimd", lambda e, e_i=e_i, wd1=wd1: e.dma_start(
                        out=wd1[:, :], in_=d_wd[e_i].rearrange("(kc p) n -> p kc n", p=128)[:, 8:16, :]),
                        writes=[b_wd1], q="qg")
                for fc in range(8):
                    psg, b_psg = (ps1, b_ps1) if fc % 2 == 0 else (ps3, b_ps3)
                    psu, b_psu = (ps2, b_ps2) if fc % 2 == 0 else (ps4, b_ps4)
                    tscx, b_tscx = (tsc[:, :], b_tsc) if fc % 2 == 0 else (tsc2[:, :], b_tsc2)
                    for kc in range(8):
                        em.op("tensor", lambda e, fc=fc, kc=kc, wg=wg, psg=psg: e.matmul(
                            psg[:, :], wg[:, kc * 1024 + fc * 128: kc * 1024 + (fc + 1) * 128],
                            hE[:, kc * 512:(kc + 1) * 512],
                            start=(kc == 0), stop=(kc == 7)),
                            reads=[b_wg, b_hE], writes=[b_psg], sig=(kc == 7))
                    em.op("scalar", lambda e, fc=fc, psg=psg, b_psg=b_psg, tscx=tscx, b_tscx=b_tscx: e.activation(
                        tscx, psg[:, :], AF.Gelu),
                        reads=[b_psg], writes=[b_tscx])
                    for kc in range(8):
                        em.op("tensor", lambda e, fc=fc, kc=kc, wu=wu, psu=psu: e.matmul(
                            psu[:, :], wu[:, kc * 1024 + fc * 128: kc * 1024 + (fc + 1) * 128],
                            hE[:, kc * 512:(kc + 1) * 512],
                            start=(kc == 0), stop=(kc == 7)),
                            reads=[b_wu, b_hE], writes=[b_psu], sig=(kc == 7))
                    em.op("vector", lambda e, fc=fc, Fh=Fh, psu=psu, b_psu=b_psu, tscx=tscx, b_tscx=b_tscx: e.tensor_tensor(
                        gu[:, (Fh * 8 + fc) * 512:(Fh * 8 + fc + 1) * 512],
                        psu[:, :], tscx, ALU.mult),
                        reads=[b_psu, b_tscx], writes=[b_gu])
            em.op("tensor", lambda e, e_i=e_i, co=co: e.matmul(
                ps5[:, :], sel_sb[:, e_i * 128:(e_i + 1) * 128], ceT[:, co:co + 512],
                start=True, stop=True),
                reads=[b_sel, b_ceT], writes=[b_ps5])
            for dd in range(8):
                for ic in range(16):
                    wb = wd0 if ic < 8 else wd1
                    b_wb = b_wd0 if ic < 8 else b_wd1
                    em.op("tensor", lambda e, dd=dd, ic=ic, wb=wb: e.matmul(
                        ps1[:, :], wb[:, (ic % 8) * 1024 + dd * 128:(ic % 8) * 1024 + (dd + 1) * 128],
                        gu[:, ic * 512:(ic + 1) * 512],
                        start=(ic == 0), stop=(ic == 15)),
                        reads=[b_wb, b_gu], writes=[b_ps1], sig=(ic == 15))
                em.op("vector", lambda e, dd=dd: e.tensor_tensor(
                    tsc[:, :], ps1[:, :], xn[:, dd * 512:(dd + 1) * 512], ALU.add),
                    reads=[b_ps1, b_xn], writes=[b_tsc])
                em.op("vector", lambda e: e.tensor_tensor(tsc[:, :], tsc[:, :], ps5[:, :], ALU.mult),
                      reads=[b_tsc, b_ps5], writes=[b_tsc])
                em.op("vector", lambda e, dd=dd: e.tensor_tensor(
                    y[:, dd * 512:(dd + 1) * 512], y[:, dd * 512:(dd + 1) * 512], tsc[:, :], ALU.add),
                    reads=[b_tsc], writes=[b_y])
        for c in range(8):
            em.op("vector", lambda e, c=c: e.tensor_scalar(
                hE[:, c * 512:(c + 1) * 512], xnr[:, c * 512:(c + 1) * 512],
                fn_sb[:, c:c + 1], None, ALU.mult),
                reads=[b_xnr, b_fn], writes=[b_hE])
        fwg_sb, b_fwg = rW.next(); fwu_sb, b_fwu = rW.next(); fwd_sb, b_fwd = rW.next()
        em.op("gpsimd", lambda e, fwg_sb=fwg_sb: e.dma_start(
            out=fwg_sb[:, 0:4096], in_=d_fwg.rearrange("(kc p) n -> p kc n", p=128)),
            writes=[b_fwg], q="qg")
        em.op("gpsimd", lambda e, fwu_sb=fwu_sb: e.dma_start(
            out=fwu_sb[:, 0:4096], in_=d_fwu.rearrange("(kc p) n -> p kc n", p=128)),
            writes=[b_fwu], q="qg")
        em.op("gpsimd", lambda e, fwd_sb=fwd_sb: e.dma_start(
            out=fwd_sb[:, 0:4096], in_=d_fwd.rearrange("(kc p) n -> p kc n", p=128)),
            writes=[b_fwd], q="qg")
        for fc in range(4):
            for kc in range(8):
                em.op("tensor", lambda e, fc=fc, kc=kc, fwg_sb=fwg_sb: e.matmul(
                    ps1[:, :], fwg_sb[:, kc * 512 + fc * 128: kc * 512 + (fc + 1) * 128],
                    hE[:, kc * 512:(kc + 1) * 512],
                    start=(kc == 0), stop=(kc == 7)),
                    reads=[b_fwg, b_hE], writes=[b_ps1], sig=(kc == 7))
            em.op("scalar", lambda e, fc=fc: e.activation(tsc[:, :], ps1[:, :], AF.Gelu),
                  reads=[b_ps1], writes=[b_tsc])
            for kc in range(8):
                em.op("tensor", lambda e, fc=fc, kc=kc, fwu_sb=fwu_sb: e.matmul(
                    ps2[:, :], fwu_sb[:, kc * 512 + fc * 128: kc * 512 + (fc + 1) * 128],
                    hE[:, kc * 512:(kc + 1) * 512],
                    start=(kc == 0), stop=(kc == 7)),
                    reads=[b_fwu, b_hE], writes=[b_ps2], sig=(kc == 7))
            em.op("vector", lambda e, fc=fc: e.tensor_tensor(
                gu[:, fc * 512:(fc + 1) * 512], ps2[:, :], tsc[:, :], ALU.mult),
                reads=[b_ps2, b_tsc], writes=[b_gu])
        em.op("tensor", lambda e, co=co: e.matmul(
            ps5[:, :], sel_sb[:, 8 * 128:9 * 128], ceT[:, co:co + 512],
            start=True, stop=True),
            reads=[b_sel, b_ceT], writes=[b_ps5])
        for dd in range(8):
            for ic in range(4):
                em.op("tensor", lambda e, dd=dd, ic=ic, fwd_sb=fwd_sb: e.matmul(
                    ps1[:, :], fwd_sb[:, ic * 1024 + dd * 128: ic * 1024 + (dd + 1) * 128],
                    gu[:, ic * 512:(ic + 1) * 512],
                    start=(ic == 0), stop=(ic == 3)),
                    reads=[b_fwd, b_gu], writes=[b_ps1], sig=(ic == 3))
            em.op("vector", lambda e, dd=dd: e.tensor_tensor(
                tsc[:, :], ps1[:, :], xn[:, dd * 512:(dd + 1) * 512], ALU.add),
                reads=[b_ps1, b_xn], writes=[b_tsc])
            em.op("vector", lambda e, dd=dd: e.tensor_tensor(
                tsc2[:, :], tsc[:, :], y[:, dd * 512:(dd + 1) * 512], ALU.subtract),
                reads=[b_tsc, b_y], writes=[b_tsc2])
            em.op("vector", lambda e: e.tensor_tensor(tsc2[:, :], tsc2[:, :], ps5[:, :], ALU.mult),
                  reads=[b_tsc2, b_ps5], writes=[b_tsc2])
            em.op("vector", lambda e, dd=dd: e.tensor_tensor(
                y[:, dd * 512:(dd + 1) * 512], y[:, dd * 512:(dd + 1) * 512], tsc2[:, :], ALU.add),
                reads=[b_tsc2], writes=[b_y])
        emit_ln(em, C, S, ps1[0:1, :], ps2[0:1, :], ps3[:, :], ps4[:, :],
                b_ps1, b_ps2, b_ps3, b_ps4,
                src_ap=lambda c: y[:, c * 512:(c + 1) * 512], b_src=b_y,
                out_ap=None, b_out=None,
                out32_ap=lambda c: xt[:, c * 512:(c + 1) * 512], b_out32=b_xt,
                res_ap=lambda c: xt[:, c * 512:(c + 1) * 512], b_res=b_xt)
        em.op("sync", lambda e, b0=b0, t0=t0: e.dma_start(
            out=d_x4[b0].rearrange("(c p) t -> p c t", p=128)[:, :, t0:t0 + 512],
            in_=xt[:, :].rearrange("p (c t) -> p c t", c=8)),
            reads=[b_xt], writes=[bd_out], q="qs")

    ctx.close()
    if F is None:
        em.block()
        qs_final = em.cnt["qs"]
        em.op("sync", lambda e: e.nop(), extra=[("qs", qs_final)], sig=False)
        em.close()
        octx.close()
        return nc
    return None


def d1_inputs(inp):
    wr = np.asarray(inp["L1_ffn_Wr"], np.float32)
    wrc = np.ascontiguousarray(wr.reshape(8, 128, 8).transpose(1, 0, 2))
    return {"wrc": wrc}


def d2_inputs(inp):
    nw = np.asarray(inp["L1_ffn_norm_w"], np.float32)
    nwc = np.ascontiguousarray(nw.reshape(8, 8, 128).transpose(0, 2, 1))
    fn = np.asarray(inp["L1_ffn_f_norm"], np.float32)
    fnc = np.ascontiguousarray(fn.reshape(8, 128).T)
    return {"nwc": nwc,
            "wg": np.ascontiguousarray(inp["L1_ffn_Wg"], np.float32),
            "wu": np.ascontiguousarray(inp["L1_ffn_Wu"], np.float32),
            "wd": np.ascontiguousarray(inp["L1_ffn_Wd"], np.float32),
            "fnc": fnc,
            "fwg": np.ascontiguousarray(inp["L1_ffn_fWg"], np.float32),
            "fwu": np.ascontiguousarray(inp["L1_ffn_fWu"], np.float32),
            "fwd": np.ascontiguousarray(inp["L1_ffn_fWd"], np.float32)}


def run_d1(x3_fm_cores, winp):
    nc = build_d1()
    in_maps = []
    for c in range(NCORES):
        m = dict(winp)
        m["x3"] = x3_fm_cores[c]
        in_maps.append(m)
    res = run_bass_kernel_spmd(nc, in_maps, core_ids=list(range(NCORES)))
    return [r["lg"] for r in res.results]


def run_d2(x3_fm_cores, lg_full, winp, debug=False):
    nc = build_d2(debug=debug)
    in_maps = []
    for c in range(NCORES):
        m = dict(winp)
        m["x3"] = x3_fm_cores[c]
        m["lgall"] = lg_full
        m["lgown"] = np.ascontiguousarray(lg_full[c * NTOK:(c + 1) * NTOK])
        gsel = np.zeros((1, lg_full.shape[0] // 128), np.float32)
        gsel[0, : c * (NTOK // 128)] = 1.0
        m["gsel"] = np.ascontiguousarray(np.broadcast_to(gsel, (128, gsel.shape[1])))
        in_maps.append(m)
    res = run_bass_kernel_spmd(nc, in_maps, core_ids=list(range(NCORES)))
    if debug:
        return res.results
    return [r["x4"] for r in res.results]



def _to_fm(x):
    out = []
    for c in range(NCORES):
        xc = x[c * BPC:(c + 1) * BPC]
        out.append(np.ascontiguousarray(xc.transpose(0, 2, 1), dtype=np.float32))
    return out


def _from_fm(parts):
    full = np.concatenate([p.transpose(0, 2, 1) for p in parts], axis=0)
    return np.ascontiguousarray(full)


def build_fused():
    nc = bass.Bass("TRN2", target_bir_lowering=False)
    octx = ExitStack()
    em = Em(nc, octx)
    C = Consts(nc, em, octx)
    S = LnScratch(nc, octx)
    streams = {
        "x": nc.dram_tensor("x", [BPC, D, T], F32, kind="ExternalInput").ap(),
        "x1": nc.dram_tensor("x1s", [BPC, D, T], F32, kind="Internal").ap(),
        "x2": nc.dram_tensor("x2s", [BPC, D, T], F32, kind="Internal").ap(),
        "x3": nc.dram_tensor("x3s", [BPC, D, T], F32, kind="Internal").ap(),
        "lg": nc.dram_tensor("lgs", [NTOK, 8], F32, kind="Internal").ap(),
        "x4": nc.dram_tensor("x4", [BPC, D, T], F32, kind="ExternalOutput").ap(),
    }
    dbufs = {k: Buf("d_" + k) for k in streams}
    F = (nc, em, C, S, streams, dbufs)

    def drain():
        for eng in ("sync", "gpsimd", "vector", "scalar", "tensor"):
            em.op(eng, lambda e: e.nop(),
                  extra=[("qs", em.cnt["qs"]), ("qg", em.cnt["qg"])], sig=False)

    build_p1(F=F)
    drain()
    build_p2(F=F)
    drain()
    build_p3(F=F)
    drain()
    build_d1(F=F)
    drain()
    build_d2(F=F)
    em.block()
    qs_final = em.cnt["qs"]
    em.op("sync", lambda e: e.nop(), extra=[("qs", qs_final)], sig=False)
    em.close()
    octx.close()
    return nc


def fused_inputs(inp):
    m = {}
    m["wq"] = np.ascontiguousarray(np.asarray(inp["L0_attn_Wq"], np.float32))
    m["wk"] = np.ascontiguousarray(np.asarray(inp["L0_attn_Wk"], np.float32))
    m["wv"] = np.ascontiguousarray(np.asarray(inp["L0_attn_Wv"], np.float32))
    m["wo"] = np.ascontiguousarray(np.asarray(inp["L0_attn_Wo"], np.float32))
    m.update(p2_inputs(inp))
    m.update(p3_inputs(inp))
    m.update(d1_inputs(inp))
    m.update(d2_inputs(inp))
    return m


def kernel(**inputs):
    inp = inputs
    x = np.ascontiguousarray(np.asarray(inp["L0_attn_x"], dtype=np.float32))
    x_fm = _to_fm(x)
    wm = fused_inputs(inp)
    nc = build_fused()
    in_maps = []
    for c in range(NCORES):
        m = dict(wm)
        m["x"] = x_fm[c]
        gsel = np.zeros((128, 32), np.float32)
        glob = (np.arange(32)[None, :] // 8) * 128 + np.arange(128)[:, None]
        gsel[:, :] = (glob < 64 * c)
        m["gselc"] = np.ascontiguousarray(gsel)
        in_maps.append(m)
    res = run_bass_kernel_spmd(nc, in_maps, core_ids=list(range(NCORES)))
    return _from_fm([r["x4"] for r in res.results])


def kernel_unfused(**inputs):
    inp = inputs
    x = np.ascontiguousarray(np.asarray(inp["L0_attn_x"], dtype=np.float32))
    x_fm = _to_fm(x)

    w1 = {"wq": np.ascontiguousarray(np.asarray(inp["L0_attn_Wq"], np.float32)),
          "wk": np.ascontiguousarray(np.asarray(inp["L0_attn_Wk"], np.float32)),
          "wv": np.ascontiguousarray(np.asarray(inp["L0_attn_Wv"], np.float32)),
          "wo": np.ascontiguousarray(np.asarray(inp["L0_attn_Wo"], np.float32))}
    x1_fm = run_p1(x_fm, w1)

    x2_fm = run_p2(x1_fm, p2_inputs(inp))
    x3_fm = run_p3(x2_fm, p3_inputs(inp))
    lg_parts = run_d1(x3_fm, d1_inputs(inp))
    lg_full = np.ascontiguousarray(np.concatenate(lg_parts, axis=0))
    x4_fm = run_d2(x3_fm, lg_full, d2_inputs(inp))
    return _from_fm(x4_fm)
